# version 26
# baseline (speedup 1.0000x reference)
import numpy as np
import ml_dtypes

import concourse.bass as bass
import concourse.mybir as mybir
from concourse.bass_utils import run_bass_kernel_spmd

try:
    from concourse.compiler_utils import get_compiler_flags, set_compiler_flags
    set_compiler_flags([f.replace("--enable-ldw-opt=false", "--enable-ldw-opt=true")
                        for f in get_compiler_flags()])
except Exception:
    pass

F32 = mybir.dt.float32
BF16 = mybir.dt.bfloat16
FP16 = mybir.dt.float16
I16 = mybir.dt.int16
I32 = mybir.dt.int32
AF = mybir.ActivationFunctionType
ALU = mybir.AluOpType
AX = mybir.AxisListType

D = 1024
KD = 8
HEADS = 8
PAIRS = 4
TOK = 2048
NT = 16
HALO = 2054
NVT = 17
HID = 4096
KH = 32
NG = 4
GC = 4
EPS_LN = 1e-5
EPS_Z = 1e-8
EPS_Q = 1e-5
OFF = 1536.0
PHI_NT = [512, 512, 512, 512, 6]

ENGS = ("pe", "dv", "se", "gp", "sy")


NQS = 4


class Step:
    __slots__ = ("eng", "idx", "cord", "deps", "fn", "dma", "qord", "name")

    def __init__(self, eng, idx, deps, fn, dma, name):
        self.eng = eng
        self.idx = idx
        self.cord = None
        self.deps = deps
        self.fn = fn
        self.dma = dma
        self.qord = None
        self.name = name


class Prog:
    def __init__(self):
        self.steps = {e: [] for e in ENGS}

    def step(self, eng, deps, fn, name=""):
        s = Step(eng, len(self.steps[eng]), [d for d in deps if d is not None],
                 fn, False, name)
        s.cord = sum(1 for t in self.steps[eng] if not t.dma)
        self.steps[eng].append(s)
        return s

    def dma(self, eng, deps, fn, name=""):
        s = Step(eng, len(self.steps[eng]), [d for d in deps if d is not None],
                 fn, True, name)
        s.qord = sum(1 for t in self.steps[eng] if t.dma)
        self.steps[eng].append(s)
        return s

    def emit(self, blk, ctrs, qctrs):
        engobjs = {"pe": blk.tensor, "dv": blk.vector, "se": blk.scalar,
                   "gp": blk.gpsimd, "sy": blk.sync}
        dmalist = {e: [s for s in self.steps[e] if s.dma] for e in ENGS}

        def make_body(eng_name):
            def body(eng):
                hw = {}

                def wait_for(d):
                    if d.dma:
                        key = ("q", d.eng, d.qord % NQS)
                        val = 16 * (d.qord // NQS + 1)
                        sem = qctrs[d.eng][d.qord % NQS]
                    else:
                        key, val, sem = ("c", d.eng), d.cord + 1, ctrs[d.eng]
                    if hw.get(key, 0) < val:
                        hw[key] = val
                        eng.wait_ge(sem, val)

                for s in self.steps[eng_name]:
                    for d in s.deps:
                        if d.eng == eng_name and not d.dma and not s.dma:
                            continue
                        wait_for(d)
                    if s.dma:
                        if s.qord >= NQS:
                            wait_for(dmalist[eng_name][s.qord - NQS])
                        sem = qctrs[eng_name][s.qord % NQS]
                        s.fn(eng, sem, count_only=False)
                    else:
                        if eng_name in ("dv", "se", "gp"):
                            eng.drain()
                        last = s.fn(eng)
                        assert last is not None, f"step {s.name} returned None"
                        last.then_inc(ctrs[eng_name], 1)
            return body

        for e in ENGS:
            engobjs[e](make_body(e))


def build_nc():
    nc = bass.Bass("TRN2", target_bir_lowering=False)

    x_in = nc.dram_tensor("x_halo", [HALO, D], F32, kind="ExternalInput")
    wqkT_in = nc.dram_tensor("wqkT", [2 * D, D], F32, kind="ExternalInput")
    w_v_in = nc.dram_tensor("w_v", [D, D], F32, kind="ExternalInput")
    proj_in = nc.dram_tensor("proj", [128, 64], F32, kind="ExternalInput")
    w_out_in = nc.dram_tensor("w_out", [D, D], F32, kind="ExternalInput")
    b_out_in = nc.dram_tensor("b_out_r", [128, 8], F32, kind="ExternalInput")
    w1T_in = nc.dram_tensor("w1T", [D, HID], F32, kind="ExternalInput")
    w2T_in = nc.dram_tensor("w2T", [HID, D], F32, kind="ExternalInput")
    mask1_in = nc.dram_tensor("mask1", [128, 128], F32, kind="ExternalInput")
    mask2_in = nc.dram_tensor("mask2", [6, 128], F32, kind="ExternalInput")
    idf_in = nc.dram_tensor("identf", [128, 128], F32, kind="ExternalInput")
    idb_in = nc.dram_tensor("identb", [128, 128], BF16, kind="ExternalInput")
    idh_in = nc.dram_tensor("identh", [128, 128], FP16, kind="ExternalInput")
    edge_in = nc.dram_tensor("edgem", [128, 6], F32, kind="ExternalInput")
    y_out = nc.dram_tensor("y", [TOK, D], F32, kind="ExternalOutput")
    wq1_scr = nc.dram_tensor("wq1_scr", [D, HID], BF16, kind="Internal")
    wq2_scr = nc.dram_tensor("wq2_scr", [HID, D], BF16, kind="Internal")

    from contextlib import ExitStack
    st = ExitStack()
    SB = lambda n, s, d: st.enter_context(nc.sbuf_tensor(n, s, d))

    x_tm = SB("x_tm", [128, NT, D], BF16)
    mask1 = SB("m1", [128, 128], F32)
    mask2 = SB("m2", [6, 128], F32)
    idf = SB("idf", [128, 128], F32)
    idb = SB("idb", [128, 128], BF16)
    idh = SB("idh", [128, 128], FP16)
    edge = SB("edge", [128, 6], F32)
    b_out = SB("b_out", [128, 8], F32)
    proj_f = SB("proj_f", [128, 64], F32)
    proj_b = SB("proj_b", [128, 64], BF16)
    ones_col = SB("ones_col", [128, 1], F32)
    ones_row = SB("ones_row", [1, 128], F32)
    LNS = {}
    for lname in ("l1", "l2", "l3", "l4"):
        LNS[lname] = [dict(bn=SB(f"{lname}bn{i}", [128, 12], F32),
                           ag=SB(f"{lname}ag{i}", [128, 2], F32),
                           rs=SB(f"{lname}rs{i}", [128, 1], F32),
                           nm=SB(f"{lname}nm{i}", [128, 1], F32)) for i in range(2)]
    QXs = [dict(mx=SB(f"qxm{i}", [128, 1], F32), isc=SB(f"qxi{i}", [128, 1], F32),
                sc=SB(f"qxs{i}", [128, 1], F32), fac=SB(f"qxf{i}", [128, 1], F32))
           for i in range(2)]
    QHs = [dict(mx=SB(f"qhm{i}", [128, 2], F32), isc=SB(f"qhi{i}", [128, 1], F32),
                sc=SB(f"qhs{i}", [128, 1], F32)) for i in range(2)]
    fac2buf = SB("fac2buf", [128, 8], F32)
    za = SB("za", [128, 1], F32)
    zb = SB("zb", [128, 1], F32)
    w1acc = SB("w1acc", [128, 32], F32)
    w2acc = SB("w2acc", [128, 32], F32)
    wtot2 = SB("wtot2", [128, 1], F32)
    wtot1 = SB("wtot1", [128, 1], F32)
    sv1 = SB("sv1", [1, 1], F32)
    sv2 = SB("sv2", [1, 1], F32)
    s1_t = SB("s1_t", [128, 1], F32)
    s2_t = SB("s2_t", [128, 1], F32)
    rh1_t = SB("rh1_t", [128, 1], F32)
    rh2_t = SB("rh2_t", [128, 1], F32)

    pA = st.enter_context(nc.psum_tensor("pA", [128, 2048], F32))
    pB = st.enter_context(nc.psum_tensor("pB", [128, 2048], F32))

    class Ovl:
        def __init__(self, base_off):
            self.off = base_off

        def sbuf_tensor(self, name, shape, dtype):
            import functools, operator
            per = functools.reduce(operator.mul, shape[1:], 1) * mybir.dt.size(dtype)
            off = self.off
            self.off = (self.off + per + 31) & ~31
            hnd = nc.virtual_sbuf_tensor(name, shape, dtype)
            nc._tensor(hnd.name, hnd.shape, hnd.dtype, type="SB")
            mloc = nc.lookup_mloc(hnd)
            mloc.allocated = True
            pend.append((mloc, off))
            return hnd

    pend = []
    arAV = Ovl(0)
    A_tm = arAV.sbuf_tensor("A_tm", [128, NT, D], BF16)
    Vt = arAV.sbuf_tensor("Vt", [128, 3, HEADS, 129], BF16)
    S1m = arAV.sbuf_tensor("S1m", [128, 2, 256], BF16)
    S2m = arAV.sbuf_tensor("S2m", [6, 2, 256], BF16)
    AVSZ = (arAV.off + 31) & ~31

    arAT = Ovl(AVSZ)
    x_ln = arAT.sbuf_tensor("x_ln", [128, KD, HALO], BF16)
    qphi = arAT.sbuf_tensor("qphi", [128, PAIRS, HALO], BF16)
    kphi = arAT.sbuf_tensor("kphi", [128, PAIRS, HALO], BF16)
    wqp = arAT.sbuf_tensor("wqp", [128, KD, 512], BF16)
    wkp = arAT.sbuf_tensor("wkp", [128, KD, 512], BF16)
    wv_sb = arAT.sbuf_tensor("wv_sb", [128, KD, D], BF16)
    xin = arAT.sbuf_tensor("xin", [128, 2, D], F32)
    xhalo6 = arAT.sbuf_tensor("xh6", [6, D], F32)
    XLNTM_OFF = arAT.off
    xlntm = arAT.sbuf_tensor("xlntm", [128, D], F32)
    stage32 = arAT.sbuf_tensor("stage32", [128, 2, D], F32)
    wqk_b = arAT.sbuf_tensor("wqk_b", [128, 2, D], BF16)
    elu_t = arAT.sbuf_tensor("elu_t", [128, 2, 512], F32)

    arP23 = Ovl(AVSZ)
    A_fm = arP23.sbuf_tensor("A_fm", [128, KD, TOK], BF16)
    opfm = arP23.sbuf_tensor("opfm", [128, 8, TOK], BF16)
    wo_sb = arP23.sbuf_tensor("wo_sb", [128, KD, D], BF16)
    y_sb = arP23.sbuf_tensor("y_sb", [128, 2, D], F32)
    yn_sb = arP23.sbuf_tensor("yn_sb", [128, 2, D], F32)

    arP4 = Ovl(0)
    x3_sb = arP4.sbuf_tensor("x3_sb", [128, D], F32)
    Ax_tm = arP4.sbuf_tensor("Ax_tm", [128, 2, D], FP16)
    Axf = arP4.sbuf_tensor("Axf", [128, 2, KD, 128], BF16)
    u05h = arP4.sbuf_tensor("u05h", [128, 2, D], F32)
    wq2r = arP4.sbuf_tensor("wq2r", [128, 4, D], BF16)
    arP4.sbuf_tensor("pad4", [128, 2048], mybir.dt.uint8)
    h_sb = arP4.sbuf_tensor("h_sb", [128, 2, HID], BF16)
    Ah_tm = arP4.sbuf_tensor("Ah_tm", [128, HID], FP16)
    wq1g = arP4.sbuf_tensor("wq1g", [128, KD, HID], BF16)
    Ahfm = arP4.sbuf_tensor("Ahfm", [128, KH, 512], BF16)
    o2fm = arP4.sbuf_tensor("o2fm", [128, 8, 512], BF16)
    y2_sb = arP4.sbuf_tensor("y2_sb", [128, D], BF16)
    yn4_sb = arP4.sbuf_tensor("yn4_sb", [128, D], BF16)
    yout_sb = arP4.sbuf_tensor("yout", [128, D], F32)

    arWQ = Ovl(XLNTM_OFF)
    w23s = arWQ.sbuf_tensor("w23s", [128, 3, D], F32)
    wqt_o = arWQ.sbuf_tensor("wqt_o", [128, 3, D], BF16)
    wqtmp = arWQ.sbuf_tensor("wqtmp", [128, 2, D], BF16)

    OVLSZ = max(arAT.off, arP23.off, arP4.off, arWQ.off)
    ovl = st.enter_context(nc.sbuf_tensor("ovl", [128, OVLSZ], mybir.dt.uint8))
    ovl_base = nc.lookup_mloc(ovl).addr
    for mloc, off in pend:
        mloc.addr = ovl_base + off

    cbias = SB("cbias", [128, 4], F32)
    with nc.Block("initc") as iblk:
        @iblk.gpsimd
        def _(g):
            g.memset(cbias[:, 0:1], -OFF)
            g.memset(cbias[:, 1:2], OFF - 0.5)
            g.memset(cbias[:, 2:3], EPS_LN)
            g.memset(cbias[:, 3:4], OFF + 0.5)
    nc.const_aps.register(F32, -OFF, cbias[:, 0:1])
    nc.const_aps.register(F32, OFF - 0.5, cbias[:, 1:2])
    nc.const_aps.register(F32, EPS_LN, cbias[:, 2:3])
    nc.const_aps.register(F32, OFF + 0.5, cbias[:, 3:4])

    ctr_h, qctr_h = {}, {}
    for e in ENGS:
        ctr_h[e] = st.enter_context(nc.semaphore(f"c_{e}"))
        qctr_h[e] = [st.enter_context(nc.semaphore(f"q_{e}{i}")) for i in range(4)]

    P = Prog()

    _dc = {}
    for _nm, _dst, _src in (("m1", mask1, mask1_in), ("m2", mask2, mask2_in),
                            ("idf", idf, idf_in), ("idb", idb, idb_in),
                            ("idh", idh, idh_in), ("edge", edge, edge_in),
                            ("bout", b_out, b_out_in), ("proj", proj_f, proj_in)):
        def f_dc(sy, sem, count_only=False, _dst=_dst, _src=_src):
            if count_only:
                return 1
            sy.dma_start(_dst[:], _src.ap()).then_inc(sem, 16)
        _dc[_nm] = P.dma("sy", [], f_dc, f"dc_{_nm}")
    def f_dcx(sy, sem, count_only=False):
        if count_only:
            return 1
        sy.dma_start(xhalo6[0:3, :], x_in.ap()[0:3, :]).then_inc(sem, 16)
    _dc["xh6a"] = P.dma("sy", [], f_dcx, "dc_xh6a")
    dc_m1, dc_m2, dc_idf, dc_idb = _dc["m1"], _dc["m2"], _dc["idf"], _dc["idb"]
    dc_idh, dc_edge, dc_bout, dc_proj = _dc["idh"], _dc["edge"], _dc["bout"], _dc["proj"]
    dc_xh6a = _dc["xh6a"]

    def f_init0(v):
        v.memset(ones_col[:], 1.0)
        v.drain()
        v.memset(ones_row[:], 1.0)
        v.drain()
        v.memset(Vt[:, :, :, 128:129], 1.0)
        v.drain()
        return v.memset(S2m[:, :, :], 0.0)
    init0 = P.step("dv", [], f_init0, "init0")

    xinld = [None] * NT
    xst = [None] * NT; xsq = [None] * NT; xfin = [None] * NT
    xapply = [None] * NT; xcast = [None] * NT
    t1 = [None] * NT; t1e = [None] * NT

    for t in range(NT):
        def f_xin(sy, sem, count_only=False, t=t):
            if count_only:
                return 1
            sy.dma_start(xin[:, t % 2, :],
                         x_in.ap()[3 + 128 * t:3 + 128 * (t + 1), :]).then_inc(sem, 16)
        deps = [xst[t - 2], xapply[t - 2], xcast[t - 2]] if t >= 2 else []
        xinld[t] = P.dma("sy" if t % 2 == 0 else "se", deps, f_xin, f"xin{t}")

        L = LNS["l1"][t % 2]

        def f_xst(v, t=t, L=L):
            v.bn_stats(L["bn"][:, 0:6], xin[:, t % 2, 0:512])
            v.bn_stats(L["bn"][:, 6:12], xin[:, t % 2, 512:1024])
            v.drain()
            v.bn_aggr(L["ag"][:], L["bn"][:])
            v.drain()
            return v.tensor_scalar(L["rs"][:], L["ag"][:, 1:2], EPS_LN, None, ALU.add)
        xst[t] = P.step("dv", [xinld[t], xapply[t - 2] if t >= 2 else None],
                        f_xst, f"xst{t}")

        def f_xsq(s, L=L):
            return s.activation(L["rs"][:], L["rs"][:], AF.Sqrt, scale=1.0)
        xsq[t] = P.step("se", [xst[t]], f_xsq, f"xsq{t}")

        def f_xfin(v, L=L):
            v.reciprocal(L["rs"][:], L["rs"][:])
            v.drain()
            return v.scalar_tensor_tensor(L["nm"][:], L["ag"][:, 0:1], -1.0, L["rs"][:],
                                          ALU.mult, ALU.mult)
        xfin[t] = P.step("dv", [xsq[t]], f_xfin, f"xfin{t}")

        def f_xapply(s, t=t, L=L):
            return s.activation(xlntm[:, :], xin[:, t % 2, :], AF.Identity,
                                bias=L["nm"][:], scale=L["rs"][:])
        xapply[t] = P.step("se", [xfin[t], t1[t - 1] if t >= 1 else None],
                           f_xapply, f"xap{t}")

        def f_xcast(v, t=t):
            return v.tensor_copy(x_tm[:, t, :], xin[:, t % 2, :])
        xcast[t] = P.step("dv", [xinld[t]], f_xcast, f"xcast{t}")

        def f_t1(te, t=t):
            r = None
            for k in range(KD):
                half, q = k // 4, k % 4
                r = te.transpose(pB[:, 1024 * half + 128 * q:1024 * half + 128 * q + 128],
                                 xlntm[:, 128 * k:128 * k + 128], idf[:])
            return r
        t1[t] = P.step("pe", [xapply[t], dc_idf, t1e[t - 1] if t >= 1 else None],
                       f_t1, f"t1_{t}")

        def f_t1e(s, t=t):
            r = None
            for half in range(2):
                src = pB[:, 1024 * half:1024 * half + 512]
                dst = x_ln[:, 4 * half:4 * half + 4, 3 + 128 * t:3 + 128 * t + 128]
                r = s.activation(dst, src.rearrange("p (k n) -> p k n", k=4),
                                 AF.Copy, scale=1.0)
                if half == 0:
                    s.drain()
            return r
        t1e[t] = P.step("se", [t1[t]], f_t1e, f"t1e{t}")

    def f_xin6(sy, sem, count_only=False):
        if count_only:
            return 1
        sy.dma_start(xhalo6[3:6, :], x_in.ap()[2051:2054, :]).then_inc(sem, 16)
    xin6 = P.dma("sy", [], f_xin6, "xin6")

    L6 = dict(bn=SB("l6bn", [6, 12], F32), ag=SB("l6ag", [6, 2], F32),
              rs=SB("l6rs", [6, 1], F32), nm=SB("l6nm", [6, 1], F32))

    def f_xst6(v):
        v.bn_stats(L6["bn"][0:6, 0:6], xhalo6[:, 0:512])
        v.bn_stats(L6["bn"][0:6, 6:12], xhalo6[:, 512:1024])
        v.drain()
        v.bn_aggr(L6["ag"][0:6, :], L6["bn"][0:6, :])
        v.drain()
        return v.tensor_scalar(L6["rs"][0:6, :], L6["ag"][0:6, 1:2], EPS_LN, None,
                               ALU.add)
    xst6 = P.step("dv", [dc_xh6a, xin6], f_xst6, "xst6")

    def f_xsq6(s):
        return s.activation(L6["rs"][0:6, :], L6["rs"][0:6, :], AF.Sqrt, scale=1.0)
    xsq6 = P.step("se", [xst6], f_xsq6, "xsq6")

    def f_xfin6(v):
        v.reciprocal(L6["rs"][0:6, :], L6["rs"][0:6, :])
        v.drain()
        return v.scalar_tensor_tensor(L6["nm"][0:6, :], L6["ag"][0:6, 0:1], -1.0,
                                      L6["rs"][0:6, :], ALU.mult, ALU.mult)
    xfin6 = P.step("dv", [xsq6], f_xfin6, "xfin6")

    def f_xap6(s):
        return s.activation(xhalo6[:, :], xhalo6[:, :], AF.Identity,
                            bias=L6["nm"][0:6, :], scale=L6["rs"][0:6, :])
    xap6 = P.step("se", [xfin6], f_xap6, "xap6")

    def f_t16(te):
        r = None
        for k in range(KD):
            r = te.transpose(pA[:, 1024 + 12 * k:1024 + 12 * k + 6],
                             xhalo6[:, 128 * k:128 * k + 128], idf[0:6, 0:6])
        return r
    t16 = P.step("pe", [xap6, dc_idf], f_t16, "t16")

    def f_t16e(s):
        r = None
        for k in range(KD):
            s.activation(x_ln[:, k, 0:3], pA[:, 1024 + 12 * k:1024 + 12 * k + 3],
                         AF.Copy, scale=1.0)
            s.drain()
            r = s.activation(x_ln[:, k, 2051:2054], pA[:, 1024 + 12 * k + 3:1024 + 12 * k + 6],
                             AF.Copy, scale=1.0)
            if k < KD - 1:
                s.drain()
        return r
    t16e = P.step("se", [t16], f_t16e, "t16e")

    def f_pjc(v):
        return v.tensor_copy(proj_b[:], proj_f[:])
    pjc = P.step("dv", [dc_proj], f_pjc, "pjc")

    wq_ld = [None] * 16; wq_cast = [None] * 16
    wqmm = [None] * 16; wqev = [None] * 16
    for i in range(16):
        def f_wqld(gp, sem, count_only=False, i=i):
            if count_only:
                return 1
            gp.dma_start(stage32[:, i % 2, :],
                         wqkT_in.ap()[128 * i:128 * (i + 1), :]).then_inc(sem, 16)
        wq_ld[i] = P.dma("gp", [wq_cast[i - 2]] if i >= 2 else [], f_wqld, f"wqld{i}")

        def f_wqcast(v, i=i):
            return v.tensor_copy(wqk_b[:, i % 2, :], stage32[:, i % 2, :])
        wq_cast[i] = P.step("dv", [wq_ld[i], wqmm[i - 2] if i >= 2 else None],
                            f_wqcast, f"wqc{i}")

        def f_wqmm(te, i=i):
            bank = pA[:, 512 * (i % 2):512 * (i % 2) + 512]
            r = None
            for kd in range(KD):
                r = te.matmul(bank[:, 64 * kd:64 * kd + 64],
                              wqk_b[:, i % 2, 128 * kd:128 * kd + 128],
                              proj_b[:], start=True, stop=True)
            return r
        wqmm[i] = P.step("pe", [wq_cast[i], pjc, wqev[i - 2] if i >= 2 else None],
                         f_wqmm, f"wqmm{i}")

        def f_wqev(s, i=i):
            qk, h = divmod(i, 8)
            wt = wqp if qk == 0 else wkp
            bank = pA[:, 512 * (i % 2):512 * (i % 2) + 512]
            p, hh = divmod(h, 2)
            dst = wt[:, 0:KD, 128 * p + 64 * hh:128 * p + 64 * hh + 64]
            return s.activation(dst, bank.rearrange("p (k f) -> p k f", k=KD),
                                AF.Copy, scale=1.0)
        wqev[i] = P.step("se", [wqmm[i]], f_wqev, f"wqev{i}")

    wv_ld = [None] * 8; wv_cast = [None] * 8
    for j in range(8):
        def f_wvld(gp, sem, count_only=False, j=j):
            if count_only:
                return 1
            gp.dma_start(stage32[:, j % 2, :],
                         w_v_in.ap()[128 * j:128 * (j + 1), :]).then_inc(sem, 16)
        prev = wq_cast[14 + j] if 14 + j < 16 else wv_cast[j - 2]
        wv_ld[j] = P.dma("gp", [prev], f_wvld, f"wvld{j}")

        def f_wvcast(v, j=j):
            return v.tensor_copy(wv_sb[:, j, :], stage32[:, j % 2, :])
        wv_cast[j] = P.step("dv", [wv_ld[j]], f_wvcast, f"wvc{j}")

    wab_ld = [None] * 64
    wab_abs = [None] * 64

    def make_wab(k):
        which = "w2" if k < 32 else "w1"
        j = k if k < 32 else k - 32

        def f_ld(sy, sem, count_only=False, which=which, j=j, k=k):
            if count_only:
                return 1
            if which == "w2":
                s_ = w2T_in.ap()[128 * j:128 * (j + 1), :]
            else:
                dk, q = divmod(j, 4)
                s_ = w1T_in.ap()[128 * dk:128 * (dk + 1), 1024 * q:1024 * (q + 1)]
            sy.dma_start(stage32[:, k % 2, :], s_).then_inc(sem, 16)
        if k == 0:
            pdep = [wv_cast[6]]
        elif k == 1:
            pdep = [wv_cast[7]]
        else:
            pdep = [wab_abs[k - 2]]
        wab_ld[k] = P.dma("sy" if k % 2 == 0 else "se", pdep, f_ld, f"wabl{k}")

        def f_abs(s, which=which, j=j, k=k):
            acc = w2acc if which == "w2" else w1acc
            return s.activation(stage32[:, k % 2, :], stage32[:, k % 2, :], AF.Abs,
                                scale=1.0, accum_out=acc[:, j:j + 1])
        wab_abs[k] = P.step("se", [wab_ld[k]], f_abs, f"waba{k}")


    phimm = [None] * 40; phimin = [None] * 40
    phiexp = [None] * 40; phic = [None] * 40
    idx = 0
    for qk in range(2):
        for p in range(PAIRS):
            for ntile in range(5):
                n0 = 512 * ntile
                w = PHI_NT[ntile]

                def f_phimm(te, qk=qk, p=p, n0=n0, w=w, idx=idx):
                    wt = wqp if qk == 0 else wkp
                    bank = pA[:, 512 * (idx % 4):512 * (idx % 4) + w]
                    r = None
                    for kd in range(KD):
                        r = te.matmul(bank, wt[:, kd, 128 * p:128 * p + 128],
                                      x_ln[:, kd, n0:n0 + w],
                                      start=(kd == 0), stop=(kd == KD - 1))
                    return r
                deps = [t1e[NT - 1], t16e, wqev[7 if qk == 0 else 15]]
                if idx >= 4:
                    deps.append(phic[idx - 4])
                elif idx < 2:
                    deps.append(wqev[14 + idx])
                phimm[idx] = P.step("pe", deps, f_phimm, f"phimm{idx}")

                def f_phimin(v, w=w, idx=idx):
                    bank = pA[:, 512 * (idx % 4):512 * (idx % 4) + w]
                    return v.tensor_scalar(elu_t[:, 0, 0:w], bank, 0.0, None, ALU.min)
                phimin[idx] = P.step("dv", [phimm[idx],
                                            phiexp[idx - 1] if idx >= 1 else None],
                                     f_phimin, f"phimin{idx}")

                def f_phiexp(s, w=w):
                    return s.activation(elu_t[:, 1, 0:w], elu_t[:, 0, 0:w], AF.Exp,
                                       scale=1.0)
                phiexp[idx] = P.step("se", [phimin[idx],
                                            phic[idx - 1] if idx >= 1 else None],
                                     f_phiexp, f"phiexp{idx}")

                def f_phic(v, qk=qk, p=p, n0=n0, w=w, idx=idx):
                    dstT = qphi if qk == 0 else kphi
                    bank = pA[:, 512 * (idx % 4):512 * (idx % 4) + w]
                    return v.scalar_tensor_tensor(dstT[:, p, n0:n0 + w], bank, 1.0,
                                                  elu_t[:, 1, 0:w], ALU.add, ALU.max)
                phic[idx] = P.step("dv", [phiexp[idx]], f_phic, f"phic{idx}")
                for _k in range((idx * 64) // 40, ((idx + 1) * 64) // 40):
                    make_wab(_k)
                idx += 1

    def f_kedge(v):
        r = None
        for p in range(PAIRS):
            v.tensor_tensor(kphi[:, p, 0:3], kphi[:, p, 0:3], edge[:, 0:3], ALU.mult)
            v.drain()
            r = v.tensor_tensor(kphi[:, p, 2051:2054], kphi[:, p, 2051:2054],
                                edge[:, 3:6], ALU.mult)
            if p < PAIRS - 1:
                v.drain()
        return r
    kedge = P.step("dv", [phic[39], dc_edge], f_kedge, "kedge")
    def f_wcl(v):
        v.reduce_sum(wtot2[:], w2acc[:], axis=AX.X)
        v.drain()
        return v.reduce_sum(wtot1[:], w1acc[:], axis=AX.X)
    wcl = P.step("dv", [wab_abs[-1]], f_wcl, "wcl")

    def f_wsm(te):
        te.matmul(pA[0:1, 0:1], wtot2[:], ones_col[:], start=True, stop=True)
        return te.matmul(pA[0:1, 64:65], wtot1[:], ones_col[:], start=True, stop=True)
    wsm = P.step("pe", [wcl, init0, phic[36]], f_wsm, "wsm")

    def f_wsv(v):
        v.tensor_scalar(sv2[:], pA[0:1, 0:1], 1.0 / (HID * D), EPS_Q, ALU.mult, ALU.max)
        v.drain()
        return v.tensor_scalar(sv1[:], pA[0:1, 64:65], 1.0 / (HID * D), EPS_Q,
                               ALU.mult, ALU.max)
    wsv = P.step("dv", [wsm], f_wsv, "wsv")

    def f_wsb(te):
        te.matmul(pA[:, 32:33], ones_row[:], sv2[:], start=True, stop=True)
        return te.matmul(pA[:, 96:97], ones_row[:], sv1[:], start=True, stop=True)
    wsb = P.step("pe", [wsv], f_wsb, "wsb")

    def f_wsf(v):
        v.tensor_copy(s2_t[:], pA[:, 32:33])
        v.drain()
        v.tensor_copy(s1_t[:], pA[:, 96:97])
        v.drain()
        v.tensor_scalar(rh2_t[:], pA[:, 32:33], 0.5, None, ALU.mult)
        v.drain()
        v.reciprocal(rh2_t[:], rh2_t[:])
        v.drain()
        v.tensor_scalar(rh1_t[:], pA[:, 96:97], 0.5, None, ALU.mult)
        v.drain()
        return v.reciprocal(rh1_t[:], rh1_t[:])
    wsf = P.step("dv", [wsb], f_wsf, "wsf")

    NWQ = 64
    wql = [None] * NWQ; wqle = [None] * NWQ; wqge = [None] * NWQ
    wqsub = [None] * NWQ; wqs = [None] * NWQ

    def make_quant(k):
        which = "w1" if k < 32 else "w2"
        j = k if k < 32 else k - 32

        parts = []
        for qt in range(4):
            def f_ld(sy, sem, count_only=False, which=which, j=j, k=k, qt=qt):
                if count_only:
                    return 1
                r0, r1 = 32 * qt, 32 * qt + 32
                if which == "w2":
                    s_ = w2T_in.ap()[128 * j + r0:128 * j + r1, :]
                else:
                    dk, q = divmod(j, 4)
                    s_ = w1T_in.ap()[128 * dk + r0:128 * dk + r1,
                                     1024 * q:1024 * (q + 1)]
                sy.dma_start(w23s[r0:r1, k % 3, :], s_).then_inc(sem, 16)
            if k < 3:
                pdep = [wab_abs[-1], t1[NT - 1]]
            else:
                pdep = [wqge[k - 3]]
            parts.append(P.dma("sy", pdep, f_ld, f"wql{k}_{qt}"))
        wql[k] = parts

        def f_le(v, which=which, k=k):
            rh = rh1_t if which == "w1" else rh2_t
            return v.tensor_scalar(wqtmp[:, k % 2, :], w23s[:, k % 3, :], rh[:], -1.0,
                                   ALU.mult, ALU.is_le)
        ldeps = list(wql[k]) + [wsf]
        if k < 2:
            ldeps.append(phic[39])
        else:
            ldeps.append(wqsub[k - 2])
        wqle[k] = P.step("dv", ldeps, f_le, f"wqle{k}")

        def f_ge(v, which=which, k=k):
            rh = rh1_t if which == "w1" else rh2_t
            return v.tensor_scalar(wqt_o[:, k % 3, :], w23s[:, k % 3, :], rh[:], 1.0,
                                   ALU.mult, ALU.is_ge)
        gdeps = [wqle[k]]
        if k < 2:
            gdeps.append(wqmm[15])
        else:
            gdeps.append(wqsub[k - 2])
        if k >= 3:
            gdeps += wqs[k - 3]
        wqge[k] = P.step("dv", gdeps, f_ge, f"wqge{k}")

        def f_sub(gp, k=k):
            return gp.tensor_tensor(wqt_o[:, k % 3, :], wqt_o[:, k % 3, :],
                                    wqtmp[:, k % 2, :], ALU.subtract)
        wqsub[k] = P.step("gp", [wqge[k]], f_sub, f"wqsub{k}")

        sparts = []
        for sh in range(2):
            def f_st(s, sem, count_only=False, which=which, j=j, k=k, sh=sh):
                if count_only:
                    return 1
                r0, r1 = 64 * sh, 64 * sh + 64
                if which == "w2":
                    dst = wq2_scr.ap()[128 * j + r0:128 * j + r1, :]
                else:
                    dk, q = divmod(j, 4)
                    dst = wq1_scr.ap()[128 * dk + r0:128 * dk + r1,
                                       1024 * q:1024 * (q + 1)]
                s.dma_start(dst, wqt_o[r0:r1, k % 3, :]).then_inc(sem, 16)
            sparts.append(P.dma("se", [wqsub[k]], f_st, f"wqs{k}_{sh}"))
        wqs[k] = sparts

    vmm = [None] * NVT; vev = [None] * NVT
    NH = NT * PAIRS
    smm = [None] * NH; sev = [None] * NH
    amm = [None] * NH; aev = [None] * NH

    def make_v(i):
        rows = 128 if i < 16 else 6

        def f_vmm(te, i=i, rows=rows):
            half = pA[:, 0:1024]
            r = None
            for kd in range(KD):
                xs = x_ln[:, kd, 128 * i:128 * i + rows]
                r = te.matmul(half[0:rows, 0:512], xs, wv_sb[:, kd, 0:512],
                              start=(kd == 0), stop=(kd == KD - 1))
            for kd in range(KD):
                xs = x_ln[:, kd, 128 * i:128 * i + rows]
                r = te.matmul(half[0:rows, 512:1024], xs, wv_sb[:, kd, 512:1024],
                              start=(kd == 0), stop=(kd == KD - 1))
            return r
        deps = [t1e[NT - 1], t16e, wv_cast[7], wsf]
        if i == 0:
            deps += [phic[36], phic[37]]
        else:
            deps.append(vev[i - 1])
        vmm[i] = P.step("pe", deps, f_vmm, f"vmm{i}")

        def f_vev(s, i=i, rows=rows):
            half = pA[:, 0:1024]
            return s.activation(Vt[0:rows, i % 3, 0:8, 0:128],
                                half[0:rows, :].rearrange("p (h n) -> p h n", h=8),
                                AF.Copy, scale=1.0)
        vdeps = [vmm[i], init0]
        if i >= 3:
            vdeps.append(aev[4 * (i - 3) + 3])
        vev[i] = P.step("se", vdeps, f_vev, f"vev{i}")

    for i in range(3):
        make_v(i)

    def make_amm(j):
        c2, p2 = divmod(j, PAIRS)
        h0, h1 = 2 * p2, 2 * p2 + 1

        def f_amm(te, c=c2, i=j, h0=h0, h1=h1):
            sb = i % 2
            ab = i % 2
            bank = pB[:, 1024 + 512 * ab:1024 + 512 * ab + 512]
            te.matmul(bank[:, 0:129], S1m[:, sb, 0:128], Vt[:, c % 3, h0, :],
                      start=True, stop=False)
            te.matmul(bank[:, 0:129], S2m[0:6, sb, 0:128],
                      Vt[0:6, (c + 1) % 3, h0, :], start=False, stop=True)
            te.matmul(bank[:, 256:385], S1m[:, sb, 128:256], Vt[:, c % 3, h1, :],
                      start=True, stop=False)
            return te.matmul(bank[:, 256:385], S2m[0:6, sb, 128:256],
                             Vt[0:6, (c + 1) % 3, h1, :], start=False, stop=True)
        adeps = [sev[j], vev[c2], vev[c2 + 1]]
        if j >= 2:
            adeps.append(aev[j - 2])
        amm[j] = P.step("pe", adeps, f_amm, f"amm{j}")

        def f_aev(v, c=c2, i=j, h0=h0, h1=h1):
            ab = i % 2
            bank = pB[:, 1024 + 512 * ab:1024 + 512 * ab + 512]
            v.tensor_scalar(za[:], bank[:, 128:129], EPS_Z, None, ALU.add)
            v.tensor_scalar(zb[:], bank[:, 384:385], EPS_Z, None, ALU.add)
            v.drain()
            v.reciprocal(za[:], za[:])
            v.reciprocal(zb[:], zb[:])
            v.drain()
            v.tensor_scalar(A_tm[:, c, 128 * h0:128 * h0 + 128], bank[:, 0:128],
                            za[:], None, ALU.mult)
            return v.tensor_scalar(A_tm[:, c, 128 * h1:128 * h1 + 128],
                                   bank[:, 256:384], zb[:], None, ALU.mult)
        aev[j] = P.step("dv", [amm[j]], f_aev, f"aev{j}")

    for c in range(NT):
        for p in range(PAIRS):
            i = c * PAIRS + p

            def f_smm(te, c=c, p=p, i=i):
                base = pB[:, 0:1024] if (i % 2 == 0) else pA[:, 1024:2048]
                bankA, bankB = base[:, 0:512], base[:, 512:1024]
                qap0 = qphi[0:64, p, 128 * c + 3:128 * c + 131]
                qap1 = qphi[64:128, p, 128 * c + 3:128 * c + 131]
                te.matmul(bankA[:, 0:128], kphi[0:64, p, 128 * c:128 * c + 128],
                          qap0, start=True, stop=True)
                te.matmul(bankB[:, 0:128], kphi[64:128, p, 128 * c:128 * c + 128],
                          qap1, start=True, stop=True)
                te.matmul(bankA[0:6, 128:256],
                          kphi[0:64, p, 128 * c + 128:128 * c + 134], qap0,
                          start=True, stop=True)
                return te.matmul(bankB[0:6, 128:256],
                                 kphi[64:128, p, 128 * c + 128:128 * c + 134], qap1,
                                 start=True, stop=True)
            sdeps = [kedge]
            if i == 1:
                sdeps += [phic[38], phic[39], wsf]
            if i >= 2:
                sdeps.append(sev[i - 2])
            smm[i] = P.step("pe", sdeps, f_smm, f"smm{i}")

            def f_sev(v, i=i):
                sb = i % 2
                base = pB[:, 0:1024] if (i % 2 == 0) else pA[:, 1024:2048]
                bankA, bankB = base[:, 0:512], base[:, 512:1024]
                v.tensor_tensor(S1m[:, sb, 0:128], bankA[:, 0:128], mask1[:], ALU.mult)
                v.tensor_tensor(S1m[:, sb, 128:256], bankB[:, 0:128], mask1[:], ALU.mult)
                v.tensor_tensor(S2m[0:6, sb, 122:128], bankA[0:6, 250:256],
                                mask2[0:6, 122:128], ALU.mult)
                return v.tensor_tensor(S2m[0:6, sb, 250:256], bankB[0:6, 250:256],
                                       mask2[0:6, 122:128], ALU.mult)
            svdeps = [smm[i], init0, dc_m1, dc_m2]
            if i >= 2:
                svdeps.append(amm[i - 2])
            sev[i] = P.step("dv", svdeps, f_sev, f"sev{i}")

            if i >= 1:
                make_amm(i - 1)
            if p == 0 and c >= 1 and c + 2 <= NVT - 1:
                make_v(c + 2)

        if c >= 2:
            for _k in range(((c - 2) * NWQ) // 14, ((c - 1) * NWQ) // 14):
                make_quant(_k)

    make_amm(NT * PAIRS - 1)


    wo_ld = [None] * KD; wo_cast = [None] * KD
    for j in range(KD):
        def f_wold(gp, sem, count_only=False, j=j):
            if count_only:
                return 1
            gp.dma_start(w23s[:, j % 3, :],
                         w_out_in.ap()[128 * j:128 * (j + 1), :]).then_inc(sem, 16)
        deps = [wqge[NWQ - 3 + j]] if j < 3 else [wo_cast[j - 3]]
        wo_ld[j] = P.dma("gp", deps, f_wold, f"wold{j}")

        def f_wocast(v, j=j):
            return v.tensor_copy(wo_sb[:, j, :], w23s[:, j % 3, :])
        wo_cast[j] = P.step("dv", [wo_ld[j]] + ([smm[NH - 1]] if j < 2 else []),
                            f_wocast, f"woc{j}")

    t2 = [None] * NT; t2e = [None] * NT
    for c in range(NT):
        def f_t2(te, c=c):
            bank = pA[:, 512 * (c % 2):512 * (c % 2) + 512].bitcast(BF16)
            r = None
            for k in range(KD):
                r = te.transpose(bank[:, 128 * k:128 * k + 128],
                                 A_tm[:, c, 128 * k:128 * k + 128], idb[:])
            return r
        deps = [aev[4 * c + 3], vev[NVT - 1], dc_idb]
        if c >= 2:
            deps.append(t2e[c - 2])
        t2[c] = P.step("pe", deps, f_t2, f"t2_{c}")

        def f_t2e(s, c=c):
            bank = pA[:, 512 * (c % 2):512 * (c % 2) + 512].bitcast(BF16)
            dst = A_fm[:, :, 128 * c:128 * c + 128]
            return s.activation(dst, bank.rearrange("p (k n) -> p k n", k=KD),
                                AF.Copy, scale=1.0)
        t2e[c] = P.step("se", [t2[c]], f_t2e, f"t2e{c}")

    womm = {}; woev = {}
    for m in range(8):
        for n in range(4):
            def f_womm(te, m=m, n=n):
                bank = pB[:, 512 * n:512 * n + 512]
                r = None
                for kd in range(KD):
                    r = te.matmul(bank, wo_sb[:, kd, 128 * m:128 * m + 128],
                                  A_fm[:, kd, 512 * n:512 * n + 512],
                                  start=(kd == 0), stop=(kd == KD - 1))
                return r
            deps = [wo_cast[KD - 1], t2e[4 * n + 3], aev[NH - 1]]
            if m >= 1:
                deps.append(woev[(m - 1, n)])
            womm[(m, n)] = P.step("pe", deps, f_womm, f"womm{m}_{n}")

            def f_woev(s, m=m, n=n):
                bank = pB[:, 512 * n:512 * n + 512]
                return s.activation(opfm[:, m, 512 * n:512 * n + 512], bank, AF.Identity,
                                    bias=b_out[:, m:m + 1], scale=1.0)
            woev[(m, n)] = P.step("se", [womm[(m, n)], dc_bout], f_woev, f"woev{m}_{n}")

    t3 = [None] * NT; y_ev = [None] * NT
    ln2s = [None] * NT; ln2q = [None] * NT; ln2f = [None] * NT
    ln2a = [None] * NT; res2 = [None] * NT
    ln3s = [None] * NT; ln3q = [None] * NT; ln3f = [None] * NT; ln3a = [None] * NT
    qx_mx = [None] * NT; qx_sc = [None] * NT; qx_u = [None] * NT; qx_c = [None] * NT
    qx_b = [None] * NT; qx_r = [None] * NT
    for c in range(NT):
        def f_t3(te, c=c):
            bank = pA[:, 512 * (c % 2):512 * (c % 2) + 512].bitcast(BF16)
            r = None
            for m in range(8):
                r = te.transpose(bank[:, 128 * m:128 * m + 128],
                                 opfm[:, m, 128 * c:128 * c + 128], idb[:])
            return r
        deps = [woev[(7, c // 4)], t2e[NT - 1]]
        if c >= 2:
            deps.append(y_ev[c - 2])
        t3[c] = P.step("pe", deps, f_t3, f"t3_{c}")

        def f_yev(s, c=c):
            bank = pA[:, 512 * (c % 2):512 * (c % 2) + 512].bitcast(BF16)
            return s.activation(y_sb[:, c % 2, :], bank, AF.Copy, scale=1.0)
        ydeps = [t3[c]]
        if c >= 2:
            ydeps += [ln2a[c - 2], ln2s[c - 2]]
        y_ev[c] = P.step("se", ydeps, f_yev, f"yev{c}")

        L = LNS["l2"][c % 2]

        def f_ln2s(v, c=c, L=L):
            v.bn_stats(L["bn"][:, 0:6], y_sb[:, c % 2, 0:512])
            v.bn_stats(L["bn"][:, 6:12], y_sb[:, c % 2, 512:1024])
            v.drain()
            v.bn_aggr(L["ag"][:], L["bn"][:])
            v.drain()
            return v.tensor_scalar(L["rs"][:], L["ag"][:, 1:2], EPS_LN, None, ALU.add)
        ln2s[c] = P.step("dv", [y_ev[c]], f_ln2s, f"ln2s{c}")

        def f_ln2q(s, L=L):
            return s.activation(L["rs"][:], L["rs"][:], AF.Sqrt, scale=1.0)
        ln2q[c] = P.step("se", [ln2s[c]], f_ln2q, f"ln2q{c}")

        def f_ln2f(v, L=L):
            v.reciprocal(L["rs"][:], L["rs"][:])
            v.drain()
            return v.scalar_tensor_tensor(L["nm"][:], L["ag"][:, 0:1], -1.0, L["rs"][:],
                                          ALU.mult, ALU.mult)
        ln2f[c] = P.step("dv", [ln2q[c]], f_ln2f, f"ln2f{c}")

        def f_ln2a(s, c=c, L=L):
            return s.activation(yn_sb[:, c % 2, :], y_sb[:, c % 2, :], AF.Identity,
                                bias=L["nm"][:], scale=L["rs"][:])
        adeps = [ln2f[c]]
        if c >= 2:
            adeps.append(res2[c - 2])
        ln2a[c] = P.step("se", adeps, f_ln2a, f"ln2a{c}")

        def f_res2(gp, c=c):
            return gp.tensor_tensor(x_tm[:, c, :], x_tm[:, c, :], yn_sb[:, c % 2, :],
                                    ALU.add)
        res2[c] = P.step("gp", [ln2a[c], xcast[c]], f_res2, f"res2_{c}")


    wq1_ld = [None] * KD
    for k in range(KD):
        def f_wq1ld(gp, sem, count_only=False, k=k):
            if count_only:
                return 1
            gp.dma_start(wq1g[:, k, :],
                         wq1_scr.ap()[128 * k:128 * (k + 1), :]).then_inc(sem, 16)
        wq1_ld[k] = P.dma("gp", sum([wqs[4 * k + q] for q in range(4)], [])
                          + [womm[(7, 3)], t3[NT - 1]], f_wq1ld, f"wq1l{k}")

    t4 = [None] * NT; t4e = [None] * NT
    f1 = {}; f1g = {}
    qh_mx2 = {}; qh_sc = [None] * NT
    qh_u = {}; qh_c = {}; qh_b = {}; qh_r = {}
    t5 = {}; t5e = {}
    f2 = {}; f2ev = {}; f2evA = {}
    wq2_ld = {}
    t6 = {}; t6e = {}
    ln4s = [None] * NT; ln4q = [None] * NT; ln4f = [None] * NT; ln4a = [None] * NT
    outadd = [None] * NT; yst = [None] * NT

    def emit_lnqx(c):
        L3 = LNS["l3"][c % 2]
        Q = QXs[c % 2]

        def f_ln3s(v, c=c, L=L3):
            v.bn_stats(L["bn"][:, 0:6], x_tm[:, c, 0:512])
            v.bn_stats(L["bn"][:, 6:12], x_tm[:, c, 512:1024])
            v.drain()
            v.bn_aggr(L["ag"][:], L["bn"][:])
            v.drain()
            return v.tensor_scalar(L["rs"][:], L["ag"][:, 1:2], EPS_LN, None,
                                   ALU.add)
        deps = [res2[c]]
        if c >= 2:
            deps.append(ln3a[c - 2])
        ln3s[c] = P.step("dv", deps, f_ln3s, f"ln3s{c}")

        def f_ln3q(s, L=L3):
            return s.activation(L["rs"][:], L["rs"][:], AF.Sqrt, scale=1.0)
        ln3q[c] = P.step("se", [ln3s[c]], f_ln3q, f"ln3q{c}")

        def f_ln3f(v, L=L3):
            v.reciprocal(L["rs"][:], L["rs"][:])
            v.drain()
            return v.scalar_tensor_tensor(L["nm"][:], L["ag"][:, 0:1], -1.0,
                                          L["rs"][:], ALU.mult, ALU.mult)
        ln3f[c] = P.step("dv", [ln3q[c]], f_ln3f, f"ln3f{c}")

        def f_ln3a(s, c=c, L=L3):
            return s.activation(x3_sb[:, :], x_tm[:, c, :], AF.Identity,
                                bias=L["nm"][:], scale=L["rs"][:])
        adeps = [ln3f[c]]
        if c == 0:
            adeps.append(t2[NT - 1])
        if c >= 1:
            adeps.append(qx_b[c - 1])
        ln3a[c] = P.step("se", adeps, f_ln3a, f"ln3a{c}")

        def f_qxmx(v, c=c, Q=Q):
            return v.reduce_max(Q["mx"][:], x3_sb[:, :], axis=AX.X,
                                apply_absolute_value=True)
        qx_mx[c] = P.step("dv", [ln3a[c]], f_qxmx, f"qxmx{c}")

        def f_qxsc(v, Q=Q):
            v.tensor_scalar(Q["isc"][:], Q["mx"][:], EPS_Q, 1.0 / 127.0,
                            ALU.max, ALU.mult)
            v.drain()
            v.reciprocal(Q["sc"][:], Q["isc"][:])
            return v.tensor_tensor(Q["fac"][:], s1_t[:], Q["isc"][:], ALU.mult)
        qx_sc[c] = P.step("dv", [qx_mx[c], wsf], f_qxsc, f"qxsc{c}")

        def f_qxu(v, c=c, Q=Q):
            return v.tensor_scalar(x3_sb[:, :], x3_sb[:, :],
                                   Q["sc"][:], OFF + 0.5, ALU.mult, ALU.add)
        cdeps = [qx_sc[c]]
        if c == 0:
            cdeps.append(t2[NT - 1])
        if c >= 1:
            cdeps.append(qx_r[c - 1])
        qx_u[c] = qx_c[c] = P.step("dv", cdeps, f_qxu, f"qxu{c}")

        def f_qxr(v, c=c):
            v.tensor_scalar(x3_sb[:, :].bitcast(I32), x3_sb[:, :].bitcast(I32),
                            -8192, None, ALU.bitwise_and)
            v.drain()
            return v.tensor_copy(Ax_tm[:, c % 2, :], x3_sb[:, :])
        rdeps = [qx_u[c]]
        if c >= 2:
            rdeps.append(t4[c - 2])
        qx_r[c] = qx_b[c] = P.step("dv", rdeps, f_qxr, f"qxr{c}")

    def emit_t4(c):
        g, cc = divmod(c, GC)

        def f_t4(te, c=c):
            bank = pB[:, 0:512].bitcast(FP16)
            r = None
            for k in range(KD):
                r = te.transpose(bank[:, 128 * k:128 * k + 128],
                                 Ax_tm[:, c % 2, 128 * k:128 * k + 128], idh[:])
            return r
        deps = [qx_r[c], dc_idh]
        if c == 0:
            deps.append(woev[(7, 3)])
        else:
            deps.append(t4e[c - 1])
        if c % GC == 0 and g >= 1:
            deps.append(f2ev[g - 1])
        t4[c] = P.step("pe", deps, f_t4, f"t4_{c}")

        def f_t4e(s, c=c):
            src = pB[:, 0:512].bitcast(FP16).rearrange("p (k n) -> p k n", k=KD)
            return s.activation(Axf[:, c % 2, :, :], src, AF.Identity, bias=-OFF,
                                scale=1.0)
        edeps = [t4[c]]
        if c == 0:
            edeps.append(t2[NT - 1])
        elif c >= 2:
            edeps.append(f1[(c - 2, 3)])
        t4e[c] = P.step("se", edeps, f_t4e, f"t4e{c}")

    def emit_f1(c):
        g, cc = divmod(c, GC)
        for quarter in range(4):
            def f_f1(te, c=c, quarter=quarter):
                base = 1024 * (quarter % 2)
                r = None
                for kd in range(KD):
                    for n in range(2):
                        off = 1024 * quarter + 512 * n
                        r = te.matmul(pA[:, base + 512 * n:base + 512 * n + 512],
                                      Axf[:, c % 2, kd, :],
                                      wq1g[:, kd, off:off + 512],
                                      start=(kd == 0), stop=(kd == KD - 1))
                return r
            deps = [t4e[c]] + list(wq1_ld)
            if quarter >= 2:
                deps.append(f1g[(c, quarter - 2)])
            elif c >= 1:
                deps.append(f1g[(c - 1, quarter + 2)])
            else:
                deps.append(y_ev[NT - 1])
            if quarter < 2 and c % GC == 0 and c >= 4:
                deps.append(f2evA[c // GC - 1])
            f1[(c, quarter)] = P.step("pe", deps, f_f1, f"f1_{c}_{quarter}")

            Q = QXs[c % 2]

            def f_f1g(s, c=c, quarter=quarter, Q=Q):
                base = 1024 * (quarter % 2)
                return s.activation(h_sb[:, c % 2, 1024 * quarter:1024 * quarter + 1024],
                                    pA[:, base:base + 1024], AF.Gelu, bias=0.0,
                                    scale=Q["fac"][:])
            gdeps = [f1[(c, quarter)], qx_sc[c]]
            if c < 2:
                gdeps.append(womm[(7, 3)])
            if c >= 2:
                gdeps += [qh_u[(c - 2, quarter)], qh_mx2[(c - 2, quarter // 2)]]
            f1g[(c, quarter)] = P.step("se", gdeps, f_f1g, f"f1g{c}_{quarter}")



    def emit_qhx(c):
        QH = QHs[c % 2]
        for half in range(2):
            def f_qhmx(v, c=c, half=half, QH=QH):
                return v.reduce_max(QH["mx"][:, half:half + 1],
                                    h_sb[:, c % 2, 2048 * half:2048 * half + 2048],
                                    axis=AX.X, apply_absolute_value=True)
            qh_mx2[(c, half)] = P.step("dv", [f1g[(c, 2 * half)],
                                              f1g[(c, 2 * half + 1)]], f_qhmx,
                                       f"qhmx{c}_{half}")

        def f_qhsc(v, c=c, QH=QH):
            v.tensor_tensor(QH["mx"][:, 0:1], QH["mx"][:, 0:1], QH["mx"][:, 1:2],
                            ALU.max)
            v.drain()
            v.tensor_scalar(QH["isc"][:], QH["mx"][:, 0:1], EPS_Q, 1.0 / 127.0,
                            ALU.max, ALU.mult)
            v.drain()
            v.reciprocal(QH["sc"][:], QH["isc"][:])
            return v.tensor_tensor(fac2buf[:, c % 8:c % 8 + 1], s2_t[:],
                                   QH["isc"][:], ALU.mult)
        qh_sc[c] = P.step("dv", [qh_mx2[(c, 0)], qh_mx2[(c, 1)], wsf],
                          f_qhsc, f"qhsc{c}")

    def emit_strips(c):
        QH = QHs[c % 2]
        for q in range(4):
            def f_qhu(v, c=c, q=q, QH=QH):
                return v.tensor_scalar(u05h[:, q % 2, :],
                                       h_sb[:, c % 2, 1024 * q:1024 * q + 1024],
                                       QH["sc"][:], OFF + 0.5, ALU.mult, ALU.add)
            ueng = "dv"
            deps = [qh_sc[c]]
            if q >= 2:
                deps.append(qh_r[(c, q - 2)])
            elif c >= 1:
                deps.append(qh_r[(c - 1, q + 2)])
            elif q < 2:
                deps.append(t2[NT - 1])
            qh_u[(c, q)] = qh_c[(c, q)] = P.step(ueng, deps, f_qhu, f"qhu{c}_{q}")

            def f_qhr(v, c=c, q=q):
                v.tensor_scalar(u05h[:, q % 2, :].bitcast(I32),
                                u05h[:, q % 2, :].bitcast(I32),
                                -8192, None, ALU.bitwise_and)
                v.drain()
                return v.tensor_copy(Ah_tm[:, 1024 * q:1024 * q + 1024],
                                     u05h[:, q % 2, :])
            deps = [qh_u[(c, q)]]
            if c >= 1:
                deps.append(t5[(c - 1, 3)])
            elif q == 0:
                deps.append(womm[(7, 3)])
            qh_r[(c, q)] = qh_b[(c, q)] = P.step("dv", deps, f_qhr, f"qhr{c}_{q}")

    def emit_tfb(c):
        g, cc = divmod(c, GC)
        for j in range(4):
            def f_t5(te, c=c, j=j):
                bank = pB[:, 512 + 512 * (j % 2):1024 + 512 * (j % 2)].bitcast(FP16)
                r = None
                for b in range(8):
                    k = 8 * j + b
                    r = te.transpose(bank[:, 128 * b:128 * b + 128],
                                     Ah_tm[:, 128 * k:128 * k + 128], idh[:])
                return r
            deps = [qh_r[(c, j)]]
            if j == 0:
                if c >= 1:
                    deps.append(t5e[(c - 1, 2)])
            elif j == 1:
                if c >= 1:
                    deps.append(t5e[(c - 1, 3)])
            else:
                deps.append(t5e[(c, j - 2)])
            t5[(c, j)] = P.step("pe", deps, f_t5, f"t5_{c}_{j}")

            def f_t5e(v, c=c, cc=cc, j=j):
                src = pB[:, 512 + 512 * (j % 2):1024 + 512 * (j % 2)].bitcast(FP16)
                dst = Ahfm[:, 8 * j:8 * j + 8, 128 * cc:128 * cc + 128]
                return v.tensor_scalar(dst, src.rearrange("p (k n) -> p k n", k=8),
                                       -OFF, None, ALU.add)
            deps = [t5[(c, j)]]
            if c == 0:
                deps += [womm[(7, 3)], ln2a[NT - 1], ln2s[NT - 1], res2[NT - 1]]
            if cc == 0 and g >= 1:
                deps.append(f2[(g - 1, KH - 1)])
            t5e[(c, j)] = P.step("dv", deps, f_t5e, f"t5e{c}_{j}")

    def emit_f2(g):
        for k in range(KH):
            deps = list(wqs[32 + k])
            if g == 0:
                deps.append(t2[NT - 1])
            if g == 0 and k < 4:
                deps += sum(wqs[NWQ - 4:], [])
            if k >= 4:
                deps.append(f2[(g, k - 4)])
            elif g >= 1:
                deps.append(f2[(g - 1, KH - 4 + k)])
            parts = []
            for qt in range(4):
                def f_wq2ld(gp, sem, count_only=False, g=g, k=k, qt=qt):
                    if count_only:
                        return 1
                    gp.dma_start(wq2r[32 * qt:32 * qt + 32, k % 4, :],
                                 wq2_scr.ap()[128 * k + 32 * qt:128 * k + 32 * qt + 32,
                                              :]).then_inc(sem, 16)
                parts.append(P.dma("gp" if qt % 2 == 0 else "sy", deps, f_wq2ld,
                                   f"wq2l{g}_{k}_{qt}"))
            wq2_ld[(g, k)] = parts

            def f_f2(te, g=g, k=k):
                r = None
                for m in range(8):
                    bank = (pA[:, 512 * m:512 * m + 512] if m < 4
                            else pB[:, 512 * (m - 4):512 * (m - 4) + 512])
                    r = te.matmul(bank, wq2r[:, k % 4, 128 * m:128 * m + 128],
                                  Ahfm[:, k, :], start=(k == 0), stop=(k == KH - 1))
                return r
            deps = list(wq2_ld[(g, k)])
            if k % 8 == 0:
                deps += [t5e[(4 * g + cc2, k // 8)] for cc2 in range(GC)]
            if k == 0:
                deps += [f1g[(4 * g + 3, 3)], t5e[(4 * g + 3, 2)],
                         t5e[(4 * g + 3, 3)]]
                if g >= 1:
                    deps.append(t6e[(g - 1, 3)])
            f2[(g, k)] = P.step("pe", deps, f_f2, f"f2_{g}_{k}")

        def f_f2ev(v, g=g):
            r = None
            for m in range(4, 8):
                r = v.tensor_copy(o2fm[:, m, :],
                                  pB[:, 512 * (m - 4):512 * (m - 4) + 512])
            return r
        fdeps = [f2[(g, KH - 1)]]
        if g == 0:
            fdeps.append(wqsub[NWQ - 1])
        f2ev[g] = P.step("dv", fdeps, f_f2ev, f"f2ev{g}")

        def f_f2evA(s, g=g):
            r = None
            for m in range(4):
                r = s.activation(o2fm[:, m, :], pA[:, 512 * m:512 * m + 512],
                                 AF.Copy, scale=1.0)
            return r
        f2evA[g] = P.step("se", list(fdeps), f_f2evA, f"f2evA{g}")

    def emit_tail(g, cc):
        c = 4 * g + cc

        def f_t6(te, g=g, cc=cc):
            bank = pB[:, 1536:2048].bitcast(BF16)
            r = None
            for m in range(8):
                r = te.transpose(bank[:, 128 * m:128 * m + 128],
                                 o2fm[:, m, 128 * cc:128 * cc + 128], idb[:])
            return r
        deps = [f2ev[g], f2evA[g]]
        if cc >= 1:
            deps.append(t6e[(g, cc - 1)])
        t6[(g, cc)] = P.step("pe", deps, f_t6, f"t6_{g}_{cc}")

        def f_t6e(v, g=g, cc=cc, c=c):
            bank = pB[:, 1536:2048].bitcast(BF16)
            return v.tensor_scalar(y2_sb[:, :], bank,
                                   fac2buf[:, c % 8:c % 8 + 1], None, ALU.mult)
        deps = [t6[(g, cc)], qh_sc[c]]
        if c == 0:
            deps.append(wqge[NWQ - 1])
        if c >= 1:
            deps += [ln4a[c - 1], ln4s[c - 1]]
        t6e[(g, cc)] = P.step("dv", deps, f_t6e, f"t6e{g}_{cc}")

        L = LNS["l4"][c % 2]

        def f_ln4s(v, c=c, L=L):
            v.bn_stats(L["bn"][:, 0:6], y2_sb[:, 0:512])
            v.bn_stats(L["bn"][:, 6:12], y2_sb[:, 512:1024])
            v.drain()
            v.bn_aggr(L["ag"][:], L["bn"][:])
            v.drain()
            return v.tensor_scalar(L["rs"][:], L["ag"][:, 1:2], EPS_LN, None,
                                   ALU.add)
        ln4s[c] = P.step("dv", [t6e[(g, cc)]], f_ln4s, f"ln4s{c}")

        def f_ln4q(s, L=L):
            return s.activation(L["rs"][:], L["rs"][:], AF.Sqrt, scale=1.0)
        ln4q[c] = P.step("se", [ln4s[c]], f_ln4q, f"ln4q{c}")

        def f_ln4f(v, L=L):
            v.reciprocal(L["rs"][:], L["rs"][:])
            v.drain()
            return v.scalar_tensor_tensor(L["nm"][:], L["ag"][:, 0:1], -1.0,
                                          L["rs"][:], ALU.mult, ALU.mult)
        ln4f[c] = P.step("dv", [ln4q[c]], f_ln4f, f"ln4f{c}")

        def f_ln4a(s, c=c, L=L):
            return s.activation(yn4_sb[:, :], y2_sb[:, :], AF.Identity,
                                bias=L["nm"][:], scale=L["rs"][:])
        adeps = [ln4f[c]]
        if c == 0:
            adeps.append(wqge[NWQ - 1])
        if c >= 1:
            adeps.append(outadd[c - 1])
        ln4a[c] = P.step("se", adeps, f_ln4a, f"ln4a{c}")

        def f_out(gp, c=c):
            return gp.tensor_tensor(yout_sb[:, :], x_tm[:, c, :],
                                    yn4_sb[:, :], ALU.add)
        odeps = [ln4a[c], ln3s[c]]
        if c == 0:
            odeps += [wqge[NWQ - 1]] + sum(wqs[NWQ - 4:], [])
        if c >= 1:
            odeps.append(yst[c - 1])
        outadd[c] = P.step("gp", odeps, f_out, f"out{c}")

        def f_yst(sy, sem, count_only=False, c=c):
            if count_only:
                return 1
            sy.dma_start(y_out.ap()[128 * c:128 * (c + 1), :],
                         yout_sb[:, :]).then_inc(sem, 16)
        yst[c] = P.dma("sy", [outadd[c]], f_yst, f"yst{c}")

    for g in range(NG):
        if g >= 1:
            emit_strips(4 * g - 1)
            emit_tfb(4 * g - 1)
            emit_f2(g - 1)
        emit_lnqx(4 * g)
        emit_lnqx(4 * g + 1)
        emit_t4(4 * g)
        for cc in range(GC):
            c = 4 * g + cc
            emit_f1(c)
            if cc < GC - 1:
                emit_t4(c + 1)
            if cc >= 1:
                emit_strips(c - 1)
                emit_tfb(c - 1)
            if cc + 2 < GC:
                emit_lnqx(c + 2)
            emit_qhx(c)
            if g >= 1:
                emit_tail(g - 1, cc)
    emit_strips(NT - 1)
    emit_tfb(NT - 1)
    emit_f2(NG - 1)
    for cc in range(GC):
        emit_tail(NG - 1, cc)

    def f_fin(sy):
        return sy.nop()
    P.step("sy", [yst[c] for c in range(NT)], f_fin, "fin")

    with nc.Block("mk") as blk:
        P.emit(blk, ctr_h, qctr_h)

    st.close()
    return nc



def kernel(L0_attn_x, L0_attn_w_qkv, L0_attn_proj, L0_attn_w_out, L0_attn_b_out,
           L0_ffn_w1, L0_ffn_w2):
    x = np.asarray(L0_attn_x, dtype=np.float32)
    w_qkv = np.asarray(L0_attn_w_qkv, dtype=np.float32)
    proj = np.ascontiguousarray(np.asarray(L0_attn_proj, dtype=np.float32))
    w_out = np.ascontiguousarray(np.asarray(L0_attn_w_out, dtype=np.float32))
    b_out = np.asarray(L0_attn_b_out, dtype=np.float32)
    w1 = np.asarray(L0_ffn_w1, dtype=np.float32)
    w2 = np.asarray(L0_ffn_w2, dtype=np.float32)

    wqkT = np.ascontiguousarray(w_qkv[:, :2048].T)
    w_v = np.ascontiguousarray(w_qkv[:, 2048:])
    b_out_r = np.ascontiguousarray(b_out.reshape(8, 128).T)
    w1T = np.ascontiguousarray(w1.T)
    w2T = np.ascontiguousarray(w2.T)

    jj, rr = np.meshgrid(np.arange(128), np.arange(128), indexing="ij")
    mask1 = ((jj - rr >= 0) & (jj - rr <= 6)).astype(np.float32)
    j2, r2 = np.meshgrid(np.arange(6), np.arange(128), indexing="ij")
    mask2 = (r2 - j2 >= 122).astype(np.float32)
    identf = np.eye(128, dtype=np.float32)
    identb = np.eye(128).astype(ml_dtypes.bfloat16)
    identh = np.eye(128, dtype=np.float16)

    in_maps = []
    for core in range(8):
        b = core // 4
        s0 = (core % 4) * TOK
        xh = np.zeros((HALO, D), dtype=np.float32)
        lo, hi = s0 - 3, s0 + TOK + 3
        src_lo, src_hi = max(lo, 0), min(hi, 8192)
        xh[src_lo - lo:src_hi - lo, :] = x[b, src_lo:src_hi, :]
        edge = np.ones((128, 6), dtype=np.float32)
        if s0 == 0:
            edge[:, 0:3] = 0.0
        if s0 + TOK == 8192:
            edge[:, 3:6] = 0.0
        in_maps.append({
            "x_halo": xh, "wqkT": wqkT, "w_v": w_v, "proj": proj,
            "w_out": w_out, "b_out_r": b_out_r, "w1T": w1T, "w2T": w2T,
            "mask1": mask1, "mask2": mask2, "identf": identf, "identb": identb,
            "identh": identh, "edgem": edge,
        })

    nc = build_nc()
    res = run_bass_kernel_spmd(nc, in_maps, core_ids=list(range(8)))
    out = np.concatenate([np.asarray(res.results[i]["y"]) for i in range(8)], axis=0)
    return out.reshape(2, 8192, D)


# version 27
# speedup vs baseline: 1.0224x; 1.0224x over previous
import numpy as np
import ml_dtypes

import concourse.bass as bass
import concourse.mybir as mybir
from concourse.bass_utils import run_bass_kernel_spmd

try:
    from concourse.compiler_utils import get_compiler_flags, set_compiler_flags
    set_compiler_flags([f.replace("--enable-ldw-opt=false", "--enable-ldw-opt=true")
                        for f in get_compiler_flags()])
except Exception:
    pass

F32 = mybir.dt.float32
BF16 = mybir.dt.bfloat16
FP16 = mybir.dt.float16
I16 = mybir.dt.int16
I32 = mybir.dt.int32
AF = mybir.ActivationFunctionType
ALU = mybir.AluOpType
AX = mybir.AxisListType

D = 1024
KD = 8
HEADS = 8
PAIRS = 4
TOK = 2048
NT = 16
HALO = 2054
NVT = 17
HID = 4096
KH = 32
NG = 4
GC = 4
EPS_LN = 1e-5
EPS_Z = 1e-8
EPS_Q = 1e-5
OFF = 1536.0
PHI_NT = [512, 512, 512, 512, 6]

ENGS = ("pe", "dv", "se", "gp", "sy")


NQS = 4


class Step:
    __slots__ = ("eng", "idx", "cord", "deps", "fn", "dma", "qord", "name")

    def __init__(self, eng, idx, deps, fn, dma, name):
        self.eng = eng
        self.idx = idx
        self.cord = None
        self.deps = deps
        self.fn = fn
        self.dma = dma
        self.qord = None
        self.name = name


class Prog:
    def __init__(self):
        self.steps = {e: [] for e in ENGS}

    def step(self, eng, deps, fn, name=""):
        s = Step(eng, len(self.steps[eng]), [d for d in deps if d is not None],
                 fn, False, name)
        s.cord = sum(1 for t in self.steps[eng] if not t.dma)
        self.steps[eng].append(s)
        return s

    def dma(self, eng, deps, fn, name=""):
        s = Step(eng, len(self.steps[eng]), [d for d in deps if d is not None],
                 fn, True, name)
        s.qord = sum(1 for t in self.steps[eng] if t.dma)
        self.steps[eng].append(s)
        return s

    def emit(self, blk, ctrs, qctrs):
        engobjs = {"pe": blk.tensor, "dv": blk.vector, "se": blk.scalar,
                   "gp": blk.gpsimd, "sy": blk.sync}
        dmalist = {e: [s for s in self.steps[e] if s.dma] for e in ENGS}

        def make_body(eng_name):
            def body(eng):
                hw = {}

                def wait_for(d):
                    if d.dma:
                        key = ("q", d.eng, d.qord % NQS)
                        val = 16 * (d.qord // NQS + 1)
                        sem = qctrs[d.eng][d.qord % NQS]
                    else:
                        key, val, sem = ("c", d.eng), d.cord + 1, ctrs[d.eng]
                    if hw.get(key, 0) < val:
                        hw[key] = val
                        eng.wait_ge(sem, val)

                for s in self.steps[eng_name]:
                    for d in s.deps:
                        if d.eng == eng_name and not d.dma and not s.dma:
                            continue
                        wait_for(d)
                    if s.dma:
                        if s.qord >= NQS:
                            wait_for(dmalist[eng_name][s.qord - NQS])
                        sem = qctrs[eng_name][s.qord % NQS]
                        s.fn(eng, sem, count_only=False)
                    else:
                        if eng_name in ("dv", "se", "gp"):
                            eng.drain()
                        last = s.fn(eng)
                        assert last is not None, f"step {s.name} returned None"
                        last.then_inc(ctrs[eng_name], 1)
            return body

        for e in ENGS:
            engobjs[e](make_body(e))


def build_nc():
    nc = bass.Bass("TRN2", target_bir_lowering=False)

    x_in = nc.dram_tensor("x_halo", [HALO, D], F32, kind="ExternalInput")
    wqkT_in = nc.dram_tensor("wqkT", [2 * D, D], F32, kind="ExternalInput")
    w_v_in = nc.dram_tensor("w_v", [D, D], F32, kind="ExternalInput")
    proj_in = nc.dram_tensor("proj", [128, 64], F32, kind="ExternalInput")
    w_out_in = nc.dram_tensor("w_out", [D, D], F32, kind="ExternalInput")
    b_out_in = nc.dram_tensor("b_out_r", [128, 8], F32, kind="ExternalInput")
    w1T_in = nc.dram_tensor("w1T", [D, HID], F32, kind="ExternalInput")
    w2T_in = nc.dram_tensor("w2T", [HID, D], F32, kind="ExternalInput")
    mask1_in = nc.dram_tensor("mask1", [128, 128], F32, kind="ExternalInput")
    mask2_in = nc.dram_tensor("mask2", [6, 128], F32, kind="ExternalInput")
    idf_in = nc.dram_tensor("identf", [128, 128], F32, kind="ExternalInput")
    idb_in = nc.dram_tensor("identb", [128, 128], BF16, kind="ExternalInput")
    idh_in = nc.dram_tensor("identh", [128, 128], FP16, kind="ExternalInput")
    edge_in = nc.dram_tensor("edgem", [128, 6], F32, kind="ExternalInput")
    y_out = nc.dram_tensor("y", [TOK, D], F32, kind="ExternalOutput")
    wq1_scr = nc.dram_tensor("wq1_scr", [D, HID], BF16, kind="Internal")
    wq2_scr = nc.dram_tensor("wq2_scr", [HID, D], BF16, kind="Internal")

    from contextlib import ExitStack
    st = ExitStack()
    SB = lambda n, s, d: st.enter_context(nc.sbuf_tensor(n, s, d))

    x_tm = SB("x_tm", [128, NT, D], BF16)
    mask1 = SB("m1", [128, 128], F32)
    mask2 = SB("m2", [6, 128], F32)
    idf = SB("idf", [128, 128], F32)
    idb = SB("idb", [128, 128], BF16)
    idh = SB("idh", [128, 128], FP16)
    edge = SB("edge", [128, 6], F32)
    b_out = SB("b_out", [128, 8], F32)
    proj_f = SB("proj_f", [128, 64], F32)
    proj_b = SB("proj_b", [128, 64], BF16)
    ones_col = SB("ones_col", [128, 1], F32)
    ones_row = SB("ones_row", [1, 128], F32)
    LNS = {}
    for lname in ("l1", "l2", "l3", "l4"):
        LNS[lname] = [dict(bn=SB(f"{lname}bn{i}", [128, 12], F32),
                           ag=SB(f"{lname}ag{i}", [128, 2], F32),
                           rs=SB(f"{lname}rs{i}", [128, 1], F32),
                           nm=SB(f"{lname}nm{i}", [128, 1], F32)) for i in range(2)]
    QXs = [dict(mx=SB(f"qxm{i}", [128, 1], F32), isc=SB(f"qxi{i}", [128, 1], F32),
                sc=SB(f"qxs{i}", [128, 1], F32), fac=SB(f"qxf{i}", [128, 1], F32))
           for i in range(2)]
    QHs = [dict(mx=SB(f"qhm{i}", [128, 2], F32), isc=SB(f"qhi{i}", [128, 1], F32),
                sc=SB(f"qhs{i}", [128, 1], F32)) for i in range(2)]
    fac2buf = SB("fac2buf", [128, 8], F32)
    za = SB("za", [128, 1], F32)
    zb = SB("zb", [128, 1], F32)
    w1acc = SB("w1acc", [128, 32], F32)
    w2acc = SB("w2acc", [128, 32], F32)
    wtot2 = SB("wtot2", [128, 1], F32)
    wtot1 = SB("wtot1", [128, 1], F32)
    sv1 = SB("sv1", [1, 1], F32)
    sv2 = SB("sv2", [1, 1], F32)
    s1_t = SB("s1_t", [128, 1], F32)
    s2_t = SB("s2_t", [128, 1], F32)
    rh1_t = SB("rh1_t", [128, 1], F32)
    rh2_t = SB("rh2_t", [128, 1], F32)

    pA = st.enter_context(nc.psum_tensor("pA", [128, 2048], F32))
    pB = st.enter_context(nc.psum_tensor("pB", [128, 2048], F32))

    class Ovl:
        def __init__(self, base_off):
            self.off = base_off

        def sbuf_tensor(self, name, shape, dtype):
            import functools, operator
            per = functools.reduce(operator.mul, shape[1:], 1) * mybir.dt.size(dtype)
            off = self.off
            self.off = (self.off + per + 31) & ~31
            hnd = nc.virtual_sbuf_tensor(name, shape, dtype)
            nc._tensor(hnd.name, hnd.shape, hnd.dtype, type="SB")
            mloc = nc.lookup_mloc(hnd)
            mloc.allocated = True
            pend.append((mloc, off))
            return hnd

    pend = []
    arAV = Ovl(0)
    A_tm = arAV.sbuf_tensor("A_tm", [128, NT, D], BF16)
    Vt = arAV.sbuf_tensor("Vt", [128, 3, HEADS, 129], BF16)
    S1m = arAV.sbuf_tensor("S1m", [128, 2, 256], BF16)
    S2m = arAV.sbuf_tensor("S2m", [6, 2, 256], BF16)
    AVSZ = (arAV.off + 31) & ~31

    arAT = Ovl(AVSZ)
    x_ln = arAT.sbuf_tensor("x_ln", [128, KD, HALO], BF16)
    qphi = arAT.sbuf_tensor("qphi", [128, PAIRS, HALO], BF16)
    kphi = arAT.sbuf_tensor("kphi", [128, PAIRS, HALO], BF16)
    wqp = arAT.sbuf_tensor("wqp", [128, KD, 512], BF16)
    wkp = arAT.sbuf_tensor("wkp", [128, KD, 512], BF16)
    wv_sb = arAT.sbuf_tensor("wv_sb", [128, KD, D], BF16)
    xin = arAT.sbuf_tensor("xin", [128, 2, D], F32)
    xhalo6 = arAT.sbuf_tensor("xh6", [6, D], F32)
    XLNTM_OFF = arAT.off
    xlntm = arAT.sbuf_tensor("xlntm", [128, D], F32)
    stage32 = arAT.sbuf_tensor("stage32", [128, 2, D], F32)
    wqk_b = arAT.sbuf_tensor("wqk_b", [128, 2, D], BF16)
    elu_t = arAT.sbuf_tensor("elu_t", [128, 2, 512], F32)

    arP23 = Ovl(AVSZ)
    A_fm = arP23.sbuf_tensor("A_fm", [128, KD, TOK], BF16)
    opfm = arP23.sbuf_tensor("opfm", [128, 8, TOK], BF16)
    wo_sb = arP23.sbuf_tensor("wo_sb", [128, KD, D], BF16)
    y_sb = arP23.sbuf_tensor("y_sb", [128, 2, D], F32)
    yn_sb = arP23.sbuf_tensor("yn_sb", [128, 2, D], F32)

    arP4 = Ovl(0)
    x3_sb = arP4.sbuf_tensor("x3_sb", [128, D], F32)
    Ax_tm = arP4.sbuf_tensor("Ax_tm", [128, 2, D], FP16)
    Axf = arP4.sbuf_tensor("Axf", [128, 2, KD, 128], BF16)
    u05h = arP4.sbuf_tensor("u05h", [128, 2, D], F32)
    wq2r = arP4.sbuf_tensor("wq2r", [128, 4, D], BF16)
    arP4.sbuf_tensor("pad4", [128, 2048], mybir.dt.uint8)
    h_sb = arP4.sbuf_tensor("h_sb", [128, 2, HID], BF16)
    Ah_tm = arP4.sbuf_tensor("Ah_tm", [128, HID], FP16)
    wq1g = arP4.sbuf_tensor("wq1g", [128, KD, HID], BF16)
    Ahfm = arP4.sbuf_tensor("Ahfm", [128, KH, 512], BF16)
    o2fm = arP4.sbuf_tensor("o2fm", [128, 8, 512], BF16)
    y2_sb = arP4.sbuf_tensor("y2_sb", [128, D], BF16)
    yn4_sb = arP4.sbuf_tensor("yn4_sb", [128, D], BF16)
    yout_sb = arP4.sbuf_tensor("yout", [128, 2, D], F32)

    arWQ = Ovl(XLNTM_OFF)
    w23s = arWQ.sbuf_tensor("w23s", [128, 3, D], F32)
    wqt_o = arWQ.sbuf_tensor("wqt_o", [128, 3, D], BF16)
    wqtmp = arWQ.sbuf_tensor("wqtmp", [128, 2, D], BF16)

    OVLSZ = max(arAT.off, arP23.off, arP4.off, arWQ.off)
    ovl = st.enter_context(nc.sbuf_tensor("ovl", [128, OVLSZ], mybir.dt.uint8))
    ovl_base = nc.lookup_mloc(ovl).addr
    for mloc, off in pend:
        mloc.addr = ovl_base + off

    cbias = SB("cbias", [128, 4], F32)
    with nc.Block("initc") as iblk:
        @iblk.gpsimd
        def _(g):
            g.memset(cbias[:, 0:1], -OFF)
            g.memset(cbias[:, 1:2], OFF - 0.5)
            g.memset(cbias[:, 2:3], EPS_LN)
            g.memset(cbias[:, 3:4], OFF + 0.5)
    nc.const_aps.register(F32, -OFF, cbias[:, 0:1])
    nc.const_aps.register(F32, OFF - 0.5, cbias[:, 1:2])
    nc.const_aps.register(F32, EPS_LN, cbias[:, 2:3])
    nc.const_aps.register(F32, OFF + 0.5, cbias[:, 3:4])

    ctr_h, qctr_h = {}, {}
    for e in ENGS:
        ctr_h[e] = st.enter_context(nc.semaphore(f"c_{e}"))
        qctr_h[e] = [st.enter_context(nc.semaphore(f"q_{e}{i}")) for i in range(4)]

    P = Prog()

    _dc = {}
    for _nm, _dst, _src in (("m1", mask1, mask1_in), ("m2", mask2, mask2_in),
                            ("idf", idf, idf_in), ("idb", idb, idb_in),
                            ("idh", idh, idh_in), ("edge", edge, edge_in),
                            ("bout", b_out, b_out_in), ("proj", proj_f, proj_in)):
        def f_dc(sy, sem, count_only=False, _dst=_dst, _src=_src):
            if count_only:
                return 1
            sy.dma_start(_dst[:], _src.ap()).then_inc(sem, 16)
        _dc[_nm] = P.dma("sy", [], f_dc, f"dc_{_nm}")
    def f_dcx(sy, sem, count_only=False):
        if count_only:
            return 1
        sy.dma_start(xhalo6[0:3, :], x_in.ap()[0:3, :]).then_inc(sem, 16)
    _dc["xh6a"] = P.dma("sy", [], f_dcx, "dc_xh6a")
    dc_m1, dc_m2, dc_idf, dc_idb = _dc["m1"], _dc["m2"], _dc["idf"], _dc["idb"]
    dc_idh, dc_edge, dc_bout, dc_proj = _dc["idh"], _dc["edge"], _dc["bout"], _dc["proj"]
    dc_xh6a = _dc["xh6a"]

    def f_init0(v):
        v.memset(ones_col[:], 1.0)
        v.drain()
        v.memset(ones_row[:], 1.0)
        v.drain()
        v.memset(Vt[:, :, :, 128:129], 1.0)
        v.drain()
        return v.memset(S2m[:, :, :], 0.0)
    init0 = P.step("dv", [], f_init0, "init0")

    xinld = [None] * NT
    xst = [None] * NT; xsq = [None] * NT; xfin = [None] * NT
    xapply = [None] * NT; xcast = [None] * NT
    t1 = [None] * NT; t1e = [None] * NT

    for t in range(NT):
        def f_xin(sy, sem, count_only=False, t=t):
            if count_only:
                return 1
            sy.dma_start(xin[:, t % 2, :],
                         x_in.ap()[3 + 128 * t:3 + 128 * (t + 1), :]).then_inc(sem, 16)
        deps = [xst[t - 2], xapply[t - 2], xcast[t - 2]] if t >= 2 else []
        xinld[t] = P.dma("sy" if t % 2 == 0 else "se", deps, f_xin, f"xin{t}")

        L = LNS["l1"][t % 2]

        def f_xst(v, t=t, L=L):
            v.bn_stats(L["bn"][:, 0:6], xin[:, t % 2, 0:512])
            v.bn_stats(L["bn"][:, 6:12], xin[:, t % 2, 512:1024])
            v.drain()
            v.bn_aggr(L["ag"][:], L["bn"][:])
            v.drain()
            return v.tensor_scalar(L["rs"][:], L["ag"][:, 1:2], EPS_LN, None, ALU.add)
        xst[t] = P.step("dv", [xinld[t], xapply[t - 2] if t >= 2 else None],
                        f_xst, f"xst{t}")

        def f_xsq(s, L=L):
            return s.activation(L["rs"][:], L["rs"][:], AF.Sqrt, scale=1.0)
        xsq[t] = P.step("se", [xst[t]], f_xsq, f"xsq{t}")

        def f_xfin(v, L=L):
            v.reciprocal(L["rs"][:], L["rs"][:])
            v.drain()
            return v.scalar_tensor_tensor(L["nm"][:], L["ag"][:, 0:1], -1.0, L["rs"][:],
                                          ALU.mult, ALU.mult)
        xfin[t] = P.step("dv", [xsq[t]], f_xfin, f"xfin{t}")

        def f_xapply(s, t=t, L=L):
            return s.activation(xlntm[:, :], xin[:, t % 2, :], AF.Identity,
                                bias=L["nm"][:], scale=L["rs"][:])
        xapply[t] = P.step("se", [xfin[t], t1[t - 1] if t >= 1 else None],
                           f_xapply, f"xap{t}")

        def f_xcast(v, t=t):
            return v.tensor_copy(x_tm[:, t, :], xin[:, t % 2, :])
        xcast[t] = P.step("dv", [xinld[t]], f_xcast, f"xcast{t}")

        def f_t1(te, t=t):
            r = None
            for k in range(KD):
                half, q = k // 4, k % 4
                r = te.transpose(pB[:, 1024 * half + 128 * q:1024 * half + 128 * q + 128],
                                 xlntm[:, 128 * k:128 * k + 128], idf[:])
            return r
        t1[t] = P.step("pe", [xapply[t], dc_idf, t1e[t - 1] if t >= 1 else None],
                       f_t1, f"t1_{t}")

        def f_t1e(s, t=t):
            r = None
            for half in range(2):
                src = pB[:, 1024 * half:1024 * half + 512]
                dst = x_ln[:, 4 * half:4 * half + 4, 3 + 128 * t:3 + 128 * t + 128]
                r = s.activation(dst, src.rearrange("p (k n) -> p k n", k=4),
                                 AF.Copy, scale=1.0)
                if half == 0:
                    s.drain()
            return r
        t1e[t] = P.step("se", [t1[t]], f_t1e, f"t1e{t}")

    def f_xin6(sy, sem, count_only=False):
        if count_only:
            return 1
        sy.dma_start(xhalo6[3:6, :], x_in.ap()[2051:2054, :]).then_inc(sem, 16)
    xin6 = P.dma("sy", [], f_xin6, "xin6")

    L6 = dict(bn=SB("l6bn", [6, 12], F32), ag=SB("l6ag", [6, 2], F32),
              rs=SB("l6rs", [6, 1], F32), nm=SB("l6nm", [6, 1], F32))

    def f_xst6(v):
        v.bn_stats(L6["bn"][0:6, 0:6], xhalo6[:, 0:512])
        v.bn_stats(L6["bn"][0:6, 6:12], xhalo6[:, 512:1024])
        v.drain()
        v.bn_aggr(L6["ag"][0:6, :], L6["bn"][0:6, :])
        v.drain()
        return v.tensor_scalar(L6["rs"][0:6, :], L6["ag"][0:6, 1:2], EPS_LN, None,
                               ALU.add)
    xst6 = P.step("dv", [dc_xh6a, xin6], f_xst6, "xst6")

    def f_xsq6(s):
        return s.activation(L6["rs"][0:6, :], L6["rs"][0:6, :], AF.Sqrt, scale=1.0)
    xsq6 = P.step("se", [xst6], f_xsq6, "xsq6")

    def f_xfin6(v):
        v.reciprocal(L6["rs"][0:6, :], L6["rs"][0:6, :])
        v.drain()
        return v.scalar_tensor_tensor(L6["nm"][0:6, :], L6["ag"][0:6, 0:1], -1.0,
                                      L6["rs"][0:6, :], ALU.mult, ALU.mult)
    xfin6 = P.step("dv", [xsq6], f_xfin6, "xfin6")

    def f_xap6(s):
        return s.activation(xhalo6[:, :], xhalo6[:, :], AF.Identity,
                            bias=L6["nm"][0:6, :], scale=L6["rs"][0:6, :])
    xap6 = P.step("se", [xfin6], f_xap6, "xap6")

    def f_t16(te):
        r = None
        for k in range(KD):
            r = te.transpose(pA[:, 1024 + 12 * k:1024 + 12 * k + 6],
                             xhalo6[:, 128 * k:128 * k + 128], idf[0:6, 0:6])
        return r
    t16 = P.step("pe", [xap6, dc_idf], f_t16, "t16")

    def f_t16e(s):
        r = None
        for k in range(KD):
            s.activation(x_ln[:, k, 0:3], pA[:, 1024 + 12 * k:1024 + 12 * k + 3],
                         AF.Copy, scale=1.0)
            s.drain()
            r = s.activation(x_ln[:, k, 2051:2054], pA[:, 1024 + 12 * k + 3:1024 + 12 * k + 6],
                             AF.Copy, scale=1.0)
            if k < KD - 1:
                s.drain()
        return r
    t16e = P.step("se", [t16], f_t16e, "t16e")

    def f_pjc(v):
        return v.tensor_copy(proj_b[:], proj_f[:])
    pjc = P.step("dv", [dc_proj], f_pjc, "pjc")

    wq_ld = [None] * 16; wq_cast = [None] * 16
    wqmm = [None] * 16; wqev = [None] * 16
    for i in range(16):
        def f_wqld(gp, sem, count_only=False, i=i):
            if count_only:
                return 1
            gp.dma_start(stage32[:, i % 2, :],
                         wqkT_in.ap()[128 * i:128 * (i + 1), :]).then_inc(sem, 16)
        wq_ld[i] = P.dma("gp", [wq_cast[i - 2]] if i >= 2 else [], f_wqld, f"wqld{i}")

        def f_wqcast(v, i=i):
            return v.tensor_copy(wqk_b[:, i % 2, :], stage32[:, i % 2, :])
        wq_cast[i] = P.step("dv", [wq_ld[i], wqmm[i - 2] if i >= 2 else None],
                            f_wqcast, f"wqc{i}")

        def f_wqmm(te, i=i):
            bank = pA[:, 512 * (i % 2):512 * (i % 2) + 512]
            r = None
            for kd in range(KD):
                r = te.matmul(bank[:, 64 * kd:64 * kd + 64],
                              wqk_b[:, i % 2, 128 * kd:128 * kd + 128],
                              proj_b[:], start=True, stop=True)
            return r
        wqmm[i] = P.step("pe", [wq_cast[i], pjc, wqev[i - 2] if i >= 2 else None],
                         f_wqmm, f"wqmm{i}")

        def f_wqev(s, i=i):
            qk, h = divmod(i, 8)
            wt = wqp if qk == 0 else wkp
            bank = pA[:, 512 * (i % 2):512 * (i % 2) + 512]
            p, hh = divmod(h, 2)
            dst = wt[:, 0:KD, 128 * p + 64 * hh:128 * p + 64 * hh + 64]
            return s.activation(dst, bank.rearrange("p (k f) -> p k f", k=KD),
                                AF.Copy, scale=1.0)
        wqev[i] = P.step("se", [wqmm[i]], f_wqev, f"wqev{i}")

    wv_ld = [None] * 8; wv_cast = [None] * 8
    for j in range(8):
        def f_wvld(gp, sem, count_only=False, j=j):
            if count_only:
                return 1
            gp.dma_start(stage32[:, j % 2, :],
                         w_v_in.ap()[128 * j:128 * (j + 1), :]).then_inc(sem, 16)
        prev = wq_cast[14 + j] if 14 + j < 16 else wv_cast[j - 2]
        wv_ld[j] = P.dma("gp", [prev], f_wvld, f"wvld{j}")

        def f_wvcast(v, j=j):
            return v.tensor_copy(wv_sb[:, j, :], stage32[:, j % 2, :])
        wv_cast[j] = P.step("dv", [wv_ld[j]], f_wvcast, f"wvc{j}")

    wab_ld = [None] * 64
    wab_abs = [None] * 64

    def make_wab(k):
        which = "w2" if k < 32 else "w1"
        j = k if k < 32 else k - 32

        def f_ld(sy, sem, count_only=False, which=which, j=j, k=k):
            if count_only:
                return 1
            if which == "w2":
                s_ = w2T_in.ap()[128 * j:128 * (j + 1), :]
            else:
                dk, q = divmod(j, 4)
                s_ = w1T_in.ap()[128 * dk:128 * (dk + 1), 1024 * q:1024 * (q + 1)]
            sy.dma_start(stage32[:, k % 2, :], s_).then_inc(sem, 16)
        if k == 0:
            pdep = [wv_cast[6]]
        elif k == 1:
            pdep = [wv_cast[7]]
        else:
            pdep = [wab_abs[k - 2]]
        wab_ld[k] = P.dma("sy" if k % 2 == 0 else "se", pdep, f_ld, f"wabl{k}")

        def f_abs(s, which=which, j=j, k=k):
            acc = w2acc if which == "w2" else w1acc
            return s.activation(stage32[:, k % 2, :], stage32[:, k % 2, :], AF.Abs,
                                scale=1.0, accum_out=acc[:, j:j + 1])
        wab_abs[k] = P.step("se", [wab_ld[k]], f_abs, f"waba{k}")


    phimm = [None] * 40; phimin = [None] * 40
    phiexp = [None] * 40; phic = [None] * 40
    idx = 0
    for qk in range(2):
        for p in range(PAIRS):
            for ntile in range(5):
                n0 = 512 * ntile
                w = PHI_NT[ntile]

                def f_phimm(te, qk=qk, p=p, n0=n0, w=w, idx=idx):
                    wt = wqp if qk == 0 else wkp
                    bank = pA[:, 512 * (idx % 4):512 * (idx % 4) + w]
                    r = None
                    for kd in range(KD):
                        r = te.matmul(bank, wt[:, kd, 128 * p:128 * p + 128],
                                      x_ln[:, kd, n0:n0 + w],
                                      start=(kd == 0), stop=(kd == KD - 1))
                    return r
                deps = [t1e[NT - 1], t16e, wqev[7 if qk == 0 else 15]]
                if idx >= 4:
                    deps.append(phic[idx - 4])
                elif idx < 2:
                    deps.append(wqev[14 + idx])
                phimm[idx] = P.step("pe", deps, f_phimm, f"phimm{idx}")

                def f_phimin(v, w=w, idx=idx):
                    bank = pA[:, 512 * (idx % 4):512 * (idx % 4) + w]
                    return v.tensor_scalar(elu_t[:, 0, 0:w], bank, 0.0, None, ALU.min)
                phimin[idx] = P.step("dv", [phimm[idx],
                                            phiexp[idx - 1] if idx >= 1 else None],
                                     f_phimin, f"phimin{idx}")

                def f_phiexp(s, w=w):
                    return s.activation(elu_t[:, 1, 0:w], elu_t[:, 0, 0:w], AF.Exp,
                                       scale=1.0)
                phiexp[idx] = P.step("se", [phimin[idx],
                                            phic[idx - 1] if idx >= 1 else None],
                                     f_phiexp, f"phiexp{idx}")

                def f_phic(v, qk=qk, p=p, n0=n0, w=w, idx=idx):
                    dstT = qphi if qk == 0 else kphi
                    bank = pA[:, 512 * (idx % 4):512 * (idx % 4) + w]
                    return v.scalar_tensor_tensor(dstT[:, p, n0:n0 + w], bank, 1.0,
                                                  elu_t[:, 1, 0:w], ALU.add, ALU.max)
                phic[idx] = P.step("dv", [phiexp[idx]], f_phic, f"phic{idx}")
                for _k in range((idx * 64) // 40, ((idx + 1) * 64) // 40):
                    make_wab(_k)
                idx += 1

    def f_kedge(v):
        r = None
        for p in range(PAIRS):
            v.tensor_tensor(kphi[:, p, 0:3], kphi[:, p, 0:3], edge[:, 0:3], ALU.mult)
            v.drain()
            r = v.tensor_tensor(kphi[:, p, 2051:2054], kphi[:, p, 2051:2054],
                                edge[:, 3:6], ALU.mult)
            if p < PAIRS - 1:
                v.drain()
        return r
    kedge = P.step("dv", [phic[39], dc_edge], f_kedge, "kedge")
    def f_wcl(v):
        v.reduce_sum(wtot2[:], w2acc[:], axis=AX.X)
        v.drain()
        return v.reduce_sum(wtot1[:], w1acc[:], axis=AX.X)
    wcl = P.step("dv", [wab_abs[-1]], f_wcl, "wcl")

    def f_wsm(te):
        te.matmul(pA[0:1, 0:1], wtot2[:], ones_col[:], start=True, stop=True)
        return te.matmul(pA[0:1, 64:65], wtot1[:], ones_col[:], start=True, stop=True)
    wsm = P.step("pe", [wcl, init0, phic[36]], f_wsm, "wsm")

    def f_wsv(v):
        v.tensor_scalar(sv2[:], pA[0:1, 0:1], 1.0 / (HID * D), EPS_Q, ALU.mult, ALU.max)
        v.drain()
        return v.tensor_scalar(sv1[:], pA[0:1, 64:65], 1.0 / (HID * D), EPS_Q,
                               ALU.mult, ALU.max)
    wsv = P.step("dv", [wsm], f_wsv, "wsv")

    def f_wsb(te):
        te.matmul(pA[:, 32:33], ones_row[:], sv2[:], start=True, stop=True)
        return te.matmul(pA[:, 96:97], ones_row[:], sv1[:], start=True, stop=True)
    wsb = P.step("pe", [wsv], f_wsb, "wsb")

    def f_wsf(v):
        v.tensor_copy(s2_t[:], pA[:, 32:33])
        v.drain()
        v.tensor_copy(s1_t[:], pA[:, 96:97])
        v.drain()
        v.tensor_scalar(rh2_t[:], pA[:, 32:33], 0.5, None, ALU.mult)
        v.drain()
        v.reciprocal(rh2_t[:], rh2_t[:])
        v.drain()
        v.tensor_scalar(rh1_t[:], pA[:, 96:97], 0.5, None, ALU.mult)
        v.drain()
        return v.reciprocal(rh1_t[:], rh1_t[:])
    wsf = P.step("dv", [wsb], f_wsf, "wsf")

    NWQ = 64
    wql = [None] * NWQ; wqle = [None] * NWQ; wqge = [None] * NWQ
    wqsub = [None] * NWQ; wqs = [None] * NWQ

    def make_quant(k):
        which = "w1" if k < 32 else "w2"
        j = k if k < 32 else k - 32

        parts = []
        for qt in range(4):
            def f_ld(sy, sem, count_only=False, which=which, j=j, k=k, qt=qt):
                if count_only:
                    return 1
                r0, r1 = 32 * qt, 32 * qt + 32
                if which == "w2":
                    s_ = w2T_in.ap()[128 * j + r0:128 * j + r1, :]
                else:
                    dk, q = divmod(j, 4)
                    s_ = w1T_in.ap()[128 * dk + r0:128 * dk + r1,
                                     1024 * q:1024 * (q + 1)]
                sy.dma_start(w23s[r0:r1, k % 3, :], s_).then_inc(sem, 16)
            if k < 3:
                pdep = [wab_abs[-1], t1[NT - 1]]
            else:
                pdep = [wqge[k - 3]]
            parts.append(P.dma("sy", pdep, f_ld, f"wql{k}_{qt}"))
        wql[k] = parts

        def f_le(v, which=which, k=k):
            rh = rh1_t if which == "w1" else rh2_t
            return v.tensor_scalar(wqtmp[:, k % 2, :], w23s[:, k % 3, :], rh[:], -1.0,
                                   ALU.mult, ALU.is_le)
        ldeps = list(wql[k]) + [wsf]
        if k < 2:
            ldeps.append(phic[39])
        else:
            ldeps.append(wqsub[k - 2])
        wqle[k] = P.step("dv", ldeps, f_le, f"wqle{k}")

        def f_ge(v, which=which, k=k):
            rh = rh1_t if which == "w1" else rh2_t
            return v.tensor_scalar(wqt_o[:, k % 3, :], w23s[:, k % 3, :], rh[:], 1.0,
                                   ALU.mult, ALU.is_ge)
        gdeps = [wqle[k]]
        if k < 2:
            gdeps.append(wqmm[15])
        else:
            gdeps.append(wqsub[k - 2])
        if k >= 3:
            gdeps += wqs[k - 3]
        wqge[k] = P.step("dv", gdeps, f_ge, f"wqge{k}")

        def f_sub(gp, k=k):
            return gp.tensor_tensor(wqt_o[:, k % 3, :], wqt_o[:, k % 3, :],
                                    wqtmp[:, k % 2, :], ALU.subtract)
        wqsub[k] = P.step("gp", [wqge[k]], f_sub, f"wqsub{k}")

        sparts = []
        for sh in range(2):
            def f_st(s, sem, count_only=False, which=which, j=j, k=k, sh=sh):
                if count_only:
                    return 1
                r0, r1 = 64 * sh, 64 * sh + 64
                if which == "w2":
                    dst = wq2_scr.ap()[128 * j + r0:128 * j + r1, :]
                else:
                    dk, q = divmod(j, 4)
                    dst = wq1_scr.ap()[128 * dk + r0:128 * dk + r1,
                                       1024 * q:1024 * (q + 1)]
                s.dma_start(dst, wqt_o[r0:r1, k % 3, :]).then_inc(sem, 16)
            sparts.append(P.dma("se", [wqsub[k]], f_st, f"wqs{k}_{sh}"))
        wqs[k] = sparts

    vmm = [None] * NVT; vev = [None] * NVT
    NH = NT * PAIRS
    smm = [None] * NH; sev = [None] * NH
    amm = [None] * NH; aev = [None] * NH

    def make_v(i):
        rows = 128 if i < 16 else 6

        def f_vmm(te, i=i, rows=rows):
            half = pA[:, 0:1024]
            r = None
            for kd in range(KD):
                xs = x_ln[:, kd, 128 * i:128 * i + rows]
                r = te.matmul(half[0:rows, 0:512], xs, wv_sb[:, kd, 0:512],
                              start=(kd == 0), stop=(kd == KD - 1))
            for kd in range(KD):
                xs = x_ln[:, kd, 128 * i:128 * i + rows]
                r = te.matmul(half[0:rows, 512:1024], xs, wv_sb[:, kd, 512:1024],
                              start=(kd == 0), stop=(kd == KD - 1))
            return r
        deps = [t1e[NT - 1], t16e, wv_cast[7], wsf]
        if i == 0:
            deps += [phic[36], phic[37]]
        else:
            deps.append(vev[i - 1])
        vmm[i] = P.step("pe", deps, f_vmm, f"vmm{i}")

        def f_vev(s, i=i, rows=rows):
            half = pA[:, 0:1024]
            return s.activation(Vt[0:rows, i % 3, 0:8, 0:128],
                                half[0:rows, :].rearrange("p (h n) -> p h n", h=8),
                                AF.Copy, scale=1.0)
        vdeps = [vmm[i], init0]
        if i >= 3:
            vdeps.append(aev[4 * (i - 3) + 3])
        vev[i] = P.step("se", vdeps, f_vev, f"vev{i}")

    for i in range(3):
        make_v(i)

    def make_amm(j):
        c2, p2 = divmod(j, PAIRS)
        h0, h1 = 2 * p2, 2 * p2 + 1

        def f_amm(te, c=c2, i=j, h0=h0, h1=h1):
            sb = i % 2
            ab = i % 2
            bank = pB[:, 1024 + 512 * ab:1024 + 512 * ab + 512]
            te.matmul(bank[:, 0:129], S1m[:, sb, 0:128], Vt[:, c % 3, h0, :],
                      start=True, stop=False)
            te.matmul(bank[:, 0:129], S2m[0:6, sb, 0:128],
                      Vt[0:6, (c + 1) % 3, h0, :], start=False, stop=True)
            te.matmul(bank[:, 256:385], S1m[:, sb, 128:256], Vt[:, c % 3, h1, :],
                      start=True, stop=False)
            return te.matmul(bank[:, 256:385], S2m[0:6, sb, 128:256],
                             Vt[0:6, (c + 1) % 3, h1, :], start=False, stop=True)
        adeps = [sev[j], vev[c2], vev[c2 + 1]]
        if j >= 2:
            adeps.append(aev[j - 2])
        amm[j] = P.step("pe", adeps, f_amm, f"amm{j}")

        def f_aev(v, c=c2, i=j, h0=h0, h1=h1):
            ab = i % 2
            bank = pB[:, 1024 + 512 * ab:1024 + 512 * ab + 512]
            v.tensor_scalar(za[:], bank[:, 128:129], EPS_Z, None, ALU.add)
            v.tensor_scalar(zb[:], bank[:, 384:385], EPS_Z, None, ALU.add)
            v.drain()
            v.reciprocal(za[:], za[:])
            v.reciprocal(zb[:], zb[:])
            v.drain()
            v.tensor_scalar(A_tm[:, c, 128 * h0:128 * h0 + 128], bank[:, 0:128],
                            za[:], None, ALU.mult)
            return v.tensor_scalar(A_tm[:, c, 128 * h1:128 * h1 + 128],
                                   bank[:, 256:384], zb[:], None, ALU.mult)
        aev[j] = P.step("dv", [amm[j]], f_aev, f"aev{j}")

    for c in range(NT):
        for p in range(PAIRS):
            i = c * PAIRS + p

            def f_smm(te, c=c, p=p, i=i):
                base = pB[:, 0:1024] if (i % 2 == 0) else pA[:, 1024:2048]
                bankA, bankB = base[:, 0:512], base[:, 512:1024]
                qap0 = qphi[0:64, p, 128 * c + 3:128 * c + 131]
                qap1 = qphi[64:128, p, 128 * c + 3:128 * c + 131]
                te.matmul(bankA[:, 0:128], kphi[0:64, p, 128 * c:128 * c + 128],
                          qap0, start=True, stop=True)
                te.matmul(bankB[:, 0:128], kphi[64:128, p, 128 * c:128 * c + 128],
                          qap1, start=True, stop=True)
                te.matmul(bankA[0:6, 128:256],
                          kphi[0:64, p, 128 * c + 128:128 * c + 134], qap0,
                          start=True, stop=True)
                return te.matmul(bankB[0:6, 128:256],
                                 kphi[64:128, p, 128 * c + 128:128 * c + 134], qap1,
                                 start=True, stop=True)
            sdeps = [kedge]
            if i == 1:
                sdeps += [phic[38], phic[39], wsf]
            if i >= 2:
                sdeps.append(sev[i - 2])
            smm[i] = P.step("pe", sdeps, f_smm, f"smm{i}")

            def f_sev(v, i=i):
                sb = i % 2
                base = pB[:, 0:1024] if (i % 2 == 0) else pA[:, 1024:2048]
                bankA, bankB = base[:, 0:512], base[:, 512:1024]
                v.tensor_tensor(S1m[:, sb, 0:128], bankA[:, 0:128], mask1[:], ALU.mult)
                v.tensor_tensor(S1m[:, sb, 128:256], bankB[:, 0:128], mask1[:], ALU.mult)
                v.tensor_tensor(S2m[0:6, sb, 122:128], bankA[0:6, 250:256],
                                mask2[0:6, 122:128], ALU.mult)
                return v.tensor_tensor(S2m[0:6, sb, 250:256], bankB[0:6, 250:256],
                                       mask2[0:6, 122:128], ALU.mult)
            svdeps = [smm[i], init0, dc_m1, dc_m2]
            if i >= 2:
                svdeps.append(amm[i - 2])
            sev[i] = P.step("dv", svdeps, f_sev, f"sev{i}")

            if i >= 1:
                make_amm(i - 1)
            if p == 0 and c >= 1 and c + 2 <= NVT - 1:
                make_v(c + 2)

        if c >= 2:
            for _k in range(((c - 2) * NWQ) // 14, ((c - 1) * NWQ) // 14):
                make_quant(_k)

    make_amm(NT * PAIRS - 1)


    wo_ld = [None] * KD; wo_cast = [None] * KD
    for j in range(KD):
        def f_wold(gp, sem, count_only=False, j=j):
            if count_only:
                return 1
            gp.dma_start(w23s[:, j % 3, :],
                         w_out_in.ap()[128 * j:128 * (j + 1), :]).then_inc(sem, 16)
        deps = [wqge[NWQ - 3 + j]] if j < 3 else [wo_cast[j - 3]]
        wo_ld[j] = P.dma("gp", deps, f_wold, f"wold{j}")

        def f_wocast(v, j=j):
            return v.tensor_copy(wo_sb[:, j, :], w23s[:, j % 3, :])
        wo_cast[j] = P.step("dv", [wo_ld[j]] + ([smm[NH - 1]] if j < 2 else []),
                            f_wocast, f"woc{j}")

    t2 = [None] * NT; t2e = [None] * NT
    for c in range(NT):
        def f_t2(te, c=c):
            bank = pA[:, 512 * (c % 2):512 * (c % 2) + 512].bitcast(BF16)
            r = None
            for k in range(KD):
                r = te.transpose(bank[:, 128 * k:128 * k + 128],
                                 A_tm[:, c, 128 * k:128 * k + 128], idb[:])
            return r
        deps = [aev[4 * c + 3], vev[NVT - 1], dc_idb]
        if c >= 2:
            deps.append(t2e[c - 2])
        t2[c] = P.step("pe", deps, f_t2, f"t2_{c}")

        def f_t2e(s, c=c):
            bank = pA[:, 512 * (c % 2):512 * (c % 2) + 512].bitcast(BF16)
            dst = A_fm[:, :, 128 * c:128 * c + 128]
            return s.activation(dst, bank.rearrange("p (k n) -> p k n", k=KD),
                                AF.Copy, scale=1.0)
        t2e[c] = P.step("se", [t2[c]], f_t2e, f"t2e{c}")

    womm = {}; woev = {}
    for m in range(8):
        for n in range(4):
            def f_womm(te, m=m, n=n):
                bank = pB[:, 512 * n:512 * n + 512]
                r = None
                for kd in range(KD):
                    r = te.matmul(bank, wo_sb[:, kd, 128 * m:128 * m + 128],
                                  A_fm[:, kd, 512 * n:512 * n + 512],
                                  start=(kd == 0), stop=(kd == KD - 1))
                return r
            deps = [wo_cast[KD - 1], t2e[4 * n + 3], aev[NH - 1]]
            if m >= 1:
                deps.append(woev[(m - 1, n)])
            womm[(m, n)] = P.step("pe", deps, f_womm, f"womm{m}_{n}")

            def f_woev(s, m=m, n=n):
                bank = pB[:, 512 * n:512 * n + 512]
                return s.activation(opfm[:, m, 512 * n:512 * n + 512], bank, AF.Identity,
                                    bias=b_out[:, m:m + 1], scale=1.0)
            woev[(m, n)] = P.step("se", [womm[(m, n)], dc_bout], f_woev, f"woev{m}_{n}")

    t3 = [None] * NT; y_ev = [None] * NT
    ln2s = [None] * NT; ln2q = [None] * NT; ln2f = [None] * NT
    ln2a = [None] * NT; res2 = [None] * NT
    ln3s = [None] * NT; ln3q = [None] * NT; ln3f = [None] * NT; ln3a = [None] * NT
    qx_mx = [None] * NT; qx_sc = [None] * NT; qx_u = [None] * NT; qx_c = [None] * NT
    qx_b = [None] * NT; qx_r = [None] * NT
    for c in range(NT):
        def f_t3(te, c=c):
            bank = pA[:, 512 * (c % 2):512 * (c % 2) + 512].bitcast(BF16)
            r = None
            for m in range(8):
                r = te.transpose(bank[:, 128 * m:128 * m + 128],
                                 opfm[:, m, 128 * c:128 * c + 128], idb[:])
            return r
        deps = [woev[(7, c // 4)], t2e[NT - 1]]
        if c >= 2:
            deps.append(y_ev[c - 2])
        t3[c] = P.step("pe", deps, f_t3, f"t3_{c}")

        def f_yev(s, c=c):
            bank = pA[:, 512 * (c % 2):512 * (c % 2) + 512].bitcast(BF16)
            return s.activation(y_sb[:, c % 2, :], bank, AF.Copy, scale=1.0)
        ydeps = [t3[c]]
        if c >= 2:
            ydeps += [ln2a[c - 2], ln2s[c - 2]]
        y_ev[c] = P.step("se", ydeps, f_yev, f"yev{c}")

        L = LNS["l2"][c % 2]

        def f_ln2s(v, c=c, L=L):
            v.bn_stats(L["bn"][:, 0:6], y_sb[:, c % 2, 0:512])
            v.bn_stats(L["bn"][:, 6:12], y_sb[:, c % 2, 512:1024])
            v.drain()
            v.bn_aggr(L["ag"][:], L["bn"][:])
            v.drain()
            return v.tensor_scalar(L["rs"][:], L["ag"][:, 1:2], EPS_LN, None, ALU.add)
        ln2s[c] = P.step("dv", [y_ev[c]], f_ln2s, f"ln2s{c}")

        def f_ln2q(s, L=L):
            return s.activation(L["rs"][:], L["rs"][:], AF.Sqrt, scale=1.0)
        ln2q[c] = P.step("se", [ln2s[c]], f_ln2q, f"ln2q{c}")

        def f_ln2f(v, L=L):
            v.reciprocal(L["rs"][:], L["rs"][:])
            v.drain()
            return v.scalar_tensor_tensor(L["nm"][:], L["ag"][:, 0:1], -1.0, L["rs"][:],
                                          ALU.mult, ALU.mult)
        ln2f[c] = P.step("dv", [ln2q[c]], f_ln2f, f"ln2f{c}")

        def f_ln2a(s, c=c, L=L):
            return s.activation(yn_sb[:, c % 2, :], y_sb[:, c % 2, :], AF.Identity,
                                bias=L["nm"][:], scale=L["rs"][:])
        adeps = [ln2f[c]]
        if c >= 2:
            adeps.append(res2[c - 2])
        ln2a[c] = P.step("se", adeps, f_ln2a, f"ln2a{c}")

        def f_res2(gp, c=c):
            return gp.tensor_tensor(x_tm[:, c, :], x_tm[:, c, :], yn_sb[:, c % 2, :],
                                    ALU.add)
        res2[c] = P.step("gp", [ln2a[c], xcast[c]], f_res2, f"res2_{c}")


    wq1_ld = [None] * KD
    for k in range(KD):
        def f_wq1ld(gp, sem, count_only=False, k=k):
            if count_only:
                return 1
            gp.dma_start(wq1g[:, k, :],
                         wq1_scr.ap()[128 * k:128 * (k + 1), :]).then_inc(sem, 16)
        wq1_ld[k] = P.dma("gp", sum([wqs[4 * k + q] for q in range(4)], [])
                          + [womm[(7, 3)], t3[NT - 1]], f_wq1ld, f"wq1l{k}")

    t4 = [None] * NT; t4e = [None] * NT
    f1 = {}; f1g = {}
    qh_mx2 = {}; qh_sc = [None] * NT
    qh_u = {}; qh_c = {}; qh_b = {}; qh_r = {}
    t5 = {}; t5e = {}
    f2 = {}; f2ev = {}; f2evA = {}
    wq2_ld = {}
    t6 = {}; t6e = {}
    ln4s = [None] * NT; ln4q = [None] * NT; ln4f = [None] * NT; ln4a = [None] * NT
    outadd = [None] * NT; yst = [None] * NT

    def emit_lnqx(c):
        L3 = LNS["l3"][c % 2]
        Q = QXs[c % 2]

        def f_ln3s(v, c=c, L=L3):
            v.bn_stats(L["bn"][:, 0:6], x_tm[:, c, 0:512])
            v.bn_stats(L["bn"][:, 6:12], x_tm[:, c, 512:1024])
            v.drain()
            v.bn_aggr(L["ag"][:], L["bn"][:])
            v.drain()
            return v.tensor_scalar(L["rs"][:], L["ag"][:, 1:2], EPS_LN, None,
                                   ALU.add)
        deps = [res2[c]]
        if c >= 2:
            deps.append(ln3a[c - 2])
        ln3s[c] = P.step("dv", deps, f_ln3s, f"ln3s{c}")

        def f_ln3q(s, L=L3):
            return s.activation(L["rs"][:], L["rs"][:], AF.Sqrt, scale=1.0)
        ln3q[c] = P.step("se", [ln3s[c]], f_ln3q, f"ln3q{c}")

        def f_ln3f(v, L=L3):
            v.reciprocal(L["rs"][:], L["rs"][:])
            v.drain()
            return v.scalar_tensor_tensor(L["nm"][:], L["ag"][:, 0:1], -1.0,
                                          L["rs"][:], ALU.mult, ALU.mult)
        ln3f[c] = P.step("dv", [ln3q[c]], f_ln3f, f"ln3f{c}")

        def f_ln3a(s, c=c, L=L3):
            return s.activation(x3_sb[:, :], x_tm[:, c, :], AF.Identity,
                                bias=L["nm"][:], scale=L["rs"][:])
        adeps = [ln3f[c]]
        if c == 0:
            adeps.append(t2[NT - 1])
        if c >= 1:
            adeps.append(qx_b[c - 1])
        ln3a[c] = P.step("se", adeps, f_ln3a, f"ln3a{c}")

        def f_qxmx(v, c=c, Q=Q):
            return v.reduce_max(Q["mx"][:], x3_sb[:, :], axis=AX.X,
                                apply_absolute_value=True)
        qx_mx[c] = P.step("dv", [ln3a[c]], f_qxmx, f"qxmx{c}")

        def f_qxsc(v, Q=Q):
            v.tensor_scalar(Q["isc"][:], Q["mx"][:], EPS_Q, 1.0 / 127.0,
                            ALU.max, ALU.mult)
            v.drain()
            v.reciprocal(Q["sc"][:], Q["isc"][:])
            return v.tensor_tensor(Q["fac"][:], s1_t[:], Q["isc"][:], ALU.mult)
        qx_sc[c] = P.step("dv", [qx_mx[c], wsf], f_qxsc, f"qxsc{c}")

        def f_qxu(v, c=c, Q=Q):
            return v.tensor_scalar(x3_sb[:, :], x3_sb[:, :],
                                   Q["sc"][:], OFF + 0.5, ALU.mult, ALU.add)
        cdeps = [qx_sc[c]]
        if c == 0:
            cdeps.append(t2[NT - 1])
        if c >= 1:
            cdeps.append(qx_r[c - 1])
        qx_u[c] = qx_c[c] = P.step("dv", cdeps, f_qxu, f"qxu{c}")

        def f_qxr(v, c=c):
            v.tensor_scalar(x3_sb[:, :].bitcast(I32), x3_sb[:, :].bitcast(I32),
                            -8192, None, ALU.bitwise_and)
            v.drain()
            return v.tensor_copy(Ax_tm[:, c % 2, :], x3_sb[:, :])
        rdeps = [qx_u[c]]
        if c >= 2:
            rdeps.append(t4[c - 2])
        qx_r[c] = qx_b[c] = P.step("dv", rdeps, f_qxr, f"qxr{c}")

    def emit_t4(c):
        g, cc = divmod(c, GC)

        def f_t4(te, c=c):
            bank = pB[:, 0:512].bitcast(FP16)
            r = None
            for k in range(KD):
                r = te.transpose(bank[:, 128 * k:128 * k + 128],
                                 Ax_tm[:, c % 2, 128 * k:128 * k + 128], idh[:])
            return r
        deps = [qx_r[c], dc_idh]
        if c == 0:
            deps.append(woev[(7, 3)])
        else:
            deps.append(t4e[c - 1])
        if c % GC == 0 and g >= 1:
            deps.append(f2ev[g - 1])
        t4[c] = P.step("pe", deps, f_t4, f"t4_{c}")

        def f_t4e(s, c=c):
            src = pB[:, 0:512].bitcast(FP16).rearrange("p (k n) -> p k n", k=KD)
            return s.activation(Axf[:, c % 2, :, :], src, AF.Identity, bias=-OFF,
                                scale=1.0)
        edeps = [t4[c]]
        if c == 0:
            edeps.append(t2[NT - 1])
        elif c >= 2:
            edeps.append(f1[(c - 2, 3)])
        t4e[c] = P.step("se", edeps, f_t4e, f"t4e{c}")

    def emit_f1(c):
        g, cc = divmod(c, GC)
        for quarter in range(4):
            def f_f1(te, c=c, quarter=quarter):
                base = 1024 * (quarter % 2)
                r = None
                for kd in range(KD):
                    for n in range(2):
                        off = 1024 * quarter + 512 * n
                        r = te.matmul(pA[:, base + 512 * n:base + 512 * n + 512],
                                      Axf[:, c % 2, kd, :],
                                      wq1g[:, kd, off:off + 512],
                                      start=(kd == 0), stop=(kd == KD - 1))
                return r
            deps = [t4e[c]] + list(wq1_ld)
            if quarter >= 2:
                deps.append(f1g[(c, quarter - 2)])
            elif c >= 1:
                deps.append(f1g[(c - 1, quarter + 2)])
            else:
                deps.append(y_ev[NT - 1])
            if quarter < 2 and c % GC == 0 and c >= 4:
                deps.append(f2evA[c // GC - 1])
            f1[(c, quarter)] = P.step("pe", deps, f_f1, f"f1_{c}_{quarter}")

            Q = QXs[c % 2]

            def f_f1g(s, c=c, quarter=quarter, Q=Q):
                base = 1024 * (quarter % 2)
                return s.activation(h_sb[:, c % 2, 1024 * quarter:1024 * quarter + 1024],
                                    pA[:, base:base + 1024], AF.Gelu, bias=0.0,
                                    scale=Q["fac"][:])
            gdeps = [f1[(c, quarter)], qx_sc[c]]
            if c < 2:
                gdeps.append(womm[(7, 3)])
            if c >= 2:
                gdeps += [qh_u[(c - 2, quarter)], qh_mx2[(c - 2, quarter // 2)]]
            f1g[(c, quarter)] = P.step("se", gdeps, f_f1g, f"f1g{c}_{quarter}")



    def emit_qhx(c):
        QH = QHs[c % 2]
        for half in range(2):
            def f_qhmx(v, c=c, half=half, QH=QH):
                return v.reduce_max(QH["mx"][:, half:half + 1],
                                    h_sb[:, c % 2, 2048 * half:2048 * half + 2048],
                                    axis=AX.X, apply_absolute_value=True)
            qh_mx2[(c, half)] = P.step("dv", [f1g[(c, 2 * half)],
                                              f1g[(c, 2 * half + 1)]], f_qhmx,
                                       f"qhmx{c}_{half}")

        def f_qhsc(v, c=c, QH=QH):
            v.tensor_tensor(QH["mx"][:, 0:1], QH["mx"][:, 0:1], QH["mx"][:, 1:2],
                            ALU.max)
            v.drain()
            v.tensor_scalar(QH["isc"][:], QH["mx"][:, 0:1], EPS_Q, 1.0 / 127.0,
                            ALU.max, ALU.mult)
            v.drain()
            v.reciprocal(QH["sc"][:], QH["isc"][:])
            return v.tensor_tensor(fac2buf[:, c % 8:c % 8 + 1], s2_t[:],
                                   QH["isc"][:], ALU.mult)
        qh_sc[c] = P.step("dv", [qh_mx2[(c, 0)], qh_mx2[(c, 1)], wsf],
                          f_qhsc, f"qhsc{c}")

    def emit_strips(c):
        QH = QHs[c % 2]
        for q in range(4):
            def f_qhu(v, c=c, q=q, QH=QH):
                return v.tensor_scalar(u05h[:, q % 2, :],
                                       h_sb[:, c % 2, 1024 * q:1024 * q + 1024],
                                       QH["sc"][:], OFF + 0.5, ALU.mult, ALU.add)
            ueng = "dv"
            deps = [qh_sc[c]]
            if q >= 2:
                deps.append(qh_r[(c, q - 2)])
            elif c >= 1:
                deps.append(qh_r[(c - 1, q + 2)])
            elif q < 2:
                deps.append(t2[NT - 1])
            qh_u[(c, q)] = qh_c[(c, q)] = P.step(ueng, deps, f_qhu, f"qhu{c}_{q}")

            def f_qhr(v, c=c, q=q):
                v.tensor_scalar(u05h[:, q % 2, :].bitcast(I32),
                                u05h[:, q % 2, :].bitcast(I32),
                                -8192, None, ALU.bitwise_and)
                v.drain()
                return v.tensor_copy(Ah_tm[:, 1024 * q:1024 * q + 1024],
                                     u05h[:, q % 2, :])
            deps = [qh_u[(c, q)]]
            if c >= 1:
                deps.append(t5[(c - 1, 3)])
            elif q == 0:
                deps.append(womm[(7, 3)])
            qh_r[(c, q)] = qh_b[(c, q)] = P.step("dv", deps, f_qhr, f"qhr{c}_{q}")

    def emit_tfb(c):
        g, cc = divmod(c, GC)
        for j in range(4):
            def f_t5(te, c=c, j=j):
                bank = pB[:, 512 + 512 * (j % 2):1024 + 512 * (j % 2)].bitcast(FP16)
                r = None
                for b in range(8):
                    k = 8 * j + b
                    r = te.transpose(bank[:, 128 * b:128 * b + 128],
                                     Ah_tm[:, 128 * k:128 * k + 128], idh[:])
                return r
            deps = [qh_r[(c, j)]]
            if j == 0:
                if c >= 1:
                    deps.append(t5e[(c - 1, 2)])
            elif j == 1:
                if c >= 1:
                    deps.append(t5e[(c - 1, 3)])
            else:
                deps.append(t5e[(c, j - 2)])
            t5[(c, j)] = P.step("pe", deps, f_t5, f"t5_{c}_{j}")

            def f_t5e(v, c=c, cc=cc, j=j):
                src = pB[:, 512 + 512 * (j % 2):1024 + 512 * (j % 2)].bitcast(FP16)
                dst = Ahfm[:, 8 * j:8 * j + 8, 128 * cc:128 * cc + 128]
                return v.tensor_scalar(dst, src.rearrange("p (k n) -> p k n", k=8),
                                       -OFF, None, ALU.add)
            deps = [t5[(c, j)]]
            if c == 0:
                deps += [womm[(7, 3)], ln2a[NT - 1], ln2s[NT - 1], res2[NT - 1]]
            if cc == 0 and g >= 1:
                deps.append(f2[(g - 1, KH - 1)])
            t5e[(c, j)] = P.step("dv", deps, f_t5e, f"t5e{c}_{j}")

    def emit_f2(g):
        for k in range(KH):
            deps = list(wqs[32 + k])
            if g == 0:
                deps.append(t2[NT - 1])
            if g == 0 and k < 4:
                deps += sum(wqs[NWQ - 4:], [])
            if k >= 4:
                deps.append(f2[(g, k - 4)])
            elif g >= 1:
                deps.append(f2[(g - 1, KH - 4 + k)])
            parts = []
            for qt in range(4):
                def f_wq2ld(gp, sem, count_only=False, g=g, k=k, qt=qt):
                    if count_only:
                        return 1
                    gp.dma_start(wq2r[32 * qt:32 * qt + 32, k % 4, :],
                                 wq2_scr.ap()[128 * k + 32 * qt:128 * k + 32 * qt + 32,
                                              :]).then_inc(sem, 16)
                parts.append(P.dma("gp" if qt % 2 == 0 else "sy", deps, f_wq2ld,
                                   f"wq2l{g}_{k}_{qt}"))
            wq2_ld[(g, k)] = parts

            def f_f2(te, g=g, k=k):
                r = None
                for m in range(8):
                    bank = (pA[:, 512 * m:512 * m + 512] if m < 4
                            else pB[:, 512 * (m - 4):512 * (m - 4) + 512])
                    r = te.matmul(bank, wq2r[:, k % 4, 128 * m:128 * m + 128],
                                  Ahfm[:, k, :], start=(k == 0), stop=(k == KH - 1))
                return r
            deps = list(wq2_ld[(g, k)])
            if k % 8 == 0:
                deps += [t5e[(4 * g + cc2, k // 8)] for cc2 in range(GC)]
            if k == 0:
                deps += [f1g[(4 * g + 3, 3)], t5e[(4 * g + 3, 2)],
                         t5e[(4 * g + 3, 3)]]
                if g >= 1:
                    deps.append(t6e[(g - 1, 3)])
            f2[(g, k)] = P.step("pe", deps, f_f2, f"f2_{g}_{k}")

        def f_f2ev(v, g=g):
            r = None
            for m in range(4, 8):
                r = v.tensor_copy(o2fm[:, m, :],
                                  pB[:, 512 * (m - 4):512 * (m - 4) + 512])
            return r
        fdeps = [f2[(g, KH - 1)]]
        if g == 0:
            fdeps.append(wqsub[NWQ - 1])
        f2ev[g] = P.step("dv", fdeps, f_f2ev, f"f2ev{g}")

        def f_f2evA(s, g=g):
            r = None
            for m in range(4):
                r = s.activation(o2fm[:, m, :], pA[:, 512 * m:512 * m + 512],
                                 AF.Copy, scale=1.0)
            return r
        f2evA[g] = P.step("se", list(fdeps), f_f2evA, f"f2evA{g}")

    def emit_tail(g, cc):
        c = 4 * g + cc

        def f_t6(te, g=g, cc=cc):
            bank = pB[:, 1536:2048].bitcast(BF16)
            r = None
            for m in range(8):
                r = te.transpose(bank[:, 128 * m:128 * m + 128],
                                 o2fm[:, m, 128 * cc:128 * cc + 128], idb[:])
            return r
        deps = [f2ev[g], f2evA[g]]
        if cc >= 1:
            deps.append(t6e[(g, cc - 1)])
        t6[(g, cc)] = P.step("pe", deps, f_t6, f"t6_{g}_{cc}")

        def f_t6e(v, g=g, cc=cc, c=c):
            bank = pB[:, 1536:2048].bitcast(BF16)
            return v.tensor_scalar(y2_sb[:, :], bank,
                                   fac2buf[:, c % 8:c % 8 + 1], None, ALU.mult)
        deps = [t6[(g, cc)], qh_sc[c]]
        if c == 0:
            deps.append(wqge[NWQ - 1])
        if c >= 1:
            deps += [ln4a[c - 1], ln4s[c - 1]]
        t6e[(g, cc)] = P.step("dv", deps, f_t6e, f"t6e{g}_{cc}")

        L = LNS["l4"][c % 2]

        def f_ln4s(v, c=c, L=L):
            v.bn_stats(L["bn"][:, 0:6], y2_sb[:, 0:512])
            v.bn_stats(L["bn"][:, 6:12], y2_sb[:, 512:1024])
            v.drain()
            v.bn_aggr(L["ag"][:], L["bn"][:])
            v.drain()
            return v.tensor_scalar(L["rs"][:], L["ag"][:, 1:2], EPS_LN, None,
                                   ALU.add)
        ln4s[c] = P.step("dv", [t6e[(g, cc)]], f_ln4s, f"ln4s{c}")

        def f_ln4q(s, L=L):
            return s.activation(L["rs"][:], L["rs"][:], AF.Sqrt, scale=1.0)
        ln4q[c] = P.step("se", [ln4s[c]], f_ln4q, f"ln4q{c}")

        def f_ln4f(v, L=L):
            v.reciprocal(L["rs"][:], L["rs"][:])
            v.drain()
            return v.scalar_tensor_tensor(L["nm"][:], L["ag"][:, 0:1], -1.0,
                                          L["rs"][:], ALU.mult, ALU.mult)
        ln4f[c] = P.step("dv", [ln4q[c]], f_ln4f, f"ln4f{c}")

        def f_ln4a(s, c=c, L=L):
            return s.activation(yn4_sb[:, :], y2_sb[:, :], AF.Identity,
                                bias=L["nm"][:], scale=L["rs"][:])
        adeps = [ln4f[c]]
        if c == 0:
            adeps.append(wqge[NWQ - 1])
        if c >= 1:
            adeps.append(outadd[c - 1])
        ln4a[c] = P.step("se", adeps, f_ln4a, f"ln4a{c}")

        def f_out(gp, c=c):
            return gp.tensor_tensor(yout_sb[:, c % 2, :], x_tm[:, c, :],
                                    yn4_sb[:, :], ALU.add)
        odeps = [ln4a[c], ln3s[c]]
        if c == 0:
            odeps += [wqge[NWQ - 1]] + sum(wqs[NWQ - 4:], [])
        if c >= 2:
            odeps += list(yst[c - 2])
        outadd[c] = P.step("gp", odeps, f_out, f"out{c}")

        parts = []
        for sh in range(2):
            def f_yst(sy, sem, count_only=False, c=c, sh=sh):
                if count_only:
                    return 1
                r0, r1 = 64 * sh, 64 * sh + 64
                sy.dma_start(y_out.ap()[128 * c + r0:128 * c + r1, :],
                             yout_sb[r0:r1, c % 2, :]).then_inc(sem, 16)
            parts.append(P.dma("sy" if sh == 0 else "gp", [outadd[c]], f_yst,
                               f"yst{c}_{sh}"))
        yst[c] = parts

    for g in range(NG):
        if g >= 1:
            emit_strips(4 * g - 1)
            emit_tfb(4 * g - 1)
            emit_f2(g - 1)
        emit_lnqx(4 * g)
        emit_lnqx(4 * g + 1)
        emit_t4(4 * g)
        for cc in range(GC):
            c = 4 * g + cc
            emit_f1(c)
            if cc < GC - 1:
                emit_t4(c + 1)
            if cc >= 1:
                emit_strips(c - 1)
                emit_tfb(c - 1)
            emit_qhx(c)
            if cc + 2 < GC:
                emit_lnqx(c + 2)
            if g >= 1:
                emit_tail(g - 1, cc)
    emit_strips(NT - 1)
    emit_tfb(NT - 1)
    emit_f2(NG - 1)
    for cc in range(GC):
        emit_tail(NG - 1, cc)

    def f_fin(sy):
        return sy.nop()
    P.step("sy", sum([yst[c] for c in range(NT)], []), f_fin, "fin")

    with nc.Block("mk") as blk:
        P.emit(blk, ctr_h, qctr_h)

    st.close()
    return nc



def kernel(L0_attn_x, L0_attn_w_qkv, L0_attn_proj, L0_attn_w_out, L0_attn_b_out,
           L0_ffn_w1, L0_ffn_w2):
    x = np.asarray(L0_attn_x, dtype=np.float32)
    w_qkv = np.asarray(L0_attn_w_qkv, dtype=np.float32)
    proj = np.ascontiguousarray(np.asarray(L0_attn_proj, dtype=np.float32))
    w_out = np.ascontiguousarray(np.asarray(L0_attn_w_out, dtype=np.float32))
    b_out = np.asarray(L0_attn_b_out, dtype=np.float32)
    w1 = np.asarray(L0_ffn_w1, dtype=np.float32)
    w2 = np.asarray(L0_ffn_w2, dtype=np.float32)

    wqkT = np.ascontiguousarray(w_qkv[:, :2048].T)
    w_v = np.ascontiguousarray(w_qkv[:, 2048:])
    b_out_r = np.ascontiguousarray(b_out.reshape(8, 128).T)
    w1T = np.ascontiguousarray(w1.T)
    w2T = np.ascontiguousarray(w2.T)

    jj, rr = np.meshgrid(np.arange(128), np.arange(128), indexing="ij")
    mask1 = ((jj - rr >= 0) & (jj - rr <= 6)).astype(np.float32)
    j2, r2 = np.meshgrid(np.arange(6), np.arange(128), indexing="ij")
    mask2 = (r2 - j2 >= 122).astype(np.float32)
    identf = np.eye(128, dtype=np.float32)
    identb = np.eye(128).astype(ml_dtypes.bfloat16)
    identh = np.eye(128, dtype=np.float16)

    in_maps = []
    for core in range(8):
        b = core // 4
        s0 = (core % 4) * TOK
        xh = np.zeros((HALO, D), dtype=np.float32)
        lo, hi = s0 - 3, s0 + TOK + 3
        src_lo, src_hi = max(lo, 0), min(hi, 8192)
        xh[src_lo - lo:src_hi - lo, :] = x[b, src_lo:src_hi, :]
        edge = np.ones((128, 6), dtype=np.float32)
        if s0 == 0:
            edge[:, 0:3] = 0.0
        if s0 + TOK == 8192:
            edge[:, 3:6] = 0.0
        in_maps.append({
            "x_halo": xh, "wqkT": wqkT, "w_v": w_v, "proj": proj,
            "w_out": w_out, "b_out_r": b_out_r, "w1T": w1T, "w2T": w2T,
            "mask1": mask1, "mask2": mask2, "identf": identf, "identb": identb,
            "identh": identh, "edgem": edge,
        })

    nc = build_nc()
    res = run_bass_kernel_spmd(nc, in_maps, core_ids=list(range(8)))
    out = np.concatenate([np.asarray(res.results[i]["y"]) for i in range(8)], axis=0)
    return out.reshape(2, 8192, D)


# version 28 (final)
# speedup vs baseline: 1.0227x; 1.0003x over previous
import numpy as np
import ml_dtypes

import concourse.bass as bass
import concourse.mybir as mybir
from concourse.bass_utils import run_bass_kernel_spmd

try:
    from concourse.compiler_utils import get_compiler_flags, set_compiler_flags
    set_compiler_flags([f.replace("--enable-ldw-opt=false", "--enable-ldw-opt=true")
                        for f in get_compiler_flags()])
except Exception:
    pass

F32 = mybir.dt.float32
BF16 = mybir.dt.bfloat16
FP16 = mybir.dt.float16
I16 = mybir.dt.int16
I32 = mybir.dt.int32
AF = mybir.ActivationFunctionType
ALU = mybir.AluOpType
AX = mybir.AxisListType

D = 1024
KD = 8
HEADS = 8
PAIRS = 4
TOK = 2048
NT = 16
HALO = 2054
NVT = 17
HID = 4096
KH = 32
NG = 4
GC = 4
EPS_LN = 1e-5
EPS_Z = 1e-8
EPS_Q = 1e-5
OFF = 1536.0
PHI_NT = [512, 512, 512, 512, 6]

ENGS = ("pe", "dv", "se", "gp", "sy")


NQS = 4


class Step:
    __slots__ = ("eng", "idx", "cord", "deps", "fn", "dma", "qord", "name")

    def __init__(self, eng, idx, deps, fn, dma, name):
        self.eng = eng
        self.idx = idx
        self.cord = None
        self.deps = deps
        self.fn = fn
        self.dma = dma
        self.qord = None
        self.name = name


class Prog:
    def __init__(self):
        self.steps = {e: [] for e in ENGS}

    def step(self, eng, deps, fn, name=""):
        s = Step(eng, len(self.steps[eng]), [d for d in deps if d is not None],
                 fn, False, name)
        s.cord = sum(1 for t in self.steps[eng] if not t.dma)
        self.steps[eng].append(s)
        return s

    def dma(self, eng, deps, fn, name=""):
        s = Step(eng, len(self.steps[eng]), [d for d in deps if d is not None],
                 fn, True, name)
        s.qord = sum(1 for t in self.steps[eng] if t.dma)
        self.steps[eng].append(s)
        return s

    def emit(self, blk, ctrs, qctrs):
        engobjs = {"pe": blk.tensor, "dv": blk.vector, "se": blk.scalar,
                   "gp": blk.gpsimd, "sy": blk.sync}
        dmalist = {e: [s for s in self.steps[e] if s.dma] for e in ENGS}

        def make_body(eng_name):
            def body(eng):
                hw = {}

                def wait_for(d):
                    if d.dma:
                        key = ("q", d.eng, d.qord % NQS)
                        val = 16 * (d.qord // NQS + 1)
                        sem = qctrs[d.eng][d.qord % NQS]
                    else:
                        key, val, sem = ("c", d.eng), d.cord + 1, ctrs[d.eng]
                    if hw.get(key, 0) < val:
                        hw[key] = val
                        eng.wait_ge(sem, val)

                for s in self.steps[eng_name]:
                    for d in s.deps:
                        if d.eng == eng_name and not d.dma and not s.dma:
                            continue
                        wait_for(d)
                    if s.dma:
                        if s.qord >= NQS:
                            wait_for(dmalist[eng_name][s.qord - NQS])
                        sem = qctrs[eng_name][s.qord % NQS]
                        s.fn(eng, sem, count_only=False)
                    else:
                        if eng_name in ("dv", "se", "gp"):
                            eng.drain()
                        last = s.fn(eng)
                        assert last is not None, f"step {s.name} returned None"
                        last.then_inc(ctrs[eng_name], 1)
            return body

        for e in ENGS:
            engobjs[e](make_body(e))


def build_nc():
    nc = bass.Bass("TRN2", target_bir_lowering=False)

    x_in = nc.dram_tensor("x_halo", [HALO, D], F32, kind="ExternalInput")
    wqkT_in = nc.dram_tensor("wqkT", [2 * D, D], F32, kind="ExternalInput")
    w_v_in = nc.dram_tensor("w_v", [D, D], F32, kind="ExternalInput")
    proj_in = nc.dram_tensor("proj", [128, 64], F32, kind="ExternalInput")
    w_out_in = nc.dram_tensor("w_out", [D, D], F32, kind="ExternalInput")
    b_out_in = nc.dram_tensor("b_out_r", [128, 8], F32, kind="ExternalInput")
    w1T_in = nc.dram_tensor("w1T", [D, HID], F32, kind="ExternalInput")
    w2T_in = nc.dram_tensor("w2T", [HID, D], F32, kind="ExternalInput")
    mask1_in = nc.dram_tensor("mask1", [128, 128], F32, kind="ExternalInput")
    mask2_in = nc.dram_tensor("mask2", [6, 128], F32, kind="ExternalInput")
    idf_in = nc.dram_tensor("identf", [128, 128], F32, kind="ExternalInput")
    idb_in = nc.dram_tensor("identb", [128, 128], BF16, kind="ExternalInput")
    idh_in = nc.dram_tensor("identh", [128, 128], FP16, kind="ExternalInput")
    edge_in = nc.dram_tensor("edgem", [128, 6], F32, kind="ExternalInput")
    y_out = nc.dram_tensor("y", [TOK, D], F32, kind="ExternalOutput")
    wq1_scr = nc.dram_tensor("wq1_scr", [D, HID], BF16, kind="Internal")
    wq2_scr = nc.dram_tensor("wq2_scr", [HID, D], BF16, kind="Internal")

    from contextlib import ExitStack
    st = ExitStack()
    SB = lambda n, s, d: st.enter_context(nc.sbuf_tensor(n, s, d))

    x_tm = SB("x_tm", [128, NT, D], BF16)
    mask1 = SB("m1", [128, 128], F32)
    mask2 = SB("m2", [6, 128], F32)
    idf = SB("idf", [128, 128], F32)
    idb = SB("idb", [128, 128], BF16)
    idh = SB("idh", [128, 128], FP16)
    edge = SB("edge", [128, 6], F32)
    b_out = SB("b_out", [128, 8], F32)
    proj_f = SB("proj_f", [128, 64], F32)
    proj_b = SB("proj_b", [128, 64], BF16)
    ones_col = SB("ones_col", [128, 1], F32)
    ones_row = SB("ones_row", [1, 128], F32)
    LNS = {}
    for lname in ("l1", "l2", "l3", "l4"):
        LNS[lname] = [dict(bn=SB(f"{lname}bn{i}", [128, 12], F32),
                           ag=SB(f"{lname}ag{i}", [128, 2], F32),
                           rs=SB(f"{lname}rs{i}", [128, 1], F32),
                           nm=SB(f"{lname}nm{i}", [128, 1], F32)) for i in range(2)]
    QXs = [dict(mx=SB(f"qxm{i}", [128, 1], F32), isc=SB(f"qxi{i}", [128, 1], F32),
                sc=SB(f"qxs{i}", [128, 1], F32), fac=SB(f"qxf{i}", [128, 1], F32))
           for i in range(2)]
    QHs = [dict(mx=SB(f"qhm{i}", [128, 2], F32), isc=SB(f"qhi{i}", [128, 1], F32),
                sc=SB(f"qhs{i}", [128, 1], F32)) for i in range(2)]
    fac2buf = SB("fac2buf", [128, 8], F32)
    za = SB("za", [128, 1], F32)
    zb = SB("zb", [128, 1], F32)
    w1acc = SB("w1acc", [128, 32], F32)
    w2acc = SB("w2acc", [128, 32], F32)
    wtot2 = SB("wtot2", [128, 1], F32)
    wtot1 = SB("wtot1", [128, 1], F32)
    sv1 = SB("sv1", [1, 1], F32)
    sv2 = SB("sv2", [1, 1], F32)
    s1_t = SB("s1_t", [128, 1], F32)
    s2_t = SB("s2_t", [128, 1], F32)
    rh1_t = SB("rh1_t", [128, 1], F32)
    rh2_t = SB("rh2_t", [128, 1], F32)

    pA = st.enter_context(nc.psum_tensor("pA", [128, 2048], F32))
    pB = st.enter_context(nc.psum_tensor("pB", [128, 2048], F32))

    class Ovl:
        def __init__(self, base_off):
            self.off = base_off

        def sbuf_tensor(self, name, shape, dtype):
            import functools, operator
            per = functools.reduce(operator.mul, shape[1:], 1) * mybir.dt.size(dtype)
            off = self.off
            self.off = (self.off + per + 31) & ~31
            hnd = nc.virtual_sbuf_tensor(name, shape, dtype)
            nc._tensor(hnd.name, hnd.shape, hnd.dtype, type="SB")
            mloc = nc.lookup_mloc(hnd)
            mloc.allocated = True
            pend.append((mloc, off))
            return hnd

    pend = []
    arAV = Ovl(0)
    A_tm = arAV.sbuf_tensor("A_tm", [128, NT, D], BF16)
    Vt = arAV.sbuf_tensor("Vt", [128, 3, HEADS, 129], BF16)
    S1m = arAV.sbuf_tensor("S1m", [128, 2, 256], BF16)
    S2m = arAV.sbuf_tensor("S2m", [6, 2, 256], BF16)
    AVSZ = (arAV.off + 31) & ~31

    arAT = Ovl(AVSZ)
    x_ln = arAT.sbuf_tensor("x_ln", [128, KD, HALO], BF16)
    qphi = arAT.sbuf_tensor("qphi", [128, PAIRS, HALO], BF16)
    kphi = arAT.sbuf_tensor("kphi", [128, PAIRS, HALO], BF16)
    wqp = arAT.sbuf_tensor("wqp", [128, KD, 512], BF16)
    wkp = arAT.sbuf_tensor("wkp", [128, KD, 512], BF16)
    wv_sb = arAT.sbuf_tensor("wv_sb", [128, KD, D], BF16)
    xin = arAT.sbuf_tensor("xin", [128, 2, D], F32)
    xhalo6 = arAT.sbuf_tensor("xh6", [6, D], F32)
    XLNTM_OFF = arAT.off
    xlntm = arAT.sbuf_tensor("xlntm", [128, D], F32)
    stage32 = arAT.sbuf_tensor("stage32", [128, 2, D], F32)
    wqk_b = arAT.sbuf_tensor("wqk_b", [128, 2, D], BF16)
    elu_t = arAT.sbuf_tensor("elu_t", [128, 2, 512], F32)

    arP23 = Ovl(AVSZ)
    A_fm = arP23.sbuf_tensor("A_fm", [128, KD, TOK], BF16)
    opfm = arP23.sbuf_tensor("opfm", [128, 8, TOK], BF16)
    wo_sb = arP23.sbuf_tensor("wo_sb", [128, KD, D], BF16)
    y_sb = arP23.sbuf_tensor("y_sb", [128, 2, D], F32)
    yn_sb = arP23.sbuf_tensor("yn_sb", [128, 2, D], F32)

    arP4 = Ovl(0)
    x3_sb = arP4.sbuf_tensor("x3_sb", [128, D], F32)
    Ax_tm = arP4.sbuf_tensor("Ax_tm", [128, 2, D], FP16)
    Axf = arP4.sbuf_tensor("Axf", [128, 2, KD, 128], BF16)
    u05h = arP4.sbuf_tensor("u05h", [128, 2, D], F32)
    wq2r = arP4.sbuf_tensor("wq2r", [128, 4, D], BF16)
    arP4.sbuf_tensor("pad4", [128, 2048], mybir.dt.uint8)
    h_sb = arP4.sbuf_tensor("h_sb", [128, 2, HID], BF16)
    Ah_tm = arP4.sbuf_tensor("Ah_tm", [128, HID], FP16)
    wq1g = arP4.sbuf_tensor("wq1g", [128, KD, HID], BF16)
    Ahfm = arP4.sbuf_tensor("Ahfm", [128, KH, 512], BF16)
    o2fm = arP4.sbuf_tensor("o2fm", [128, 8, 512], BF16)
    y2_sb = arP4.sbuf_tensor("y2_sb", [128, D], BF16)
    yn4_sb = arP4.sbuf_tensor("yn4_sb", [128, D], BF16)
    yout_sb = arP4.sbuf_tensor("yout", [128, D], F32)

    arWQ = Ovl(XLNTM_OFF)
    w23s = arWQ.sbuf_tensor("w23s", [128, 3, D], F32)
    wqt_o = arWQ.sbuf_tensor("wqt_o", [128, 3, D], BF16)
    wqtmp = arWQ.sbuf_tensor("wqtmp", [128, 2, D], BF16)

    OVLSZ = max(arAT.off, arP23.off, arP4.off, arWQ.off)
    ovl = st.enter_context(nc.sbuf_tensor("ovl", [128, OVLSZ], mybir.dt.uint8))
    ovl_base = nc.lookup_mloc(ovl).addr
    for mloc, off in pend:
        mloc.addr = ovl_base + off

    cbias = SB("cbias", [128, 4], F32)
    with nc.Block("initc") as iblk:
        @iblk.gpsimd
        def _(g):
            g.memset(cbias[:, 0:1], -OFF)
            g.memset(cbias[:, 1:2], OFF - 0.5)
            g.memset(cbias[:, 2:3], EPS_LN)
            g.memset(cbias[:, 3:4], OFF + 0.5)
    nc.const_aps.register(F32, -OFF, cbias[:, 0:1])
    nc.const_aps.register(F32, OFF - 0.5, cbias[:, 1:2])
    nc.const_aps.register(F32, EPS_LN, cbias[:, 2:3])
    nc.const_aps.register(F32, OFF + 0.5, cbias[:, 3:4])

    ctr_h, qctr_h = {}, {}
    for e in ENGS:
        ctr_h[e] = st.enter_context(nc.semaphore(f"c_{e}"))
        qctr_h[e] = [st.enter_context(nc.semaphore(f"q_{e}{i}")) for i in range(4)]

    P = Prog()

    _dc = {}
    for _nm, _dst, _src in (("m1", mask1, mask1_in), ("m2", mask2, mask2_in),
                            ("idf", idf, idf_in), ("idb", idb, idb_in),
                            ("idh", idh, idh_in), ("edge", edge, edge_in),
                            ("bout", b_out, b_out_in), ("proj", proj_f, proj_in)):
        def f_dc(sy, sem, count_only=False, _dst=_dst, _src=_src):
            if count_only:
                return 1
            sy.dma_start(_dst[:], _src.ap()).then_inc(sem, 16)
        _dc[_nm] = P.dma("sy", [], f_dc, f"dc_{_nm}")
    def f_dcx(sy, sem, count_only=False):
        if count_only:
            return 1
        sy.dma_start(xhalo6[0:3, :], x_in.ap()[0:3, :]).then_inc(sem, 16)
    _dc["xh6a"] = P.dma("sy", [], f_dcx, "dc_xh6a")
    dc_m1, dc_m2, dc_idf, dc_idb = _dc["m1"], _dc["m2"], _dc["idf"], _dc["idb"]
    dc_idh, dc_edge, dc_bout, dc_proj = _dc["idh"], _dc["edge"], _dc["bout"], _dc["proj"]
    dc_xh6a = _dc["xh6a"]

    def f_init0(v):
        v.memset(ones_col[:], 1.0)
        v.drain()
        v.memset(ones_row[:], 1.0)
        v.drain()
        v.memset(Vt[:, :, :, 128:129], 1.0)
        v.drain()
        return v.memset(S2m[:, :, :], 0.0)
    init0 = P.step("dv", [], f_init0, "init0")

    xinld = [None] * NT
    xst = [None] * NT; xsq = [None] * NT; xfin = [None] * NT
    xapply = [None] * NT; xcast = [None] * NT
    t1 = [None] * NT; t1e = [None] * NT

    for t in range(NT):
        def f_xin(sy, sem, count_only=False, t=t):
            if count_only:
                return 1
            sy.dma_start(xin[:, t % 2, :],
                         x_in.ap()[3 + 128 * t:3 + 128 * (t + 1), :]).then_inc(sem, 16)
        deps = [xst[t - 2], xapply[t - 2], xcast[t - 2]] if t >= 2 else []
        xinld[t] = P.dma("sy" if t % 2 == 0 else "se", deps, f_xin, f"xin{t}")

        L = LNS["l1"][t % 2]

        def f_xst(v, t=t, L=L):
            v.bn_stats(L["bn"][:, 0:6], xin[:, t % 2, 0:512])
            v.bn_stats(L["bn"][:, 6:12], xin[:, t % 2, 512:1024])
            v.drain()
            v.bn_aggr(L["ag"][:], L["bn"][:])
            v.drain()
            return v.tensor_scalar(L["rs"][:], L["ag"][:, 1:2], EPS_LN, None, ALU.add)
        xst[t] = P.step("dv", [xinld[t], xapply[t - 2] if t >= 2 else None],
                        f_xst, f"xst{t}")

        def f_xsq(s, L=L):
            return s.activation(L["rs"][:], L["rs"][:], AF.Sqrt, scale=1.0)
        xsq[t] = P.step("se", [xst[t]], f_xsq, f"xsq{t}")

        def f_xfin(v, L=L):
            v.reciprocal(L["rs"][:], L["rs"][:])
            v.drain()
            return v.scalar_tensor_tensor(L["nm"][:], L["ag"][:, 0:1], -1.0, L["rs"][:],
                                          ALU.mult, ALU.mult)
        xfin[t] = P.step("dv", [xsq[t]], f_xfin, f"xfin{t}")

        def f_xapply(s, t=t, L=L):
            return s.activation(xlntm[:, :], xin[:, t % 2, :], AF.Identity,
                                bias=L["nm"][:], scale=L["rs"][:])
        xapply[t] = P.step("se", [xfin[t], t1[t - 1] if t >= 1 else None],
                           f_xapply, f"xap{t}")

        def f_xcast(v, t=t):
            return v.tensor_copy(x_tm[:, t, :], xin[:, t % 2, :])
        xcast[t] = P.step("dv", [xinld[t]], f_xcast, f"xcast{t}")

        def f_t1(te, t=t):
            r = None
            for k in range(KD):
                half, q = k // 4, k % 4
                r = te.transpose(pB[:, 1024 * half + 128 * q:1024 * half + 128 * q + 128],
                                 xlntm[:, 128 * k:128 * k + 128], idf[:])
            return r
        t1[t] = P.step("pe", [xapply[t], dc_idf, t1e[t - 1] if t >= 1 else None],
                       f_t1, f"t1_{t}")

        def f_t1e(s, t=t):
            r = None
            for half in range(2):
                src = pB[:, 1024 * half:1024 * half + 512]
                dst = x_ln[:, 4 * half:4 * half + 4, 3 + 128 * t:3 + 128 * t + 128]
                r = s.activation(dst, src.rearrange("p (k n) -> p k n", k=4),
                                 AF.Copy, scale=1.0)
                if half == 0:
                    s.drain()
            return r
        t1e[t] = P.step("se", [t1[t]], f_t1e, f"t1e{t}")

    def f_xin6(sy, sem, count_only=False):
        if count_only:
            return 1
        sy.dma_start(xhalo6[3:6, :], x_in.ap()[2051:2054, :]).then_inc(sem, 16)
    xin6 = P.dma("sy", [], f_xin6, "xin6")

    L6 = dict(bn=SB("l6bn", [6, 12], F32), ag=SB("l6ag", [6, 2], F32),
              rs=SB("l6rs", [6, 1], F32), nm=SB("l6nm", [6, 1], F32))

    def f_xst6(v):
        v.bn_stats(L6["bn"][0:6, 0:6], xhalo6[:, 0:512])
        v.bn_stats(L6["bn"][0:6, 6:12], xhalo6[:, 512:1024])
        v.drain()
        v.bn_aggr(L6["ag"][0:6, :], L6["bn"][0:6, :])
        v.drain()
        return v.tensor_scalar(L6["rs"][0:6, :], L6["ag"][0:6, 1:2], EPS_LN, None,
                               ALU.add)
    xst6 = P.step("dv", [dc_xh6a, xin6], f_xst6, "xst6")

    def f_xsq6(s):
        return s.activation(L6["rs"][0:6, :], L6["rs"][0:6, :], AF.Sqrt, scale=1.0)
    xsq6 = P.step("se", [xst6], f_xsq6, "xsq6")

    def f_xfin6(v):
        v.reciprocal(L6["rs"][0:6, :], L6["rs"][0:6, :])
        v.drain()
        return v.scalar_tensor_tensor(L6["nm"][0:6, :], L6["ag"][0:6, 0:1], -1.0,
                                      L6["rs"][0:6, :], ALU.mult, ALU.mult)
    xfin6 = P.step("dv", [xsq6], f_xfin6, "xfin6")

    def f_xap6(s):
        return s.activation(xhalo6[:, :], xhalo6[:, :], AF.Identity,
                            bias=L6["nm"][0:6, :], scale=L6["rs"][0:6, :])
    xap6 = P.step("se", [xfin6], f_xap6, "xap6")

    def f_t16(te):
        r = None
        for k in range(KD):
            r = te.transpose(pA[:, 1024 + 12 * k:1024 + 12 * k + 6],
                             xhalo6[:, 128 * k:128 * k + 128], idf[0:6, 0:6])
        return r
    t16 = P.step("pe", [xap6, dc_idf], f_t16, "t16")

    def f_t16e(s):
        r = None
        for k in range(KD):
            s.activation(x_ln[:, k, 0:3], pA[:, 1024 + 12 * k:1024 + 12 * k + 3],
                         AF.Copy, scale=1.0)
            s.drain()
            r = s.activation(x_ln[:, k, 2051:2054], pA[:, 1024 + 12 * k + 3:1024 + 12 * k + 6],
                             AF.Copy, scale=1.0)
            if k < KD - 1:
                s.drain()
        return r
    t16e = P.step("se", [t16], f_t16e, "t16e")

    def f_pjc(v):
        return v.tensor_copy(proj_b[:], proj_f[:])
    pjc = P.step("dv", [dc_proj], f_pjc, "pjc")

    wq_ld = [None] * 16; wq_cast = [None] * 16
    wqmm = [None] * 16; wqev = [None] * 16
    for i in range(16):
        def f_wqld(gp, sem, count_only=False, i=i):
            if count_only:
                return 1
            gp.dma_start(stage32[:, i % 2, :],
                         wqkT_in.ap()[128 * i:128 * (i + 1), :]).then_inc(sem, 16)
        wq_ld[i] = P.dma("gp", [wq_cast[i - 2]] if i >= 2 else [], f_wqld, f"wqld{i}")

        def f_wqcast(v, i=i):
            return v.tensor_copy(wqk_b[:, i % 2, :], stage32[:, i % 2, :])
        wq_cast[i] = P.step("dv", [wq_ld[i], wqmm[i - 2] if i >= 2 else None],
                            f_wqcast, f"wqc{i}")

        def f_wqmm(te, i=i):
            bank = pA[:, 512 * (i % 2):512 * (i % 2) + 512]
            r = None
            for kd in range(KD):
                r = te.matmul(bank[:, 64 * kd:64 * kd + 64],
                              wqk_b[:, i % 2, 128 * kd:128 * kd + 128],
                              proj_b[:], start=True, stop=True)
            return r
        wqmm[i] = P.step("pe", [wq_cast[i], pjc, wqev[i - 2] if i >= 2 else None],
                         f_wqmm, f"wqmm{i}")

        def f_wqev(s, i=i):
            qk, h = divmod(i, 8)
            wt = wqp if qk == 0 else wkp
            bank = pA[:, 512 * (i % 2):512 * (i % 2) + 512]
            p, hh = divmod(h, 2)
            dst = wt[:, 0:KD, 128 * p + 64 * hh:128 * p + 64 * hh + 64]
            return s.activation(dst, bank.rearrange("p (k f) -> p k f", k=KD),
                                AF.Copy, scale=1.0)
        wqev[i] = P.step("se", [wqmm[i]], f_wqev, f"wqev{i}")

    wv_ld = [None] * 8; wv_cast = [None] * 8
    for j in range(8):
        def f_wvld(gp, sem, count_only=False, j=j):
            if count_only:
                return 1
            gp.dma_start(stage32[:, j % 2, :],
                         w_v_in.ap()[128 * j:128 * (j + 1), :]).then_inc(sem, 16)
        prev = wq_cast[14 + j] if 14 + j < 16 else wv_cast[j - 2]
        wv_ld[j] = P.dma("gp", [prev], f_wvld, f"wvld{j}")

        def f_wvcast(v, j=j):
            return v.tensor_copy(wv_sb[:, j, :], stage32[:, j % 2, :])
        wv_cast[j] = P.step("dv", [wv_ld[j]], f_wvcast, f"wvc{j}")

    wab_ld = [None] * 64
    wab_abs = [None] * 64

    def make_wab(k):
        which = "w2" if k < 32 else "w1"
        j = k if k < 32 else k - 32

        def f_ld(sy, sem, count_only=False, which=which, j=j, k=k):
            if count_only:
                return 1
            if which == "w2":
                s_ = w2T_in.ap()[128 * j:128 * (j + 1), :]
            else:
                dk, q = divmod(j, 4)
                s_ = w1T_in.ap()[128 * dk:128 * (dk + 1), 1024 * q:1024 * (q + 1)]
            sy.dma_start(stage32[:, k % 2, :], s_).then_inc(sem, 16)
        if k == 0:
            pdep = [wv_cast[6]]
        elif k == 1:
            pdep = [wv_cast[7]]
        else:
            pdep = [wab_abs[k - 2]]
        wab_ld[k] = P.dma("sy" if k % 2 == 0 else "se", pdep, f_ld, f"wabl{k}")

        def f_abs(s, which=which, j=j, k=k):
            acc = w2acc if which == "w2" else w1acc
            return s.activation(stage32[:, k % 2, :], stage32[:, k % 2, :], AF.Abs,
                                scale=1.0, accum_out=acc[:, j:j + 1])
        wab_abs[k] = P.step("se", [wab_ld[k]], f_abs, f"waba{k}")


    phimm = [None] * 40; phimin = [None] * 40
    phiexp = [None] * 40; phic = [None] * 40
    idx = 0
    for qk in range(2):
        for p in range(PAIRS):
            for ntile in range(5):
                n0 = 512 * ntile
                w = PHI_NT[ntile]

                def f_phimm(te, qk=qk, p=p, n0=n0, w=w, idx=idx):
                    wt = wqp if qk == 0 else wkp
                    bank = pA[:, 512 * (idx % 4):512 * (idx % 4) + w]
                    r = None
                    for kd in range(KD):
                        r = te.matmul(bank, wt[:, kd, 128 * p:128 * p + 128],
                                      x_ln[:, kd, n0:n0 + w],
                                      start=(kd == 0), stop=(kd == KD - 1))
                    return r
                deps = [t1e[NT - 1], t16e, wqev[7 if qk == 0 else 15]]
                if idx >= 4:
                    deps.append(phic[idx - 4])
                elif idx < 2:
                    deps.append(wqev[14 + idx])
                phimm[idx] = P.step("pe", deps, f_phimm, f"phimm{idx}")

                def f_phimin(v, w=w, idx=idx):
                    bank = pA[:, 512 * (idx % 4):512 * (idx % 4) + w]
                    return v.tensor_scalar(elu_t[:, 0, 0:w], bank, 0.0, None, ALU.min)
                phimin[idx] = P.step("dv", [phimm[idx],
                                            phiexp[idx - 1] if idx >= 1 else None],
                                     f_phimin, f"phimin{idx}")

                def f_phiexp(s, w=w):
                    return s.activation(elu_t[:, 1, 0:w], elu_t[:, 0, 0:w], AF.Exp,
                                       scale=1.0)
                phiexp[idx] = P.step("se", [phimin[idx],
                                            phic[idx - 1] if idx >= 1 else None],
                                     f_phiexp, f"phiexp{idx}")

                def f_phic(v, qk=qk, p=p, n0=n0, w=w, idx=idx):
                    dstT = qphi if qk == 0 else kphi
                    bank = pA[:, 512 * (idx % 4):512 * (idx % 4) + w]
                    return v.scalar_tensor_tensor(dstT[:, p, n0:n0 + w], bank, 1.0,
                                                  elu_t[:, 1, 0:w], ALU.add, ALU.max)
                phic[idx] = P.step("dv", [phiexp[idx]], f_phic, f"phic{idx}")
                for _k in range((idx * 64) // 40, ((idx + 1) * 64) // 40):
                    make_wab(_k)
                idx += 1

    def f_kedge(v):
        r = None
        for p in range(PAIRS):
            v.tensor_tensor(kphi[:, p, 0:3], kphi[:, p, 0:3], edge[:, 0:3], ALU.mult)
            v.drain()
            r = v.tensor_tensor(kphi[:, p, 2051:2054], kphi[:, p, 2051:2054],
                                edge[:, 3:6], ALU.mult)
            if p < PAIRS - 1:
                v.drain()
        return r
    kedge = P.step("dv", [phic[39], dc_edge], f_kedge, "kedge")
    def f_wcl(v):
        v.reduce_sum(wtot2[:], w2acc[:], axis=AX.X)
        v.drain()
        return v.reduce_sum(wtot1[:], w1acc[:], axis=AX.X)
    wcl = P.step("dv", [wab_abs[-1]], f_wcl, "wcl")

    def f_wsm(te):
        te.matmul(pA[0:1, 0:1], wtot2[:], ones_col[:], start=True, stop=True)
        return te.matmul(pA[0:1, 64:65], wtot1[:], ones_col[:], start=True, stop=True)
    wsm = P.step("pe", [wcl, init0, phic[36]], f_wsm, "wsm")

    def f_wsv(v):
        v.tensor_scalar(sv2[:], pA[0:1, 0:1], 1.0 / (HID * D), EPS_Q, ALU.mult, ALU.max)
        v.drain()
        return v.tensor_scalar(sv1[:], pA[0:1, 64:65], 1.0 / (HID * D), EPS_Q,
                               ALU.mult, ALU.max)
    wsv = P.step("dv", [wsm], f_wsv, "wsv")

    def f_wsb(te):
        te.matmul(pA[:, 32:33], ones_row[:], sv2[:], start=True, stop=True)
        return te.matmul(pA[:, 96:97], ones_row[:], sv1[:], start=True, stop=True)
    wsb = P.step("pe", [wsv], f_wsb, "wsb")

    def f_wsf(v):
        v.tensor_copy(s2_t[:], pA[:, 32:33])
        v.drain()
        v.tensor_copy(s1_t[:], pA[:, 96:97])
        v.drain()
        v.tensor_scalar(rh2_t[:], pA[:, 32:33], 0.5, None, ALU.mult)
        v.drain()
        v.reciprocal(rh2_t[:], rh2_t[:])
        v.drain()
        v.tensor_scalar(rh1_t[:], pA[:, 96:97], 0.5, None, ALU.mult)
        v.drain()
        return v.reciprocal(rh1_t[:], rh1_t[:])
    wsf = P.step("dv", [wsb], f_wsf, "wsf")

    NWQ = 64
    wql = [None] * NWQ; wqle = [None] * NWQ; wqge = [None] * NWQ
    wqsub = [None] * NWQ; wqs = [None] * NWQ

    def make_quant(k):
        which = "w2" if k < 32 else "w1"
        j = k if k < 32 else k - 32

        parts = []
        for qt in range(4):
            def f_ld(sy, sem, count_only=False, which=which, j=j, k=k, qt=qt):
                if count_only:
                    return 1
                r0, r1 = 32 * qt, 32 * qt + 32
                if which == "w2":
                    s_ = w2T_in.ap()[128 * j + r0:128 * j + r1, :]
                else:
                    dk, q = divmod(j, 4)
                    s_ = w1T_in.ap()[128 * dk + r0:128 * dk + r1,
                                     1024 * q:1024 * (q + 1)]
                sy.dma_start(w23s[r0:r1, k % 3, :], s_).then_inc(sem, 16)
            if k < 3:
                pdep = [wab_abs[-1], t1[NT - 1]]
            elif k in (32, 33, 34):
                pdep = [wo_cast[[5, 6, 7][k - 32]]]
            else:
                pdep = [wqge[k - 3]]
            parts.append(P.dma("sy", pdep, f_ld, f"wql{k}_{qt}"))
        wql[k] = parts

        def f_le(v, which=which, k=k):
            rh = rh1_t if which == "w1" else rh2_t
            return v.tensor_scalar(wqtmp[:, k % 2, :], w23s[:, k % 3, :], rh[:], -1.0,
                                   ALU.mult, ALU.is_le)
        ldeps = list(wql[k]) + [wsf]
        if k < 2:
            ldeps.append(phic[39])
        else:
            ldeps.append(wqsub[k - 2])
        wqle[k] = P.step("dv", ldeps, f_le, f"wqle{k}")

        def f_ge(v, which=which, k=k):
            rh = rh1_t if which == "w1" else rh2_t
            return v.tensor_scalar(wqt_o[:, k % 3, :], w23s[:, k % 3, :], rh[:], 1.0,
                                   ALU.mult, ALU.is_ge)
        gdeps = [wqle[k]]
        if k < 2:
            gdeps.append(wqmm[15])
        else:
            gdeps.append(wqsub[k - 2])
        if k >= 3:
            gdeps += wqs[k - 3]
        wqge[k] = P.step("dv", gdeps, f_ge, f"wqge{k}")

        def f_sub(gp, k=k):
            return gp.tensor_tensor(wqt_o[:, k % 3, :], wqt_o[:, k % 3, :],
                                    wqtmp[:, k % 2, :], ALU.subtract)
        wqsub[k] = P.step("gp", [wqge[k]], f_sub, f"wqsub{k}")

        sparts = []
        for sh in range(2):
            def f_st(s, sem, count_only=False, which=which, j=j, k=k, sh=sh):
                if count_only:
                    return 1
                r0, r1 = 64 * sh, 64 * sh + 64
                if which == "w2":
                    dst = wq2_scr.ap()[128 * j + r0:128 * j + r1, :]
                else:
                    dk, q = divmod(j, 4)
                    dst = wq1_scr.ap()[128 * dk + r0:128 * dk + r1,
                                       1024 * q:1024 * (q + 1)]
                s.dma_start(dst, wqt_o[r0:r1, k % 3, :]).then_inc(sem, 16)
            sparts.append(P.dma("se", [wqsub[k]], f_st, f"wqs{k}_{sh}"))
        wqs[k] = sparts

    vmm = [None] * NVT; vev = [None] * NVT
    NH = NT * PAIRS
    smm = [None] * NH; sev = [None] * NH
    amm = [None] * NH; aev = [None] * NH

    def make_v(i):
        rows = 128 if i < 16 else 6

        def f_vmm(te, i=i, rows=rows):
            half = pA[:, 0:1024]
            r = None
            for kd in range(KD):
                xs = x_ln[:, kd, 128 * i:128 * i + rows]
                r = te.matmul(half[0:rows, 0:512], xs, wv_sb[:, kd, 0:512],
                              start=(kd == 0), stop=(kd == KD - 1))
            for kd in range(KD):
                xs = x_ln[:, kd, 128 * i:128 * i + rows]
                r = te.matmul(half[0:rows, 512:1024], xs, wv_sb[:, kd, 512:1024],
                              start=(kd == 0), stop=(kd == KD - 1))
            return r
        deps = [t1e[NT - 1], t16e, wv_cast[7], wsf]
        if i == 0:
            deps += [phic[36], phic[37]]
        else:
            deps.append(vev[i - 1])
        vmm[i] = P.step("pe", deps, f_vmm, f"vmm{i}")

        def f_vev(s, i=i, rows=rows):
            half = pA[:, 0:1024]
            return s.activation(Vt[0:rows, i % 3, 0:8, 0:128],
                                half[0:rows, :].rearrange("p (h n) -> p h n", h=8),
                                AF.Copy, scale=1.0)
        vdeps = [vmm[i], init0]
        if i >= 3:
            vdeps.append(aev[4 * (i - 3) + 3])
        vev[i] = P.step("se", vdeps, f_vev, f"vev{i}")

    for i in range(3):
        make_v(i)

    def make_amm(j):
        c2, p2 = divmod(j, PAIRS)
        h0, h1 = 2 * p2, 2 * p2 + 1

        def f_amm(te, c=c2, i=j, h0=h0, h1=h1):
            sb = i % 2
            ab = i % 2
            bank = pB[:, 1024 + 512 * ab:1024 + 512 * ab + 512]
            te.matmul(bank[:, 0:129], S1m[:, sb, 0:128], Vt[:, c % 3, h0, :],
                      start=True, stop=False)
            te.matmul(bank[:, 0:129], S2m[0:6, sb, 0:128],
                      Vt[0:6, (c + 1) % 3, h0, :], start=False, stop=True)
            te.matmul(bank[:, 256:385], S1m[:, sb, 128:256], Vt[:, c % 3, h1, :],
                      start=True, stop=False)
            return te.matmul(bank[:, 256:385], S2m[0:6, sb, 128:256],
                             Vt[0:6, (c + 1) % 3, h1, :], start=False, stop=True)
        adeps = [sev[j], vev[c2], vev[c2 + 1]]
        if j >= 2:
            adeps.append(aev[j - 2])
        amm[j] = P.step("pe", adeps, f_amm, f"amm{j}")

        def f_aev(v, c=c2, i=j, h0=h0, h1=h1):
            ab = i % 2
            bank = pB[:, 1024 + 512 * ab:1024 + 512 * ab + 512]
            v.tensor_scalar(za[:], bank[:, 128:129], EPS_Z, None, ALU.add)
            v.tensor_scalar(zb[:], bank[:, 384:385], EPS_Z, None, ALU.add)
            v.drain()
            v.reciprocal(za[:], za[:])
            v.reciprocal(zb[:], zb[:])
            v.drain()
            v.tensor_scalar(A_tm[:, c, 128 * h0:128 * h0 + 128], bank[:, 0:128],
                            za[:], None, ALU.mult)
            return v.tensor_scalar(A_tm[:, c, 128 * h1:128 * h1 + 128],
                                   bank[:, 256:384], zb[:], None, ALU.mult)
        aev[j] = P.step("dv", [amm[j]], f_aev, f"aev{j}")

    for c in range(NT):
        for p in range(PAIRS):
            i = c * PAIRS + p

            def f_smm(te, c=c, p=p, i=i):
                base = pB[:, 0:1024] if (i % 2 == 0) else pA[:, 1024:2048]
                bankA, bankB = base[:, 0:512], base[:, 512:1024]
                qap0 = qphi[0:64, p, 128 * c + 3:128 * c + 131]
                qap1 = qphi[64:128, p, 128 * c + 3:128 * c + 131]
                te.matmul(bankA[:, 0:128], kphi[0:64, p, 128 * c:128 * c + 128],
                          qap0, start=True, stop=True)
                te.matmul(bankB[:, 0:128], kphi[64:128, p, 128 * c:128 * c + 128],
                          qap1, start=True, stop=True)
                te.matmul(bankA[0:6, 128:256],
                          kphi[0:64, p, 128 * c + 128:128 * c + 134], qap0,
                          start=True, stop=True)
                return te.matmul(bankB[0:6, 128:256],
                                 kphi[64:128, p, 128 * c + 128:128 * c + 134], qap1,
                                 start=True, stop=True)
            sdeps = [kedge]
            if i == 1:
                sdeps += [phic[38], phic[39], wsf]
            if i >= 2:
                sdeps.append(sev[i - 2])
            smm[i] = P.step("pe", sdeps, f_smm, f"smm{i}")

            def f_sev(v, i=i):
                sb = i % 2
                base = pB[:, 0:1024] if (i % 2 == 0) else pA[:, 1024:2048]
                bankA, bankB = base[:, 0:512], base[:, 512:1024]
                v.tensor_tensor(S1m[:, sb, 0:128], bankA[:, 0:128], mask1[:], ALU.mult)
                v.tensor_tensor(S1m[:, sb, 128:256], bankB[:, 0:128], mask1[:], ALU.mult)
                v.tensor_tensor(S2m[0:6, sb, 122:128], bankA[0:6, 250:256],
                                mask2[0:6, 122:128], ALU.mult)
                return v.tensor_tensor(S2m[0:6, sb, 250:256], bankB[0:6, 250:256],
                                       mask2[0:6, 122:128], ALU.mult)
            svdeps = [smm[i], init0, dc_m1, dc_m2]
            if i >= 2:
                svdeps.append(amm[i - 2])
            sev[i] = P.step("dv", svdeps, f_sev, f"sev{i}")

            if i >= 1:
                make_amm(i - 1)
            if p == 0 and c >= 1 and c + 2 <= NVT - 1:
                make_v(c + 2)

        if c >= 2:
            for _k in range(((c - 2) * 32) // 14, ((c - 1) * 32) // 14):
                make_quant(_k)

    make_amm(NT * PAIRS - 1)


    wo_ld = [None] * KD; wo_cast = [None] * KD
    for j in range(KD):
        def f_wold(gp, sem, count_only=False, j=j):
            if count_only:
                return 1
            gp.dma_start(w23s[:, j % 3, :],
                         w_out_in.ap()[128 * j:128 * (j + 1), :]).then_inc(sem, 16)
        deps = [wqge[[30, 31, 29][j]]] if j < 3 else [wo_cast[j - 3]]
        wo_ld[j] = P.dma("gp", deps, f_wold, f"wold{j}")

        def f_wocast(v, j=j):
            return v.tensor_copy(wo_sb[:, j, :], w23s[:, j % 3, :])
        wo_cast[j] = P.step("dv", [wo_ld[j]] + ([smm[NH - 1]] if j < 2 else []),
                            f_wocast, f"woc{j}")

    t2 = [None] * NT; t2e = [None] * NT
    for c in range(NT):
        def f_t2(te, c=c):
            bank = pA[:, 512 * (c % 2):512 * (c % 2) + 512].bitcast(BF16)
            r = None
            for k in range(KD):
                r = te.transpose(bank[:, 128 * k:128 * k + 128],
                                 A_tm[:, c, 128 * k:128 * k + 128], idb[:])
            return r
        deps = [aev[4 * c + 3], vev[NVT - 1], dc_idb]
        if c >= 2:
            deps.append(t2e[c - 2])
        t2[c] = P.step("pe", deps, f_t2, f"t2_{c}")

        def f_t2e(s, c=c):
            bank = pA[:, 512 * (c % 2):512 * (c % 2) + 512].bitcast(BF16)
            dst = A_fm[:, :, 128 * c:128 * c + 128]
            return s.activation(dst, bank.rearrange("p (k n) -> p k n", k=KD),
                                AF.Copy, scale=1.0)
        t2e[c] = P.step("se", [t2[c]], f_t2e, f"t2e{c}")

    womm = {}; woev = {}
    for m in range(8):
        for n in range(4):
            def f_womm(te, m=m, n=n):
                bank = pB[:, 512 * n:512 * n + 512]
                r = None
                for kd in range(KD):
                    r = te.matmul(bank, wo_sb[:, kd, 128 * m:128 * m + 128],
                                  A_fm[:, kd, 512 * n:512 * n + 512],
                                  start=(kd == 0), stop=(kd == KD - 1))
                return r
            deps = [wo_cast[KD - 1], t2e[4 * n + 3], aev[NH - 1]]
            if m >= 1:
                deps.append(woev[(m - 1, n)])
            womm[(m, n)] = P.step("pe", deps, f_womm, f"womm{m}_{n}")

            def f_woev(s, m=m, n=n):
                bank = pB[:, 512 * n:512 * n + 512]
                return s.activation(opfm[:, m, 512 * n:512 * n + 512], bank, AF.Identity,
                                    bias=b_out[:, m:m + 1], scale=1.0)
            woev[(m, n)] = P.step("se", [womm[(m, n)], dc_bout], f_woev, f"woev{m}_{n}")
            make_quant(32 + 4 * m + n)

    t3 = [None] * NT; y_ev = [None] * NT
    ln2s = [None] * NT; ln2q = [None] * NT; ln2f = [None] * NT
    ln2a = [None] * NT; res2 = [None] * NT
    ln3s = [None] * NT; ln3q = [None] * NT; ln3f = [None] * NT; ln3a = [None] * NT
    qx_mx = [None] * NT; qx_sc = [None] * NT; qx_u = [None] * NT; qx_c = [None] * NT
    qx_b = [None] * NT; qx_r = [None] * NT
    for c in range(NT):
        def f_t3(te, c=c):
            bank = pA[:, 512 * (c % 2):512 * (c % 2) + 512].bitcast(BF16)
            r = None
            for m in range(8):
                r = te.transpose(bank[:, 128 * m:128 * m + 128],
                                 opfm[:, m, 128 * c:128 * c + 128], idb[:])
            return r
        deps = [woev[(7, c // 4)], t2e[NT - 1]]
        if c >= 2:
            deps.append(y_ev[c - 2])
        t3[c] = P.step("pe", deps, f_t3, f"t3_{c}")

        def f_yev(s, c=c):
            bank = pA[:, 512 * (c % 2):512 * (c % 2) + 512].bitcast(BF16)
            return s.activation(y_sb[:, c % 2, :], bank, AF.Copy, scale=1.0)
        ydeps = [t3[c]]
        if c >= 2:
            ydeps += [ln2a[c - 2], ln2s[c - 2]]
        y_ev[c] = P.step("se", ydeps, f_yev, f"yev{c}")

        L = LNS["l2"][c % 2]

        def f_ln2s(v, c=c, L=L):
            v.bn_stats(L["bn"][:, 0:6], y_sb[:, c % 2, 0:512])
            v.bn_stats(L["bn"][:, 6:12], y_sb[:, c % 2, 512:1024])
            v.drain()
            v.bn_aggr(L["ag"][:], L["bn"][:])
            v.drain()
            return v.tensor_scalar(L["rs"][:], L["ag"][:, 1:2], EPS_LN, None, ALU.add)
        ln2s[c] = P.step("dv", [y_ev[c]], f_ln2s, f"ln2s{c}")

        def f_ln2q(s, L=L):
            return s.activation(L["rs"][:], L["rs"][:], AF.Sqrt, scale=1.0)
        ln2q[c] = P.step("se", [ln2s[c]], f_ln2q, f"ln2q{c}")

        def f_ln2f(v, L=L):
            v.reciprocal(L["rs"][:], L["rs"][:])
            v.drain()
            return v.scalar_tensor_tensor(L["nm"][:], L["ag"][:, 0:1], -1.0, L["rs"][:],
                                          ALU.mult, ALU.mult)
        ln2f[c] = P.step("dv", [ln2q[c]], f_ln2f, f"ln2f{c}")

        def f_ln2a(s, c=c, L=L):
            return s.activation(yn_sb[:, c % 2, :], y_sb[:, c % 2, :], AF.Identity,
                                bias=L["nm"][:], scale=L["rs"][:])
        adeps = [ln2f[c]]
        if c >= 2:
            adeps.append(res2[c - 2])
        ln2a[c] = P.step("se", adeps, f_ln2a, f"ln2a{c}")

        def f_res2(gp, c=c):
            return gp.tensor_tensor(x_tm[:, c, :], x_tm[:, c, :], yn_sb[:, c % 2, :],
                                    ALU.add)
        res2[c] = P.step("gp", [ln2a[c], xcast[c]], f_res2, f"res2_{c}")


    wq1_ld = [None] * KD
    for k in range(KD):
        def f_wq1ld(gp, sem, count_only=False, k=k):
            if count_only:
                return 1
            gp.dma_start(wq1g[:, k, :],
                         wq1_scr.ap()[128 * k:128 * (k + 1), :]).then_inc(sem, 16)
        wq1_ld[k] = P.dma("gp", sum([wqs[32 + 4 * k + q] for q in range(4)], [])
                          + [womm[(7, 3)], t3[NT - 1]], f_wq1ld, f"wq1l{k}")

    t4 = [None] * NT; t4e = [None] * NT
    f1 = {}; f1g = {}
    qh_mx2 = {}; qh_sc = [None] * NT
    qh_u = {}; qh_c = {}; qh_b = {}; qh_r = {}
    t5 = {}; t5e = {}
    f2 = {}; f2ev = {}; f2evA = {}
    wq2_ld = {}
    t6 = {}; t6e = {}
    ln4s = [None] * NT; ln4q = [None] * NT; ln4f = [None] * NT; ln4a = [None] * NT
    outadd = [None] * NT; yst = [None] * NT

    def emit_lnqx(c):
        L3 = LNS["l3"][c % 2]
        Q = QXs[c % 2]

        def f_ln3s(v, c=c, L=L3):
            v.bn_stats(L["bn"][:, 0:6], x_tm[:, c, 0:512])
            v.bn_stats(L["bn"][:, 6:12], x_tm[:, c, 512:1024])
            v.drain()
            v.bn_aggr(L["ag"][:], L["bn"][:])
            v.drain()
            return v.tensor_scalar(L["rs"][:], L["ag"][:, 1:2], EPS_LN, None,
                                   ALU.add)
        deps = [res2[c]]
        if c >= 2:
            deps.append(ln3a[c - 2])
        ln3s[c] = P.step("dv", deps, f_ln3s, f"ln3s{c}")

        def f_ln3q(s, L=L3):
            return s.activation(L["rs"][:], L["rs"][:], AF.Sqrt, scale=1.0)
        ln3q[c] = P.step("se", [ln3s[c]], f_ln3q, f"ln3q{c}")

        def f_ln3f(v, L=L3):
            v.reciprocal(L["rs"][:], L["rs"][:])
            v.drain()
            return v.scalar_tensor_tensor(L["nm"][:], L["ag"][:, 0:1], -1.0,
                                          L["rs"][:], ALU.mult, ALU.mult)
        ln3f[c] = P.step("dv", [ln3q[c]], f_ln3f, f"ln3f{c}")

        def f_ln3a(s, c=c, L=L3):
            return s.activation(x3_sb[:, :], x_tm[:, c, :], AF.Identity,
                                bias=L["nm"][:], scale=L["rs"][:])
        adeps = [ln3f[c]]
        if c == 0:
            adeps.append(t2[NT - 1])
        if c >= 1:
            adeps.append(qx_b[c - 1])
        ln3a[c] = P.step("se", adeps, f_ln3a, f"ln3a{c}")

        def f_qxmx(v, c=c, Q=Q):
            return v.reduce_max(Q["mx"][:], x3_sb[:, :], axis=AX.X,
                                apply_absolute_value=True)
        qx_mx[c] = P.step("dv", [ln3a[c]], f_qxmx, f"qxmx{c}")

        def f_qxsc(v, Q=Q):
            v.tensor_scalar(Q["isc"][:], Q["mx"][:], EPS_Q, 1.0 / 127.0,
                            ALU.max, ALU.mult)
            v.drain()
            v.reciprocal(Q["sc"][:], Q["isc"][:])
            return v.tensor_tensor(Q["fac"][:], s1_t[:], Q["isc"][:], ALU.mult)
        qx_sc[c] = P.step("dv", [qx_mx[c], wsf], f_qxsc, f"qxsc{c}")

        def f_qxu(v, c=c, Q=Q):
            return v.tensor_scalar(x3_sb[:, :], x3_sb[:, :],
                                   Q["sc"][:], OFF + 0.5, ALU.mult, ALU.add)
        cdeps = [qx_sc[c]]
        if c == 0:
            cdeps.append(t2[NT - 1])
        if c >= 1:
            cdeps.append(qx_r[c - 1])
        qx_u[c] = qx_c[c] = P.step("dv", cdeps, f_qxu, f"qxu{c}")

        def f_qxr(v, c=c):
            v.tensor_scalar(x3_sb[:, :].bitcast(I32), x3_sb[:, :].bitcast(I32),
                            -8192, None, ALU.bitwise_and)
            v.drain()
            return v.tensor_copy(Ax_tm[:, c % 2, :], x3_sb[:, :])
        rdeps = [qx_u[c]]
        if c >= 2:
            rdeps.append(t4[c - 2])
        qx_r[c] = qx_b[c] = P.step("dv", rdeps, f_qxr, f"qxr{c}")

    def emit_t4(c):
        g, cc = divmod(c, GC)

        def f_t4(te, c=c):
            bank = pB[:, 0:512].bitcast(FP16)
            r = None
            for k in range(KD):
                r = te.transpose(bank[:, 128 * k:128 * k + 128],
                                 Ax_tm[:, c % 2, 128 * k:128 * k + 128], idh[:])
            return r
        deps = [qx_r[c], dc_idh]
        if c == 0:
            deps.append(woev[(7, 3)])
        else:
            deps.append(t4e[c - 1])
        if c % GC == 0 and g >= 1:
            deps.append(f2ev[g - 1])
        t4[c] = P.step("pe", deps, f_t4, f"t4_{c}")

        def f_t4e(s, c=c):
            src = pB[:, 0:512].bitcast(FP16).rearrange("p (k n) -> p k n", k=KD)
            return s.activation(Axf[:, c % 2, :, :], src, AF.Identity, bias=-OFF,
                                scale=1.0)
        edeps = [t4[c]]
        if c == 0:
            edeps.append(t2[NT - 1])
        elif c >= 2:
            edeps.append(f1[(c - 2, 3)])
        t4e[c] = P.step("se", edeps, f_t4e, f"t4e{c}")

    def emit_f1(c):
        g, cc = divmod(c, GC)
        for quarter in range(4):
            def f_f1(te, c=c, quarter=quarter):
                base = 1024 * (quarter % 2)
                r = None
                for kd in range(KD):
                    for n in range(2):
                        off = 1024 * quarter + 512 * n
                        r = te.matmul(pA[:, base + 512 * n:base + 512 * n + 512],
                                      Axf[:, c % 2, kd, :],
                                      wq1g[:, kd, off:off + 512],
                                      start=(kd == 0), stop=(kd == KD - 1))
                return r
            deps = [t4e[c]] + list(wq1_ld)
            if quarter >= 2:
                deps.append(f1g[(c, quarter - 2)])
            elif c >= 1:
                deps.append(f1g[(c - 1, quarter + 2)])
            else:
                deps.append(y_ev[NT - 1])
            if quarter < 2 and c % GC == 0 and c >= 4:
                deps.append(f2evA[c // GC - 1])
            f1[(c, quarter)] = P.step("pe", deps, f_f1, f"f1_{c}_{quarter}")

            Q = QXs[c % 2]

            def f_f1g(s, c=c, quarter=quarter, Q=Q):
                base = 1024 * (quarter % 2)
                return s.activation(h_sb[:, c % 2, 1024 * quarter:1024 * quarter + 1024],
                                    pA[:, base:base + 1024], AF.Gelu, bias=0.0,
                                    scale=Q["fac"][:])
            gdeps = [f1[(c, quarter)], qx_sc[c]]
            if c < 2:
                gdeps.append(womm[(7, 3)])
            if c >= 2:
                gdeps += [qh_u[(c - 2, quarter)], qh_mx2[(c - 2, quarter // 2)]]
            f1g[(c, quarter)] = P.step("se", gdeps, f_f1g, f"f1g{c}_{quarter}")



    def emit_qhx(c):
        QH = QHs[c % 2]
        for half in range(2):
            def f_qhmx(v, c=c, half=half, QH=QH):
                return v.reduce_max(QH["mx"][:, half:half + 1],
                                    h_sb[:, c % 2, 2048 * half:2048 * half + 2048],
                                    axis=AX.X, apply_absolute_value=True)
            qh_mx2[(c, half)] = P.step("dv", [f1g[(c, 2 * half)],
                                              f1g[(c, 2 * half + 1)]], f_qhmx,
                                       f"qhmx{c}_{half}")

        def f_qhsc(v, c=c, QH=QH):
            v.tensor_tensor(QH["mx"][:, 0:1], QH["mx"][:, 0:1], QH["mx"][:, 1:2],
                            ALU.max)
            v.drain()
            v.tensor_scalar(QH["isc"][:], QH["mx"][:, 0:1], EPS_Q, 1.0 / 127.0,
                            ALU.max, ALU.mult)
            v.drain()
            v.reciprocal(QH["sc"][:], QH["isc"][:])
            return v.tensor_tensor(fac2buf[:, c % 8:c % 8 + 1], s2_t[:],
                                   QH["isc"][:], ALU.mult)
        qh_sc[c] = P.step("dv", [qh_mx2[(c, 0)], qh_mx2[(c, 1)], wsf],
                          f_qhsc, f"qhsc{c}")

    def emit_strips(c):
        QH = QHs[c % 2]
        for q in range(4):
            def f_qhu(v, c=c, q=q, QH=QH):
                return v.tensor_scalar(u05h[:, q % 2, :],
                                       h_sb[:, c % 2, 1024 * q:1024 * q + 1024],
                                       QH["sc"][:], OFF + 0.5, ALU.mult, ALU.add)
            ueng = "dv"
            deps = [qh_sc[c]]
            if q >= 2:
                deps.append(qh_r[(c, q - 2)])
            elif c >= 1:
                deps.append(qh_r[(c - 1, q + 2)])
            elif q < 2:
                deps.append(t2[NT - 1])
            qh_u[(c, q)] = qh_c[(c, q)] = P.step(ueng, deps, f_qhu, f"qhu{c}_{q}")

            def f_qhr(v, c=c, q=q):
                v.tensor_scalar(u05h[:, q % 2, :].bitcast(I32),
                                u05h[:, q % 2, :].bitcast(I32),
                                -8192, None, ALU.bitwise_and)
                v.drain()
                return v.tensor_copy(Ah_tm[:, 1024 * q:1024 * q + 1024],
                                     u05h[:, q % 2, :])
            deps = [qh_u[(c, q)]]
            if c >= 1:
                deps.append(t5[(c - 1, 3)])
            elif q == 0:
                deps.append(womm[(7, 3)])
            qh_r[(c, q)] = qh_b[(c, q)] = P.step("dv", deps, f_qhr, f"qhr{c}_{q}")

    def emit_tfb(c):
        g, cc = divmod(c, GC)
        for j in range(4):
            def f_t5(te, c=c, j=j):
                bank = pB[:, 512 + 512 * (j % 2):1024 + 512 * (j % 2)].bitcast(FP16)
                r = None
                for b in range(8):
                    k = 8 * j + b
                    r = te.transpose(bank[:, 128 * b:128 * b + 128],
                                     Ah_tm[:, 128 * k:128 * k + 128], idh[:])
                return r
            deps = [qh_r[(c, j)]]
            if j == 0:
                if c >= 1:
                    deps.append(t5e[(c - 1, 2)])
            elif j == 1:
                if c >= 1:
                    deps.append(t5e[(c - 1, 3)])
            else:
                deps.append(t5e[(c, j - 2)])
            t5[(c, j)] = P.step("pe", deps, f_t5, f"t5_{c}_{j}")

            def f_t5e(v, c=c, cc=cc, j=j):
                src = pB[:, 512 + 512 * (j % 2):1024 + 512 * (j % 2)].bitcast(FP16)
                dst = Ahfm[:, 8 * j:8 * j + 8, 128 * cc:128 * cc + 128]
                return v.tensor_scalar(dst, src.rearrange("p (k n) -> p k n", k=8),
                                       -OFF, None, ALU.add)
            deps = [t5[(c, j)]]
            if c == 0:
                deps += [womm[(7, 3)], ln2a[NT - 1], ln2s[NT - 1], res2[NT - 1]]
            if cc == 0 and g >= 1:
                deps.append(f2[(g - 1, KH - 1)])
            t5e[(c, j)] = P.step("dv", deps, f_t5e, f"t5e{c}_{j}")

    def emit_f2(g):
        for k in range(KH):
            deps = list(wqs[k])
            if g == 0:
                deps.append(t2[NT - 1])
            if g == 0 and k < 4:
                deps += sum(wqs[NWQ - 4:], [])
            if k >= 4:
                deps.append(f2[(g, k - 4)])
            elif g >= 1:
                deps.append(f2[(g - 1, KH - 4 + k)])
            parts = []
            for qt in range(4):
                def f_wq2ld(gp, sem, count_only=False, g=g, k=k, qt=qt):
                    if count_only:
                        return 1
                    gp.dma_start(wq2r[32 * qt:32 * qt + 32, k % 4, :],
                                 wq2_scr.ap()[128 * k + 32 * qt:128 * k + 32 * qt + 32,
                                              :]).then_inc(sem, 16)
                parts.append(P.dma("gp" if qt % 2 == 0 else "sy", deps, f_wq2ld,
                                   f"wq2l{g}_{k}_{qt}"))
            wq2_ld[(g, k)] = parts

            def f_f2(te, g=g, k=k):
                r = None
                for m in range(8):
                    bank = (pA[:, 512 * m:512 * m + 512] if m < 4
                            else pB[:, 512 * (m - 4):512 * (m - 4) + 512])
                    r = te.matmul(bank, wq2r[:, k % 4, 128 * m:128 * m + 128],
                                  Ahfm[:, k, :], start=(k == 0), stop=(k == KH - 1))
                return r
            deps = list(wq2_ld[(g, k)])
            if k % 8 == 0:
                deps += [t5e[(4 * g + cc2, k // 8)] for cc2 in range(GC)]
            if k == 0:
                deps += [f1g[(4 * g + 3, 3)], t5e[(4 * g + 3, 2)],
                         t5e[(4 * g + 3, 3)]]
                if g >= 1:
                    deps.append(t6e[(g - 1, 3)])
            f2[(g, k)] = P.step("pe", deps, f_f2, f"f2_{g}_{k}")

        def f_f2ev(v, g=g):
            r = None
            for m in range(4, 8):
                r = v.tensor_copy(o2fm[:, m, :],
                                  pB[:, 512 * (m - 4):512 * (m - 4) + 512])
            return r
        fdeps = [f2[(g, KH - 1)]]
        if g == 0:
            fdeps.append(wqsub[NWQ - 1])
        f2ev[g] = P.step("dv", fdeps, f_f2ev, f"f2ev{g}")

        def f_f2evA(s, g=g):
            r = None
            for m in range(4):
                r = s.activation(o2fm[:, m, :], pA[:, 512 * m:512 * m + 512],
                                 AF.Copy, scale=1.0)
            return r
        f2evA[g] = P.step("se", list(fdeps), f_f2evA, f"f2evA{g}")

    def emit_tail(g, cc):
        c = 4 * g + cc

        def f_t6(te, g=g, cc=cc):
            bank = pB[:, 1536:2048].bitcast(BF16)
            r = None
            for m in range(8):
                r = te.transpose(bank[:, 128 * m:128 * m + 128],
                                 o2fm[:, m, 128 * cc:128 * cc + 128], idb[:])
            return r
        deps = [f2ev[g], f2evA[g]]
        if cc >= 1:
            deps.append(t6e[(g, cc - 1)])
        t6[(g, cc)] = P.step("pe", deps, f_t6, f"t6_{g}_{cc}")

        def f_t6e(v, g=g, cc=cc, c=c):
            bank = pB[:, 1536:2048].bitcast(BF16)
            return v.tensor_scalar(y2_sb[:, :], bank,
                                   fac2buf[:, c % 8:c % 8 + 1], None, ALU.mult)
        deps = [t6[(g, cc)], qh_sc[c]]
        if c == 0:
            deps.append(wqge[NWQ - 1])
        if c >= 1:
            deps += [ln4a[c - 1], ln4s[c - 1]]
        t6e[(g, cc)] = P.step("dv", deps, f_t6e, f"t6e{g}_{cc}")

        L = LNS["l4"][c % 2]

        def f_ln4s(v, c=c, L=L):
            v.bn_stats(L["bn"][:, 0:6], y2_sb[:, 0:512])
            v.bn_stats(L["bn"][:, 6:12], y2_sb[:, 512:1024])
            v.drain()
            v.bn_aggr(L["ag"][:], L["bn"][:])
            v.drain()
            return v.tensor_scalar(L["rs"][:], L["ag"][:, 1:2], EPS_LN, None,
                                   ALU.add)
        ln4s[c] = P.step("dv", [t6e[(g, cc)]], f_ln4s, f"ln4s{c}")

        def f_ln4q(s, L=L):
            return s.activation(L["rs"][:], L["rs"][:], AF.Sqrt, scale=1.0)
        ln4q[c] = P.step("se", [ln4s[c]], f_ln4q, f"ln4q{c}")

        def f_ln4f(v, L=L):
            v.reciprocal(L["rs"][:], L["rs"][:])
            v.drain()
            return v.scalar_tensor_tensor(L["nm"][:], L["ag"][:, 0:1], -1.0,
                                          L["rs"][:], ALU.mult, ALU.mult)
        ln4f[c] = P.step("dv", [ln4q[c]], f_ln4f, f"ln4f{c}")

        def f_ln4a(s, c=c, L=L):
            return s.activation(yn4_sb[:, :], y2_sb[:, :], AF.Identity,
                                bias=L["nm"][:], scale=L["rs"][:])
        adeps = [ln4f[c]]
        if c == 0:
            adeps.append(wqge[NWQ - 1])
        if c >= 1:
            adeps.append(outadd[c - 1])
        ln4a[c] = P.step("se", adeps, f_ln4a, f"ln4a{c}")

        def f_out(gp, c=c):
            return gp.tensor_tensor(yout_sb[:, :], x_tm[:, c, :],
                                    yn4_sb[:, :], ALU.add)
        odeps = [ln4a[c], ln3s[c]]
        if c == 0:
            odeps += [wqge[NWQ - 1]] + sum(wqs[NWQ - 4:], [])
        if c >= 1:
            odeps.append(yst[c - 1])
        outadd[c] = P.step("gp", odeps, f_out, f"out{c}")

        def f_yst(sy, sem, count_only=False, c=c):
            if count_only:
                return 1
            sy.dma_start(y_out.ap()[128 * c:128 * (c + 1), :],
                         yout_sb[:, :]).then_inc(sem, 16)
        yst[c] = P.dma("sy", [outadd[c]], f_yst, f"yst{c}")

    for g in range(NG):
        if g >= 1:
            emit_strips(4 * g - 1)
            emit_tfb(4 * g - 1)
            emit_f2(g - 1)
        emit_lnqx(4 * g)
        emit_lnqx(4 * g + 1)
        emit_t4(4 * g)
        for cc in range(GC):
            c = 4 * g + cc
            emit_f1(c)
            if cc < GC - 1:
                emit_t4(c + 1)
            if cc >= 1:
                emit_strips(c - 1)
                emit_tfb(c - 1)
            emit_qhx(c)
            if cc + 2 < GC:
                emit_lnqx(c + 2)
            if g >= 1:
                emit_tail(g - 1, cc)
    emit_strips(NT - 1)
    emit_tfb(NT - 1)
    emit_f2(NG - 1)
    for cc in range(GC):
        emit_tail(NG - 1, cc)

    def f_fin(sy):
        return sy.nop()
    P.step("sy", [yst[c] for c in range(NT)], f_fin, "fin")

    with nc.Block("mk") as blk:
        P.emit(blk, ctr_h, qctr_h)

    st.close()
    return nc



def kernel(L0_attn_x, L0_attn_w_qkv, L0_attn_proj, L0_attn_w_out, L0_attn_b_out,
           L0_ffn_w1, L0_ffn_w2):
    x = np.asarray(L0_attn_x, dtype=np.float32)
    w_qkv = np.asarray(L0_attn_w_qkv, dtype=np.float32)
    proj = np.ascontiguousarray(np.asarray(L0_attn_proj, dtype=np.float32))
    w_out = np.ascontiguousarray(np.asarray(L0_attn_w_out, dtype=np.float32))
    b_out = np.asarray(L0_attn_b_out, dtype=np.float32)
    w1 = np.asarray(L0_ffn_w1, dtype=np.float32)
    w2 = np.asarray(L0_ffn_w2, dtype=np.float32)

    wqkT = np.ascontiguousarray(w_qkv[:, :2048].T)
    w_v = np.ascontiguousarray(w_qkv[:, 2048:])
    b_out_r = np.ascontiguousarray(b_out.reshape(8, 128).T)
    w1T = np.ascontiguousarray(w1.T)
    w2T = np.ascontiguousarray(w2.T)

    jj, rr = np.meshgrid(np.arange(128), np.arange(128), indexing="ij")
    mask1 = ((jj - rr >= 0) & (jj - rr <= 6)).astype(np.float32)
    j2, r2 = np.meshgrid(np.arange(6), np.arange(128), indexing="ij")
    mask2 = (r2 - j2 >= 122).astype(np.float32)
    identf = np.eye(128, dtype=np.float32)
    identb = np.eye(128).astype(ml_dtypes.bfloat16)
    identh = np.eye(128, dtype=np.float16)

    in_maps = []
    for core in range(8):
        b = core // 4
        s0 = (core % 4) * TOK
        xh = np.zeros((HALO, D), dtype=np.float32)
        lo, hi = s0 - 3, s0 + TOK + 3
        src_lo, src_hi = max(lo, 0), min(hi, 8192)
        xh[src_lo - lo:src_hi - lo, :] = x[b, src_lo:src_hi, :]
        edge = np.ones((128, 6), dtype=np.float32)
        if s0 == 0:
            edge[:, 0:3] = 0.0
        if s0 + TOK == 8192:
            edge[:, 3:6] = 0.0
        in_maps.append({
            "x_halo": xh, "wqkT": wqkT, "w_v": w_v, "proj": proj,
            "w_out": w_out, "b_out_r": b_out_r, "w1T": w1T, "w2T": w2T,
            "mask1": mask1, "mask2": mask2, "identf": identf, "identb": identb,
            "identh": identh, "edgem": edge,
        })

    nc = build_nc()
    res = run_bass_kernel_spmd(nc, in_maps, core_ids=list(range(8)))
    out = np.concatenate([np.asarray(res.results[i]["y"]) for i in range(8)], axis=0)
    return out.reshape(2, 8192, D)


# version 29
# speedup vs baseline: 1.0241x; 1.0013x over previous
import numpy as np
import ml_dtypes

import concourse.bass as bass
import concourse.mybir as mybir
from concourse.bass_utils import run_bass_kernel_spmd

try:
    from concourse.compiler_utils import get_compiler_flags, set_compiler_flags
    set_compiler_flags([f.replace("--enable-ldw-opt=false", "--enable-ldw-opt=true")
                        for f in get_compiler_flags()])
except Exception:
    pass

F32 = mybir.dt.float32
BF16 = mybir.dt.bfloat16
FP16 = mybir.dt.float16
I16 = mybir.dt.int16
I32 = mybir.dt.int32
AF = mybir.ActivationFunctionType
ALU = mybir.AluOpType
AX = mybir.AxisListType

D = 1024
KD = 8
HEADS = 8
PAIRS = 4
TOK = 2048
NT = 16
HALO = 2054
NVT = 17
HID = 4096
KH = 32
NG = 4
GC = 4
EPS_LN = 1e-5
EPS_Z = 1e-8
EPS_Q = 1e-5
OFF = 1536.0
PHI_NT = [512, 512, 512, 512, 6]

ENGS = ("pe", "dv", "se", "gp", "sy")


NQS = 4


class Step:
    __slots__ = ("eng", "idx", "cord", "deps", "fn", "dma", "qord", "name")

    def __init__(self, eng, idx, deps, fn, dma, name):
        self.eng = eng
        self.idx = idx
        self.cord = None
        self.deps = deps
        self.fn = fn
        self.dma = dma
        self.qord = None
        self.name = name


class Prog:
    def __init__(self):
        self.steps = {e: [] for e in ENGS}

    def step(self, eng, deps, fn, name=""):
        s = Step(eng, len(self.steps[eng]), [d for d in deps if d is not None],
                 fn, False, name)
        s.cord = sum(1 for t in self.steps[eng] if not t.dma)
        self.steps[eng].append(s)
        return s

    def dma(self, eng, deps, fn, name=""):
        s = Step(eng, len(self.steps[eng]), [d for d in deps if d is not None],
                 fn, True, name)
        s.qord = sum(1 for t in self.steps[eng] if t.dma)
        self.steps[eng].append(s)
        return s

    def emit(self, blk, ctrs, qctrs):
        engobjs = {"pe": blk.tensor, "dv": blk.vector, "se": blk.scalar,
                   "gp": blk.gpsimd, "sy": blk.sync}
        dmalist = {e: [s for s in self.steps[e] if s.dma] for e in ENGS}

        def make_body(eng_name):
            def body(eng):
                hw = {}

                def wait_for(d):
                    if d.dma:
                        key = ("q", d.eng, d.qord % NQS)
                        val = 16 * (d.qord // NQS + 1)
                        sem = qctrs[d.eng][d.qord % NQS]
                    else:
                        key, val, sem = ("c", d.eng), d.cord + 1, ctrs[d.eng]
                    if hw.get(key, 0) < val:
                        hw[key] = val
                        eng.wait_ge(sem, val)

                for s in self.steps[eng_name]:
                    for d in s.deps:
                        if d.eng == eng_name and not d.dma and not s.dma:
                            continue
                        wait_for(d)
                    if s.dma:
                        if s.qord >= NQS:
                            wait_for(dmalist[eng_name][s.qord - NQS])
                        sem = qctrs[eng_name][s.qord % NQS]
                        s.fn(eng, sem, count_only=False)
                    else:
                        if eng_name in ("dv", "se", "gp"):
                            eng.drain()
                        last = s.fn(eng)
                        assert last is not None, f"step {s.name} returned None"
                        last.then_inc(ctrs[eng_name], 1)
            return body

        for e in ENGS:
            engobjs[e](make_body(e))


def build_nc():
    nc = bass.Bass("TRN2", target_bir_lowering=False)

    x_in = nc.dram_tensor("x_halo", [HALO, D], F32, kind="ExternalInput")
    wqkT_in = nc.dram_tensor("wqkT", [2 * D, D], F32, kind="ExternalInput")
    w_v_in = nc.dram_tensor("w_v", [D, D], F32, kind="ExternalInput")
    proj_in = nc.dram_tensor("proj", [128, 64], F32, kind="ExternalInput")
    w_out_in = nc.dram_tensor("w_out", [D, D], F32, kind="ExternalInput")
    b_out_in = nc.dram_tensor("b_out_r", [128, 8], F32, kind="ExternalInput")
    w1T_in = nc.dram_tensor("w1T", [D, HID], F32, kind="ExternalInput")
    w2T_in = nc.dram_tensor("w2T", [HID, D], F32, kind="ExternalInput")
    mask1_in = nc.dram_tensor("mask1", [128, 128], F32, kind="ExternalInput")
    mask2_in = nc.dram_tensor("mask2", [6, 128], F32, kind="ExternalInput")
    idf_in = nc.dram_tensor("identf", [128, 128], F32, kind="ExternalInput")
    idb_in = nc.dram_tensor("identb", [128, 128], BF16, kind="ExternalInput")
    idh_in = nc.dram_tensor("identh", [128, 128], FP16, kind="ExternalInput")
    edge_in = nc.dram_tensor("edgem", [128, 6], F32, kind="ExternalInput")
    y_out = nc.dram_tensor("y", [TOK, D], F32, kind="ExternalOutput")
    wq1_scr = nc.dram_tensor("wq1_scr", [D, HID], BF16, kind="Internal")
    wq2_scr = nc.dram_tensor("wq2_scr", [HID, D], BF16, kind="Internal")

    from contextlib import ExitStack
    st = ExitStack()
    SB = lambda n, s, d: st.enter_context(nc.sbuf_tensor(n, s, d))

    x_tm = SB("x_tm", [128, NT, D], BF16)
    mask1 = SB("m1", [128, 128], F32)
    mask2 = SB("m2", [6, 128], F32)
    idf = SB("idf", [128, 128], F32)
    idb = SB("idb", [128, 128], BF16)
    idh = SB("idh", [128, 128], FP16)
    edge = SB("edge", [128, 6], F32)
    b_out = SB("b_out", [128, 8], F32)
    proj_f = SB("proj_f", [128, 64], F32)
    proj_b = SB("proj_b", [128, 64], BF16)
    ones_col = SB("ones_col", [128, 1], F32)
    ones_row = SB("ones_row", [1, 128], F32)
    LNS = {}
    for lname in ("l1", "l2", "l3", "l4"):
        LNS[lname] = [dict(bn=SB(f"{lname}bn{i}", [128, 12], F32),
                           ag=SB(f"{lname}ag{i}", [128, 2], F32),
                           rs=SB(f"{lname}rs{i}", [128, 1], F32),
                           nm=SB(f"{lname}nm{i}", [128, 1], F32)) for i in range(2)]
    QXs = [dict(mx=SB(f"qxm{i}", [128, 1], F32), isc=SB(f"qxi{i}", [128, 1], F32),
                sc=SB(f"qxs{i}", [128, 1], F32), fac=SB(f"qxf{i}", [128, 1], F32))
           for i in range(2)]
    QHs = [dict(mx=SB(f"qhm{i}", [128, 2], F32), isc=SB(f"qhi{i}", [128, 1], F32),
                sc=SB(f"qhs{i}", [128, 1], F32)) for i in range(2)]
    fac2buf = SB("fac2buf", [128, 8], F32)
    za = SB("za", [128, 1], F32)
    zb = SB("zb", [128, 1], F32)
    w1acc = SB("w1acc", [128, 32], F32)
    w2acc = SB("w2acc", [128, 32], F32)
    wtot2 = SB("wtot2", [128, 1], F32)
    wtot1 = SB("wtot1", [128, 1], F32)
    sv1 = SB("sv1", [1, 1], F32)
    sv2 = SB("sv2", [1, 1], F32)
    s1_t = SB("s1_t", [128, 1], F32)
    s2_t = SB("s2_t", [128, 1], F32)
    rh1_t = SB("rh1_t", [128, 1], F32)
    rh2_t = SB("rh2_t", [128, 1], F32)

    pA = st.enter_context(nc.psum_tensor("pA", [128, 2048], F32))
    pB = st.enter_context(nc.psum_tensor("pB", [128, 2048], F32))

    class Ovl:
        def __init__(self, base_off):
            self.off = base_off

        def sbuf_tensor(self, name, shape, dtype):
            import functools, operator
            per = functools.reduce(operator.mul, shape[1:], 1) * mybir.dt.size(dtype)
            off = self.off
            self.off = (self.off + per + 31) & ~31
            hnd = nc.virtual_sbuf_tensor(name, shape, dtype)
            nc._tensor(hnd.name, hnd.shape, hnd.dtype, type="SB")
            mloc = nc.lookup_mloc(hnd)
            mloc.allocated = True
            pend.append((mloc, off))
            return hnd

    pend = []
    arAV = Ovl(0)
    A_tm = arAV.sbuf_tensor("A_tm", [128, NT, D], BF16)
    Vt = arAV.sbuf_tensor("Vt", [128, 3, HEADS, 129], BF16)
    S1m = arAV.sbuf_tensor("S1m", [128, 2, 256], BF16)
    S2m = arAV.sbuf_tensor("S2m", [6, 2, 256], BF16)
    AVSZ = (arAV.off + 31) & ~31

    arAT = Ovl(AVSZ)
    x_ln = arAT.sbuf_tensor("x_ln", [128, KD, HALO], BF16)
    qphi = arAT.sbuf_tensor("qphi", [128, PAIRS, HALO], BF16)
    kphi = arAT.sbuf_tensor("kphi", [128, PAIRS, HALO], BF16)
    wqp = arAT.sbuf_tensor("wqp", [128, KD, 512], BF16)
    wkp = arAT.sbuf_tensor("wkp", [128, KD, 512], BF16)
    wv_sb = arAT.sbuf_tensor("wv_sb", [128, KD, D], BF16)
    xin = arAT.sbuf_tensor("xin", [128, 2, D], F32)
    xhalo6 = arAT.sbuf_tensor("xh6", [6, D], F32)
    XLNTM_OFF = arAT.off
    xlntm = arAT.sbuf_tensor("xlntm", [128, D], F32)
    stage32 = arAT.sbuf_tensor("stage32", [128, 2, D], F32)
    wqk_b = arAT.sbuf_tensor("wqk_b", [128, 2, D], BF16)
    elu_t = arAT.sbuf_tensor("elu_t", [128, 2, 512], F32)

    arP23 = Ovl(AVSZ)
    A_fm = arP23.sbuf_tensor("A_fm", [128, KD, TOK], BF16)
    opfm = arP23.sbuf_tensor("opfm", [128, 8, TOK], BF16)
    wo_sb = arP23.sbuf_tensor("wo_sb", [128, KD, D], BF16)
    y_sb = arP23.sbuf_tensor("y_sb", [128, 2, D], F32)
    yn_sb = arP23.sbuf_tensor("yn_sb", [128, 2, D], F32)

    arP4 = Ovl(0)
    x3_sb = arP4.sbuf_tensor("x3_sb", [128, D], F32)
    Ax_tm = arP4.sbuf_tensor("Ax_tm", [128, 2, D], FP16)
    Axf = arP4.sbuf_tensor("Axf", [128, 2, KD, 128], BF16)
    u05h = arP4.sbuf_tensor("u05h", [128, 2, D], F32)
    wq2r = arP4.sbuf_tensor("wq2r", [128, 4, D], BF16)
    arP4.sbuf_tensor("pad4", [128, 2048], mybir.dt.uint8)
    h_sb = arP4.sbuf_tensor("h_sb", [128, 2, HID], BF16)
    Ah_tm = arP4.sbuf_tensor("Ah_tm", [128, HID], FP16)
    wq1g = arP4.sbuf_tensor("wq1g", [128, KD, HID], BF16)
    Ahfm = arP4.sbuf_tensor("Ahfm", [128, KH, 512], BF16)
    o2fm = arP4.sbuf_tensor("o2fm", [128, 8, 512], BF16)
    y2_sb = arP4.sbuf_tensor("y2_sb", [128, D], BF16)
    yn4_sb = arP4.sbuf_tensor("yn4_sb", [128, D], BF16)
    yout_sb = arP4.sbuf_tensor("yout", [128, D], F32)

    arWQ = Ovl(XLNTM_OFF)
    w23s = arWQ.sbuf_tensor("w23s", [128, 3, D], F32)
    wqt_o = arWQ.sbuf_tensor("wqt_o", [128, 3, D], BF16)
    wqtmp = arWQ.sbuf_tensor("wqtmp", [128, 2, D], BF16)

    OVLSZ = max(arAT.off, arP23.off, arP4.off, arWQ.off)
    ovl = st.enter_context(nc.sbuf_tensor("ovl", [128, OVLSZ], mybir.dt.uint8))
    ovl_base = nc.lookup_mloc(ovl).addr
    for mloc, off in pend:
        mloc.addr = ovl_base + off

    cbias = SB("cbias", [128, 4], F32)
    with nc.Block("initc") as iblk:
        @iblk.gpsimd
        def _(g):
            g.memset(cbias[:, 0:1], -OFF)
            g.memset(cbias[:, 1:2], OFF - 0.5)
            g.memset(cbias[:, 2:3], EPS_LN)
            g.memset(cbias[:, 3:4], OFF + 0.5)
    nc.const_aps.register(F32, -OFF, cbias[:, 0:1])
    nc.const_aps.register(F32, OFF - 0.5, cbias[:, 1:2])
    nc.const_aps.register(F32, EPS_LN, cbias[:, 2:3])
    nc.const_aps.register(F32, OFF + 0.5, cbias[:, 3:4])

    ctr_h, qctr_h = {}, {}
    for e in ENGS:
        ctr_h[e] = st.enter_context(nc.semaphore(f"c_{e}"))
        qctr_h[e] = [st.enter_context(nc.semaphore(f"q_{e}{i}")) for i in range(4)]

    P = Prog()

    _dc = {}
    for _nm, _dst, _src in (("m1", mask1, mask1_in), ("m2", mask2, mask2_in),
                            ("idf", idf, idf_in), ("idb", idb, idb_in),
                            ("idh", idh, idh_in), ("edge", edge, edge_in),
                            ("bout", b_out, b_out_in), ("proj", proj_f, proj_in)):
        def f_dc(sy, sem, count_only=False, _dst=_dst, _src=_src):
            if count_only:
                return 1
            sy.dma_start(_dst[:], _src.ap()).then_inc(sem, 16)
        _dc[_nm] = P.dma("sy", [], f_dc, f"dc_{_nm}")
    def f_dcx(sy, sem, count_only=False):
        if count_only:
            return 1
        sy.dma_start(xhalo6[0:3, :], x_in.ap()[0:3, :]).then_inc(sem, 16)
    _dc["xh6a"] = P.dma("sy", [], f_dcx, "dc_xh6a")
    dc_m1, dc_m2, dc_idf, dc_idb = _dc["m1"], _dc["m2"], _dc["idf"], _dc["idb"]
    dc_idh, dc_edge, dc_bout, dc_proj = _dc["idh"], _dc["edge"], _dc["bout"], _dc["proj"]
    dc_xh6a = _dc["xh6a"]

    def f_init0(v):
        v.memset(ones_col[:], 1.0)
        v.drain()
        v.memset(ones_row[:], 1.0)
        v.drain()
        v.memset(Vt[:, :, :, 128:129], 1.0)
        v.drain()
        return v.memset(S2m[:, :, :], 0.0)
    init0 = P.step("dv", [], f_init0, "init0")

    xinld = [None] * NT
    xst = [None] * NT; xsq = [None] * NT; xfin = [None] * NT
    xapply = [None] * NT; xcast = [None] * NT
    t1 = [None] * NT; t1e = [None] * NT

    for t in range(NT):
        def f_xin(sy, sem, count_only=False, t=t):
            if count_only:
                return 1
            sy.dma_start(xin[:, t % 2, :],
                         x_in.ap()[3 + 128 * t:3 + 128 * (t + 1), :]).then_inc(sem, 16)
        deps = [xst[t - 2], xapply[t - 2], xcast[t - 2]] if t >= 2 else []
        xinld[t] = P.dma("sy" if t % 2 == 0 else "se", deps, f_xin, f"xin{t}")

        L = LNS["l1"][t % 2]

        def f_xst(v, t=t, L=L):
            v.bn_stats(L["bn"][:, 0:6], xin[:, t % 2, 0:512])
            v.bn_stats(L["bn"][:, 6:12], xin[:, t % 2, 512:1024])
            v.drain()
            v.bn_aggr(L["ag"][:], L["bn"][:])
            v.drain()
            return v.tensor_scalar(L["rs"][:], L["ag"][:, 1:2], EPS_LN, None, ALU.add)
        xst[t] = P.step("dv", [xinld[t], xapply[t - 2] if t >= 2 else None],
                        f_xst, f"xst{t}")

        def f_xsq(s, L=L):
            return s.activation(L["rs"][:], L["rs"][:], AF.Sqrt, scale=1.0)
        xsq[t] = P.step("se", [xst[t]], f_xsq, f"xsq{t}")

        def f_xfin(v, L=L):
            v.reciprocal(L["rs"][:], L["rs"][:])
            v.drain()
            return v.scalar_tensor_tensor(L["nm"][:], L["ag"][:, 0:1], -1.0, L["rs"][:],
                                          ALU.mult, ALU.mult)
        xfin[t] = P.step("dv", [xsq[t]], f_xfin, f"xfin{t}")

        def f_xapply(s, t=t, L=L):
            return s.activation(xlntm[:, :], xin[:, t % 2, :], AF.Identity,
                                bias=L["nm"][:], scale=L["rs"][:])
        xapply[t] = P.step("se", [xfin[t], t1[t - 1] if t >= 1 else None],
                           f_xapply, f"xap{t}")

        def f_xcast(v, t=t):
            return v.tensor_copy(x_tm[:, t, :], xin[:, t % 2, :])
        xcast[t] = P.step("dv", [xinld[t]], f_xcast, f"xcast{t}")

        def f_t1(te, t=t):
            r = None
            for k in range(KD):
                half, q = k // 4, k % 4
                r = te.transpose(pB[:, 1024 * half + 128 * q:1024 * half + 128 * q + 128],
                                 xlntm[:, 128 * k:128 * k + 128], idf[:])
            return r
        t1[t] = P.step("pe", [xapply[t], dc_idf, t1e[t - 1] if t >= 1 else None],
                       f_t1, f"t1_{t}")

        def f_t1e(s, t=t):
            r = None
            for half in range(2):
                src = pB[:, 1024 * half:1024 * half + 512]
                dst = x_ln[:, 4 * half:4 * half + 4, 3 + 128 * t:3 + 128 * t + 128]
                r = s.activation(dst, src.rearrange("p (k n) -> p k n", k=4),
                                 AF.Copy, scale=1.0)
                if half == 0:
                    s.drain()
            return r
        t1e[t] = P.step("se", [t1[t]], f_t1e, f"t1e{t}")

    def f_xin6(sy, sem, count_only=False):
        if count_only:
            return 1
        sy.dma_start(xhalo6[3:6, :], x_in.ap()[2051:2054, :]).then_inc(sem, 16)
    xin6 = P.dma("sy", [], f_xin6, "xin6")

    L6 = dict(bn=SB("l6bn", [6, 12], F32), ag=SB("l6ag", [6, 2], F32),
              rs=SB("l6rs", [6, 1], F32), nm=SB("l6nm", [6, 1], F32))

    def f_xst6(v):
        v.bn_stats(L6["bn"][0:6, 0:6], xhalo6[:, 0:512])
        v.bn_stats(L6["bn"][0:6, 6:12], xhalo6[:, 512:1024])
        v.drain()
        v.bn_aggr(L6["ag"][0:6, :], L6["bn"][0:6, :])
        v.drain()
        return v.tensor_scalar(L6["rs"][0:6, :], L6["ag"][0:6, 1:2], EPS_LN, None,
                               ALU.add)
    xst6 = P.step("dv", [dc_xh6a, xin6], f_xst6, "xst6")

    def f_xsq6(s):
        return s.activation(L6["rs"][0:6, :], L6["rs"][0:6, :], AF.Sqrt, scale=1.0)
    xsq6 = P.step("se", [xst6], f_xsq6, "xsq6")

    def f_xfin6(v):
        v.reciprocal(L6["rs"][0:6, :], L6["rs"][0:6, :])
        v.drain()
        return v.scalar_tensor_tensor(L6["nm"][0:6, :], L6["ag"][0:6, 0:1], -1.0,
                                      L6["rs"][0:6, :], ALU.mult, ALU.mult)
    xfin6 = P.step("dv", [xsq6], f_xfin6, "xfin6")

    def f_xap6(s):
        return s.activation(xhalo6[:, :], xhalo6[:, :], AF.Identity,
                            bias=L6["nm"][0:6, :], scale=L6["rs"][0:6, :])
    xap6 = P.step("se", [xfin6], f_xap6, "xap6")

    def f_t16(te):
        r = None
        for k in range(KD):
            r = te.transpose(pA[:, 1024 + 12 * k:1024 + 12 * k + 6],
                             xhalo6[:, 128 * k:128 * k + 128], idf[0:6, 0:6])
        return r
    t16 = P.step("pe", [xap6, dc_idf], f_t16, "t16")

    def f_t16e(s):
        r = None
        for k in range(KD):
            s.activation(x_ln[:, k, 0:3], pA[:, 1024 + 12 * k:1024 + 12 * k + 3],
                         AF.Copy, scale=1.0)
            s.drain()
            r = s.activation(x_ln[:, k, 2051:2054], pA[:, 1024 + 12 * k + 3:1024 + 12 * k + 6],
                             AF.Copy, scale=1.0)
            if k < KD - 1:
                s.drain()
        return r
    t16e = P.step("se", [t16], f_t16e, "t16e")

    def f_pjc(v):
        return v.tensor_copy(proj_b[:], proj_f[:])
    pjc = P.step("dv", [dc_proj], f_pjc, "pjc")

    wq_ld = [None] * 16; wq_cast = [None] * 16
    wqmm = [None] * 16; wqev = [None] * 16
    for i in range(16):
        def f_wqld(gp, sem, count_only=False, i=i):
            if count_only:
                return 1
            gp.dma_start(stage32[:, i % 2, :],
                         wqkT_in.ap()[128 * i:128 * (i + 1), :]).then_inc(sem, 16)
        wq_ld[i] = P.dma("gp", [wq_cast[i - 2]] if i >= 2 else [], f_wqld, f"wqld{i}")

        def f_wqcast(v, i=i):
            return v.tensor_copy(wqk_b[:, i % 2, :], stage32[:, i % 2, :])
        wq_cast[i] = P.step("dv", [wq_ld[i], wqmm[i - 2] if i >= 2 else None],
                            f_wqcast, f"wqc{i}")

        def f_wqmm(te, i=i):
            bank = pA[:, 512 * (i % 2):512 * (i % 2) + 512]
            r = None
            for kd in range(KD):
                r = te.matmul(bank[:, 64 * kd:64 * kd + 64],
                              wqk_b[:, i % 2, 128 * kd:128 * kd + 128],
                              proj_b[:], start=True, stop=True)
            return r
        wqmm[i] = P.step("pe", [wq_cast[i], pjc, wqev[i - 2] if i >= 2 else None],
                         f_wqmm, f"wqmm{i}")

        def f_wqev(s, i=i):
            qk, h = divmod(i, 8)
            wt = wqp if qk == 0 else wkp
            bank = pA[:, 512 * (i % 2):512 * (i % 2) + 512]
            p, hh = divmod(h, 2)
            dst = wt[:, 0:KD, 128 * p + 64 * hh:128 * p + 64 * hh + 64]
            return s.activation(dst, bank.rearrange("p (k f) -> p k f", k=KD),
                                AF.Copy, scale=1.0)
        wqev[i] = P.step("se", [wqmm[i]], f_wqev, f"wqev{i}")

    wv_ld = [None] * 8; wv_cast = [None] * 8
    for j in range(8):
        def f_wvld(gp, sem, count_only=False, j=j):
            if count_only:
                return 1
            gp.dma_start(stage32[:, j % 2, :],
                         w_v_in.ap()[128 * j:128 * (j + 1), :]).then_inc(sem, 16)
        prev = wq_cast[14 + j] if 14 + j < 16 else wv_cast[j - 2]
        wv_ld[j] = P.dma("gp", [prev], f_wvld, f"wvld{j}")

        def f_wvcast(v, j=j):
            return v.tensor_copy(wv_sb[:, j, :], stage32[:, j % 2, :])
        wv_cast[j] = P.step("dv", [wv_ld[j]], f_wvcast, f"wvc{j}")

    wab_ld = [None] * 64
    wab_abs = [None] * 64

    def make_wab(k):
        which = "w2" if k < 32 else "w1"
        j = k if k < 32 else k - 32

        def f_ld(sy, sem, count_only=False, which=which, j=j, k=k):
            if count_only:
                return 1
            if which == "w2":
                s_ = w2T_in.ap()[128 * j:128 * (j + 1), :]
            else:
                dk, q = divmod(j, 4)
                s_ = w1T_in.ap()[128 * dk:128 * (dk + 1), 1024 * q:1024 * (q + 1)]
            sy.dma_start(stage32[:, k % 2, :], s_).then_inc(sem, 16)
        if k == 0:
            pdep = [wv_cast[6]]
        elif k == 1:
            pdep = [wv_cast[7]]
        else:
            pdep = [wab_abs[k - 2]]
        wab_ld[k] = P.dma("sy" if k % 2 == 0 else "se", pdep, f_ld, f"wabl{k}")

        def f_abs(s, which=which, j=j, k=k):
            acc = w2acc if which == "w2" else w1acc
            return s.activation(stage32[:, k % 2, :], stage32[:, k % 2, :], AF.Abs,
                                scale=1.0, accum_out=acc[:, j:j + 1])
        wab_abs[k] = P.step("se", [wab_ld[k]], f_abs, f"waba{k}")


    phimm = [None] * 40; phimin = [None] * 40
    phiexp = [None] * 40; phic = [None] * 40
    idx = 0
    for qk in range(2):
        for p in range(PAIRS):
            for ntile in range(5):
                n0 = 512 * ntile
                w = PHI_NT[ntile]

                def f_phimm(te, qk=qk, p=p, n0=n0, w=w, idx=idx):
                    wt = wqp if qk == 0 else wkp
                    bank = pA[:, 512 * (idx % 4):512 * (idx % 4) + w]
                    r = None
                    for kd in range(KD):
                        r = te.matmul(bank, wt[:, kd, 128 * p:128 * p + 128],
                                      x_ln[:, kd, n0:n0 + w],
                                      start=(kd == 0), stop=(kd == KD - 1))
                    return r
                deps = [t1e[NT - 1], t16e, wqev[7 if qk == 0 else 15]]
                if idx >= 4:
                    deps.append(phic[idx - 4])
                elif idx < 2:
                    deps.append(wqev[14 + idx])
                phimm[idx] = P.step("pe", deps, f_phimm, f"phimm{idx}")

                def f_phimin(v, w=w, idx=idx):
                    bank = pA[:, 512 * (idx % 4):512 * (idx % 4) + w]
                    return v.tensor_scalar(elu_t[:, 0, 0:w], bank, 0.0, None, ALU.min)
                phimin[idx] = P.step("dv", [phimm[idx],
                                            phiexp[idx - 1] if idx >= 1 else None],
                                     f_phimin, f"phimin{idx}")

                def f_phiexp(s, w=w):
                    return s.activation(elu_t[:, 1, 0:w], elu_t[:, 0, 0:w], AF.Exp,
                                       scale=1.0)
                phiexp[idx] = P.step("se", [phimin[idx],
                                            phic[idx - 1] if idx >= 1 else None],
                                     f_phiexp, f"phiexp{idx}")

                def f_phic(v, qk=qk, p=p, n0=n0, w=w, idx=idx):
                    dstT = qphi if qk == 0 else kphi
                    bank = pA[:, 512 * (idx % 4):512 * (idx % 4) + w]
                    return v.scalar_tensor_tensor(dstT[:, p, n0:n0 + w], bank, 1.0,
                                                  elu_t[:, 1, 0:w], ALU.add, ALU.max)
                phic[idx] = P.step("dv", [phiexp[idx]], f_phic, f"phic{idx}")
                for _k in range((idx * 64) // 40, ((idx + 1) * 64) // 40):
                    make_wab(_k)
                idx += 1

    def f_kedge(v):
        r = None
        for p in range(PAIRS):
            v.tensor_tensor(kphi[:, p, 0:3], kphi[:, p, 0:3], edge[:, 0:3], ALU.mult)
            v.drain()
            r = v.tensor_tensor(kphi[:, p, 2051:2054], kphi[:, p, 2051:2054],
                                edge[:, 3:6], ALU.mult)
            if p < PAIRS - 1:
                v.drain()
        return r
    kedge = P.step("dv", [phic[39], dc_edge], f_kedge, "kedge")
    def f_wcl(v):
        v.reduce_sum(wtot2[:], w2acc[:], axis=AX.X)
        v.drain()
        return v.reduce_sum(wtot1[:], w1acc[:], axis=AX.X)
    wcl = P.step("dv", [wab_abs[-1]], f_wcl, "wcl")

    def f_wsm(te):
        te.matmul(pA[0:1, 0:1], wtot2[:], ones_col[:], start=True, stop=True)
        return te.matmul(pA[0:1, 64:65], wtot1[:], ones_col[:], start=True, stop=True)
    wsm = P.step("pe", [wcl, init0, phic[36]], f_wsm, "wsm")

    def f_wsv(v):
        v.tensor_scalar(sv2[:], pA[0:1, 0:1], 1.0 / (HID * D), EPS_Q, ALU.mult, ALU.max)
        v.drain()
        return v.tensor_scalar(sv1[:], pA[0:1, 64:65], 1.0 / (HID * D), EPS_Q,
                               ALU.mult, ALU.max)
    wsv = P.step("dv", [wsm], f_wsv, "wsv")

    def f_wsb(te):
        te.matmul(pA[:, 32:33], ones_row[:], sv2[:], start=True, stop=True)
        return te.matmul(pA[:, 96:97], ones_row[:], sv1[:], start=True, stop=True)
    wsb = P.step("pe", [wsv], f_wsb, "wsb")

    def f_wsf(v):
        v.tensor_copy(s2_t[:], pA[:, 32:33])
        v.drain()
        v.tensor_copy(s1_t[:], pA[:, 96:97])
        v.drain()
        v.tensor_scalar(rh2_t[:], pA[:, 32:33], 0.5, None, ALU.mult)
        v.drain()
        v.reciprocal(rh2_t[:], rh2_t[:])
        v.drain()
        v.tensor_scalar(rh1_t[:], pA[:, 96:97], 0.5, None, ALU.mult)
        v.drain()
        return v.reciprocal(rh1_t[:], rh1_t[:])
    wsf = P.step("dv", [wsb], f_wsf, "wsf")

    NWQ = 64
    wql = [None] * NWQ; wqle = [None] * NWQ; wqge = [None] * NWQ
    wqsub = [None] * NWQ; wqs = [None] * NWQ

    def make_quant(k):
        which = "w2" if k < 32 else "w1"
        j = k if k < 32 else k - 32

        parts = []
        for qt in range(4):
            def f_ld(sy, sem, count_only=False, which=which, j=j, k=k, qt=qt):
                if count_only:
                    return 1
                r0, r1 = 32 * qt, 32 * qt + 32
                if which == "w2":
                    s_ = w2T_in.ap()[128 * j + r0:128 * j + r1, :]
                else:
                    dk, q = divmod(j, 4)
                    s_ = w1T_in.ap()[128 * dk + r0:128 * dk + r1,
                                     1024 * q:1024 * (q + 1)]
                sy.dma_start(w23s[r0:r1, k % 3, :], s_).then_inc(sem, 16)
            if k < 3:
                pdep = [wab_abs[-1], t1[NT - 1]]
            elif k in (24, 25, 26):
                pdep = [wo_cast[[6, 7, 5][k - 24]]]
            else:
                pdep = [wqge[k - 3]]
            parts.append(P.dma("sy", pdep, f_ld, f"wql{k}_{qt}"))
        wql[k] = parts

        def f_le(v, which=which, k=k):
            rh = rh1_t if which == "w1" else rh2_t
            return v.tensor_scalar(wqtmp[:, k % 2, :], w23s[:, k % 3, :], rh[:], -1.0,
                                   ALU.mult, ALU.is_le)
        ldeps = list(wql[k]) + [wsf]
        if k < 2:
            ldeps.append(phic[39])
        else:
            ldeps.append(wqsub[k - 2])
        wqle[k] = P.step("dv", ldeps, f_le, f"wqle{k}")

        def f_ge(v, which=which, k=k):
            rh = rh1_t if which == "w1" else rh2_t
            return v.tensor_scalar(wqt_o[:, k % 3, :], w23s[:, k % 3, :], rh[:], 1.0,
                                   ALU.mult, ALU.is_ge)
        gdeps = [wqle[k]]
        if k < 2:
            gdeps.append(wqmm[15])
        else:
            gdeps.append(wqsub[k - 2])
        if k >= 3:
            gdeps += wqs[k - 3]
        wqge[k] = P.step("dv", gdeps, f_ge, f"wqge{k}")

        def f_sub(gp, k=k):
            return gp.tensor_tensor(wqt_o[:, k % 3, :], wqt_o[:, k % 3, :],
                                    wqtmp[:, k % 2, :], ALU.subtract)
        wqsub[k] = P.step("gp", [wqge[k]], f_sub, f"wqsub{k}")

        sparts = []
        for sh in range(2):
            def f_st(s, sem, count_only=False, which=which, j=j, k=k, sh=sh):
                if count_only:
                    return 1
                r0, r1 = 64 * sh, 64 * sh + 64
                if which == "w2":
                    dst = wq2_scr.ap()[128 * j + r0:128 * j + r1, :]
                else:
                    dk, q = divmod(j, 4)
                    dst = wq1_scr.ap()[128 * dk + r0:128 * dk + r1,
                                       1024 * q:1024 * (q + 1)]
                s.dma_start(dst, wqt_o[r0:r1, k % 3, :]).then_inc(sem, 16)
            sparts.append(P.dma("se", [wqsub[k]], f_st, f"wqs{k}_{sh}"))
        wqs[k] = sparts

    vmm = [None] * NVT; vev = [None] * NVT
    NH = NT * PAIRS
    smm = [None] * NH; sev = [None] * NH
    amm = [None] * NH; aev = [None] * NH

    def make_v(i):
        rows = 128 if i < 16 else 6

        def f_vmm(te, i=i, rows=rows):
            half = pA[:, 0:1024]
            r = None
            for kd in range(KD):
                xs = x_ln[:, kd, 128 * i:128 * i + rows]
                r = te.matmul(half[0:rows, 0:512], xs, wv_sb[:, kd, 0:512],
                              start=(kd == 0), stop=(kd == KD - 1))
            for kd in range(KD):
                xs = x_ln[:, kd, 128 * i:128 * i + rows]
                r = te.matmul(half[0:rows, 512:1024], xs, wv_sb[:, kd, 512:1024],
                              start=(kd == 0), stop=(kd == KD - 1))
            return r
        deps = [t1e[NT - 1], t16e, wv_cast[7], wsf]
        if i == 0:
            deps += [phic[36], phic[37]]
        else:
            deps.append(vev[i - 1])
        vmm[i] = P.step("pe", deps, f_vmm, f"vmm{i}")

        def f_vev(s, i=i, rows=rows):
            half = pA[:, 0:1024]
            return s.activation(Vt[0:rows, i % 3, 0:8, 0:128],
                                half[0:rows, :].rearrange("p (h n) -> p h n", h=8),
                                AF.Copy, scale=1.0)
        vdeps = [vmm[i], init0]
        if i >= 3:
            vdeps.append(aev[4 * (i - 3) + 3])
        vev[i] = P.step("se", vdeps, f_vev, f"vev{i}")

    for i in range(3):
        make_v(i)

    def make_amm(j):
        c2, p2 = divmod(j, PAIRS)
        h0, h1 = 2 * p2, 2 * p2 + 1

        def f_amm(te, c=c2, i=j, h0=h0, h1=h1):
            sb = i % 2
            ab = i % 2
            bank = pB[:, 1024 + 512 * ab:1024 + 512 * ab + 512]
            te.matmul(bank[:, 0:129], S1m[:, sb, 0:128], Vt[:, c % 3, h0, :],
                      start=True, stop=False)
            te.matmul(bank[:, 0:129], S2m[0:6, sb, 0:128],
                      Vt[0:6, (c + 1) % 3, h0, :], start=False, stop=True)
            te.matmul(bank[:, 256:385], S1m[:, sb, 128:256], Vt[:, c % 3, h1, :],
                      start=True, stop=False)
            return te.matmul(bank[:, 256:385], S2m[0:6, sb, 128:256],
                             Vt[0:6, (c + 1) % 3, h1, :], start=False, stop=True)
        adeps = [sev[j], vev[c2], vev[c2 + 1]]
        if j >= 2:
            adeps.append(aev[j - 2])
        amm[j] = P.step("pe", adeps, f_amm, f"amm{j}")

        def f_aev(v, c=c2, i=j, h0=h0, h1=h1):
            ab = i % 2
            bank = pB[:, 1024 + 512 * ab:1024 + 512 * ab + 512]
            v.tensor_scalar(za[:], bank[:, 128:129], EPS_Z, None, ALU.add)
            v.tensor_scalar(zb[:], bank[:, 384:385], EPS_Z, None, ALU.add)
            v.drain()
            v.reciprocal(za[:], za[:])
            v.reciprocal(zb[:], zb[:])
            v.drain()
            v.tensor_scalar(A_tm[:, c, 128 * h0:128 * h0 + 128], bank[:, 0:128],
                            za[:], None, ALU.mult)
            return v.tensor_scalar(A_tm[:, c, 128 * h1:128 * h1 + 128],
                                   bank[:, 256:384], zb[:], None, ALU.mult)
        aev[j] = P.step("dv", [amm[j]], f_aev, f"aev{j}")

    for c in range(NT):
        for p in range(PAIRS):
            i = c * PAIRS + p

            def f_smm(te, c=c, p=p, i=i):
                base = pB[:, 0:1024] if (i % 2 == 0) else pA[:, 1024:2048]
                bankA, bankB = base[:, 0:512], base[:, 512:1024]
                qap0 = qphi[0:64, p, 128 * c + 3:128 * c + 131]
                qap1 = qphi[64:128, p, 128 * c + 3:128 * c + 131]
                te.matmul(bankA[:, 0:128], kphi[0:64, p, 128 * c:128 * c + 128],
                          qap0, start=True, stop=True)
                te.matmul(bankB[:, 0:128], kphi[64:128, p, 128 * c:128 * c + 128],
                          qap1, start=True, stop=True)
                te.matmul(bankA[0:6, 128:256],
                          kphi[0:64, p, 128 * c + 128:128 * c + 134], qap0,
                          start=True, stop=True)
                return te.matmul(bankB[0:6, 128:256],
                                 kphi[64:128, p, 128 * c + 128:128 * c + 134], qap1,
                                 start=True, stop=True)
            sdeps = [kedge]
            if i == 1:
                sdeps += [phic[38], phic[39], wsf]
            if i >= 2:
                sdeps.append(sev[i - 2])
            smm[i] = P.step("pe", sdeps, f_smm, f"smm{i}")

            def f_sev(v, i=i):
                sb = i % 2
                base = pB[:, 0:1024] if (i % 2 == 0) else pA[:, 1024:2048]
                bankA, bankB = base[:, 0:512], base[:, 512:1024]
                v.tensor_tensor(S1m[:, sb, 0:128], bankA[:, 0:128], mask1[:], ALU.mult)
                v.tensor_tensor(S1m[:, sb, 128:256], bankB[:, 0:128], mask1[:], ALU.mult)
                v.tensor_tensor(S2m[0:6, sb, 122:128], bankA[0:6, 250:256],
                                mask2[0:6, 122:128], ALU.mult)
                return v.tensor_tensor(S2m[0:6, sb, 250:256], bankB[0:6, 250:256],
                                       mask2[0:6, 122:128], ALU.mult)
            svdeps = [smm[i], init0, dc_m1, dc_m2]
            if i >= 2:
                svdeps.append(amm[i - 2])
            sev[i] = P.step("dv", svdeps, f_sev, f"sev{i}")

            if i >= 1:
                make_amm(i - 1)
            if p == 0 and c >= 1 and c + 2 <= NVT - 1:
                make_v(c + 2)

        if c >= 2:
            for _k in range(((c - 2) * 24) // 14, ((c - 1) * 24) // 14):
                make_quant(_k)

    make_amm(NT * PAIRS - 1)


    wo_ld = [None] * KD; wo_cast = [None] * KD
    for j in range(KD):
        def f_wold(gp, sem, count_only=False, j=j):
            if count_only:
                return 1
            gp.dma_start(w23s[:, j % 3, :],
                         w_out_in.ap()[128 * j:128 * (j + 1), :]).then_inc(sem, 16)
        deps = [wqge[[21, 22, 23][j]]] if j < 3 else [wo_cast[j - 3]]
        wo_ld[j] = P.dma("gp", deps, f_wold, f"wold{j}")

        def f_wocast(v, j=j):
            return v.tensor_copy(wo_sb[:, j, :], w23s[:, j % 3, :])
        wo_cast[j] = P.step("dv", [wo_ld[j]] + ([smm[NH - 1]] if j < 2 else []),
                            f_wocast, f"woc{j}")

    t2 = [None] * NT; t2e = [None] * NT
    for c in range(NT):
        def f_t2(te, c=c):
            bank = pA[:, 512 * (c % 2):512 * (c % 2) + 512].bitcast(BF16)
            r = None
            for k in range(KD):
                r = te.transpose(bank[:, 128 * k:128 * k + 128],
                                 A_tm[:, c, 128 * k:128 * k + 128], idb[:])
            return r
        deps = [aev[4 * c + 3], vev[NVT - 1], dc_idb]
        if c >= 2:
            deps.append(t2e[c - 2])
        t2[c] = P.step("pe", deps, f_t2, f"t2_{c}")

        def f_t2e(s, c=c):
            bank = pA[:, 512 * (c % 2):512 * (c % 2) + 512].bitcast(BF16)
            dst = A_fm[:, :, 128 * c:128 * c + 128]
            return s.activation(dst, bank.rearrange("p (k n) -> p k n", k=KD),
                                AF.Copy, scale=1.0)
        t2e[c] = P.step("se", [t2[c]], f_t2e, f"t2e{c}")

    womm = {}; woev = {}
    for m in range(8):
        for n in range(4):
            def f_womm(te, m=m, n=n):
                bank = pB[:, 512 * n:512 * n + 512]
                r = None
                for kd in range(KD):
                    r = te.matmul(bank, wo_sb[:, kd, 128 * m:128 * m + 128],
                                  A_fm[:, kd, 512 * n:512 * n + 512],
                                  start=(kd == 0), stop=(kd == KD - 1))
                return r
            deps = [wo_cast[KD - 1], t2e[4 * n + 3], aev[NH - 1]]
            if m >= 1:
                deps.append(woev[(m - 1, n)])
            womm[(m, n)] = P.step("pe", deps, f_womm, f"womm{m}_{n}")

            def f_woev(s, m=m, n=n):
                bank = pB[:, 512 * n:512 * n + 512]
                return s.activation(opfm[:, m, 512 * n:512 * n + 512], bank, AF.Identity,
                                    bias=b_out[:, m:m + 1], scale=1.0)
            woev[(m, n)] = P.step("se", [womm[(m, n)], dc_bout], f_woev, f"woev{m}_{n}")
            _idx = 4 * m + n
            make_quant(24 + _idx if _idx < 8 else 32 + _idx - 8)

    t3 = [None] * NT; y_ev = [None] * NT
    ln2s = [None] * NT; ln2q = [None] * NT; ln2f = [None] * NT
    ln2a = [None] * NT; res2 = [None] * NT
    ln3s = [None] * NT; ln3q = [None] * NT; ln3f = [None] * NT; ln3a = [None] * NT
    qx_mx = [None] * NT; qx_sc = [None] * NT; qx_u = [None] * NT; qx_c = [None] * NT
    qx_b = [None] * NT; qx_r = [None] * NT
    for c in range(NT):
        def f_t3(te, c=c):
            bank = pA[:, 512 * (c % 2):512 * (c % 2) + 512].bitcast(BF16)
            r = None
            for m in range(8):
                r = te.transpose(bank[:, 128 * m:128 * m + 128],
                                 opfm[:, m, 128 * c:128 * c + 128], idb[:])
            return r
        deps = [woev[(7, c // 4)], t2e[NT - 1]]
        if c >= 2:
            deps.append(y_ev[c - 2])
        t3[c] = P.step("pe", deps, f_t3, f"t3_{c}")

        def f_yev(s, c=c):
            bank = pA[:, 512 * (c % 2):512 * (c % 2) + 512].bitcast(BF16)
            return s.activation(y_sb[:, c % 2, :], bank, AF.Copy, scale=1.0)
        ydeps = [t3[c]]
        if c >= 2:
            ydeps += [ln2a[c - 2], ln2s[c - 2]]
        y_ev[c] = P.step("se", ydeps, f_yev, f"yev{c}")

        L = LNS["l2"][c % 2]

        def f_ln2s(v, c=c, L=L):
            v.bn_stats(L["bn"][:, 0:6], y_sb[:, c % 2, 0:512])
            v.bn_stats(L["bn"][:, 6:12], y_sb[:, c % 2, 512:1024])
            v.drain()
            v.bn_aggr(L["ag"][:], L["bn"][:])
            v.drain()
            return v.tensor_scalar(L["rs"][:], L["ag"][:, 1:2], EPS_LN, None, ALU.add)
        ln2s[c] = P.step("dv", [y_ev[c]], f_ln2s, f"ln2s{c}")

        def f_ln2q(s, L=L):
            return s.activation(L["rs"][:], L["rs"][:], AF.Sqrt, scale=1.0)
        ln2q[c] = P.step("se", [ln2s[c]], f_ln2q, f"ln2q{c}")

        def f_ln2f(v, L=L):
            v.reciprocal(L["rs"][:], L["rs"][:])
            v.drain()
            return v.scalar_tensor_tensor(L["nm"][:], L["ag"][:, 0:1], -1.0, L["rs"][:],
                                          ALU.mult, ALU.mult)
        ln2f[c] = P.step("dv", [ln2q[c]], f_ln2f, f"ln2f{c}")

        def f_ln2a(s, c=c, L=L):
            return s.activation(yn_sb[:, c % 2, :], y_sb[:, c % 2, :], AF.Identity,
                                bias=L["nm"][:], scale=L["rs"][:])
        adeps = [ln2f[c]]
        if c >= 2:
            adeps.append(res2[c - 2])
        ln2a[c] = P.step("se", adeps, f_ln2a, f"ln2a{c}")

        def f_res2(gp, c=c):
            return gp.tensor_tensor(x_tm[:, c, :], x_tm[:, c, :], yn_sb[:, c % 2, :],
                                    ALU.add)
        res2[c] = P.step("gp", [ln2a[c], xcast[c]], f_res2, f"res2_{c}")
        if c < 8:
            make_quant(56 + c)


    wq1_ld = [None] * KD
    for k in range(KD):
        def f_wq1ld(gp, sem, count_only=False, k=k):
            if count_only:
                return 1
            gp.dma_start(wq1g[:, k, :],
                         wq1_scr.ap()[128 * k:128 * (k + 1), :]).then_inc(sem, 16)
        wq1_ld[k] = P.dma("gp", sum([wqs[32 + 4 * k + q] for q in range(4)], [])
                          + [womm[(7, 3)], t3[NT - 1]], f_wq1ld, f"wq1l{k}")

    t4 = [None] * NT; t4e = [None] * NT
    f1 = {}; f1g = {}
    qh_mx2 = {}; qh_sc = [None] * NT
    qh_u = {}; qh_c = {}; qh_b = {}; qh_r = {}
    t5 = {}; t5e = {}
    f2 = {}; f2ev = {}; f2evA = {}
    wq2_ld = {}
    t6 = {}; t6e = {}
    ln4s = [None] * NT; ln4q = [None] * NT; ln4f = [None] * NT; ln4a = [None] * NT
    outadd = [None] * NT; yst = [None] * NT

    def emit_lnqx(c):
        L3 = LNS["l3"][c % 2]
        Q = QXs[c % 2]

        def f_ln3s(v, c=c, L=L3):
            v.bn_stats(L["bn"][:, 0:6], x_tm[:, c, 0:512])
            v.bn_stats(L["bn"][:, 6:12], x_tm[:, c, 512:1024])
            v.drain()
            v.bn_aggr(L["ag"][:], L["bn"][:])
            v.drain()
            return v.tensor_scalar(L["rs"][:], L["ag"][:, 1:2], EPS_LN, None,
                                   ALU.add)
        deps = [res2[c]]
        if c >= 2:
            deps.append(ln3a[c - 2])
        ln3s[c] = P.step("dv", deps, f_ln3s, f"ln3s{c}")

        def f_ln3q(s, L=L3):
            return s.activation(L["rs"][:], L["rs"][:], AF.Sqrt, scale=1.0)
        ln3q[c] = P.step("se", [ln3s[c]], f_ln3q, f"ln3q{c}")

        def f_ln3f(v, L=L3):
            v.reciprocal(L["rs"][:], L["rs"][:])
            v.drain()
            return v.scalar_tensor_tensor(L["nm"][:], L["ag"][:, 0:1], -1.0,
                                          L["rs"][:], ALU.mult, ALU.mult)
        ln3f[c] = P.step("dv", [ln3q[c]], f_ln3f, f"ln3f{c}")

        def f_ln3a(s, c=c, L=L3):
            return s.activation(x3_sb[:, :], x_tm[:, c, :], AF.Identity,
                                bias=L["nm"][:], scale=L["rs"][:])
        adeps = [ln3f[c]]
        if c == 0:
            adeps.append(t2[NT - 1])
        if c >= 1:
            adeps.append(qx_b[c - 1])
        ln3a[c] = P.step("se", adeps, f_ln3a, f"ln3a{c}")

        def f_qxmx(v, c=c, Q=Q):
            return v.reduce_max(Q["mx"][:], x3_sb[:, :], axis=AX.X,
                                apply_absolute_value=True)
        qx_mx[c] = P.step("dv", [ln3a[c]], f_qxmx, f"qxmx{c}")

        def f_qxsc(v, Q=Q):
            v.tensor_scalar(Q["isc"][:], Q["mx"][:], EPS_Q, 1.0 / 127.0,
                            ALU.max, ALU.mult)
            v.drain()
            v.reciprocal(Q["sc"][:], Q["isc"][:])
            return v.tensor_tensor(Q["fac"][:], s1_t[:], Q["isc"][:], ALU.mult)
        qx_sc[c] = P.step("dv", [qx_mx[c], wsf], f_qxsc, f"qxsc{c}")

        def f_qxu(v, c=c, Q=Q):
            return v.tensor_scalar(x3_sb[:, :], x3_sb[:, :],
                                   Q["sc"][:], OFF + 0.5, ALU.mult, ALU.add)
        cdeps = [qx_sc[c]]
        if c == 0:
            cdeps.append(t2[NT - 1])
        if c >= 1:
            cdeps.append(qx_r[c - 1])
        qx_u[c] = qx_c[c] = P.step("dv", cdeps, f_qxu, f"qxu{c}")

        def f_qxr(v, c=c):
            v.tensor_scalar(x3_sb[:, :].bitcast(I32), x3_sb[:, :].bitcast(I32),
                            -8192, None, ALU.bitwise_and)
            v.drain()
            return v.tensor_copy(Ax_tm[:, c % 2, :], x3_sb[:, :])
        rdeps = [qx_u[c]]
        if c >= 2:
            rdeps.append(t4[c - 2])
        qx_r[c] = qx_b[c] = P.step("dv", rdeps, f_qxr, f"qxr{c}")

    def emit_t4(c):
        g, cc = divmod(c, GC)

        def f_t4(te, c=c):
            bank = pB[:, 0:512].bitcast(FP16)
            r = None
            for k in range(KD):
                r = te.transpose(bank[:, 128 * k:128 * k + 128],
                                 Ax_tm[:, c % 2, 128 * k:128 * k + 128], idh[:])
            return r
        deps = [qx_r[c], dc_idh]
        if c == 0:
            deps.append(woev[(7, 3)])
        else:
            deps.append(t4e[c - 1])
        if c % GC == 0 and g >= 1:
            deps.append(f2ev[g - 1])
        t4[c] = P.step("pe", deps, f_t4, f"t4_{c}")

        def f_t4e(s, c=c):
            src = pB[:, 0:512].bitcast(FP16).rearrange("p (k n) -> p k n", k=KD)
            return s.activation(Axf[:, c % 2, :, :], src, AF.Identity, bias=-OFF,
                                scale=1.0)
        edeps = [t4[c]]
        if c == 0:
            edeps.append(t2[NT - 1])
        elif c >= 2:
            edeps.append(f1[(c - 2, 3)])
        t4e[c] = P.step("se", edeps, f_t4e, f"t4e{c}")

    def emit_f1(c):
        g, cc = divmod(c, GC)
        for quarter in range(4):
            def f_f1(te, c=c, quarter=quarter):
                base = 1024 * (quarter % 2)
                r = None
                for kd in range(KD):
                    for n in range(2):
                        off = 1024 * quarter + 512 * n
                        r = te.matmul(pA[:, base + 512 * n:base + 512 * n + 512],
                                      Axf[:, c % 2, kd, :],
                                      wq1g[:, kd, off:off + 512],
                                      start=(kd == 0), stop=(kd == KD - 1))
                return r
            deps = [t4e[c]] + list(wq1_ld)
            if quarter >= 2:
                deps.append(f1g[(c, quarter - 2)])
            elif c >= 1:
                deps.append(f1g[(c - 1, quarter + 2)])
            else:
                deps.append(y_ev[NT - 1])
            if quarter < 2 and c % GC == 0 and c >= 4:
                deps.append(f2evA[c // GC - 1])
            f1[(c, quarter)] = P.step("pe", deps, f_f1, f"f1_{c}_{quarter}")

            Q = QXs[c % 2]

            def f_f1g(s, c=c, quarter=quarter, Q=Q):
                base = 1024 * (quarter % 2)
                return s.activation(h_sb[:, c % 2, 1024 * quarter:1024 * quarter + 1024],
                                    pA[:, base:base + 1024], AF.Gelu, bias=0.0,
                                    scale=Q["fac"][:])
            gdeps = [f1[(c, quarter)], qx_sc[c]]
            if c < 2:
                gdeps.append(womm[(7, 3)])
            if c >= 2:
                gdeps += [qh_u[(c - 2, quarter)], qh_mx2[(c - 2, quarter // 2)]]
            f1g[(c, quarter)] = P.step("se", gdeps, f_f1g, f"f1g{c}_{quarter}")



    def emit_qhx(c):
        QH = QHs[c % 2]
        for half in range(2):
            def f_qhmx(v, c=c, half=half, QH=QH):
                return v.reduce_max(QH["mx"][:, half:half + 1],
                                    h_sb[:, c % 2, 2048 * half:2048 * half + 2048],
                                    axis=AX.X, apply_absolute_value=True)
            qh_mx2[(c, half)] = P.step("dv", [f1g[(c, 2 * half)],
                                              f1g[(c, 2 * half + 1)]], f_qhmx,
                                       f"qhmx{c}_{half}")

        def f_qhsc(v, c=c, QH=QH):
            v.tensor_tensor(QH["mx"][:, 0:1], QH["mx"][:, 0:1], QH["mx"][:, 1:2],
                            ALU.max)
            v.drain()
            v.tensor_scalar(QH["isc"][:], QH["mx"][:, 0:1], EPS_Q, 1.0 / 127.0,
                            ALU.max, ALU.mult)
            v.drain()
            v.reciprocal(QH["sc"][:], QH["isc"][:])
            return v.tensor_tensor(fac2buf[:, c % 8:c % 8 + 1], s2_t[:],
                                   QH["isc"][:], ALU.mult)
        qh_sc[c] = P.step("dv", [qh_mx2[(c, 0)], qh_mx2[(c, 1)], wsf],
                          f_qhsc, f"qhsc{c}")

    def emit_strips(c):
        QH = QHs[c % 2]
        for q in range(4):
            def f_qhu(v, c=c, q=q, QH=QH):
                return v.tensor_scalar(u05h[:, q % 2, :],
                                       h_sb[:, c % 2, 1024 * q:1024 * q + 1024],
                                       QH["sc"][:], OFF + 0.5, ALU.mult, ALU.add)
            ueng = "dv"
            deps = [qh_sc[c]]
            if q >= 2:
                deps.append(qh_r[(c, q - 2)])
            elif c >= 1:
                deps.append(qh_r[(c - 1, q + 2)])
            elif q < 2:
                deps.append(t2[NT - 1])
            qh_u[(c, q)] = qh_c[(c, q)] = P.step(ueng, deps, f_qhu, f"qhu{c}_{q}")

            def f_qhr(v, c=c, q=q):
                v.tensor_scalar(u05h[:, q % 2, :].bitcast(I32),
                                u05h[:, q % 2, :].bitcast(I32),
                                -8192, None, ALU.bitwise_and)
                v.drain()
                return v.tensor_copy(Ah_tm[:, 1024 * q:1024 * q + 1024],
                                     u05h[:, q % 2, :])
            deps = [qh_u[(c, q)]]
            if c >= 1:
                deps.append(t5[(c - 1, 3)])
            elif q == 0:
                deps.append(womm[(7, 3)])
            qh_r[(c, q)] = qh_b[(c, q)] = P.step("dv", deps, f_qhr, f"qhr{c}_{q}")

    def emit_tfb(c):
        g, cc = divmod(c, GC)
        for j in range(4):
            def f_t5(te, c=c, j=j):
                bank = pB[:, 512 + 512 * (j % 2):1024 + 512 * (j % 2)].bitcast(FP16)
                r = None
                for b in range(8):
                    k = 8 * j + b
                    r = te.transpose(bank[:, 128 * b:128 * b + 128],
                                     Ah_tm[:, 128 * k:128 * k + 128], idh[:])
                return r
            deps = [qh_r[(c, j)]]
            if j == 0:
                if c >= 1:
                    deps.append(t5e[(c - 1, 2)])
            elif j == 1:
                if c >= 1:
                    deps.append(t5e[(c - 1, 3)])
            else:
                deps.append(t5e[(c, j - 2)])
            t5[(c, j)] = P.step("pe", deps, f_t5, f"t5_{c}_{j}")

            def f_t5e(v, c=c, cc=cc, j=j):
                src = pB[:, 512 + 512 * (j % 2):1024 + 512 * (j % 2)].bitcast(FP16)
                dst = Ahfm[:, 8 * j:8 * j + 8, 128 * cc:128 * cc + 128]
                return v.tensor_scalar(dst, src.rearrange("p (k n) -> p k n", k=8),
                                       -OFF, None, ALU.add)
            deps = [t5[(c, j)]]
            if c == 0:
                deps += [womm[(7, 3)], ln2a[NT - 1], ln2s[NT - 1], res2[NT - 1]]
            if cc == 0 and g >= 1:
                deps.append(f2[(g - 1, KH - 1)])
            t5e[(c, j)] = P.step("dv", deps, f_t5e, f"t5e{c}_{j}")

    def emit_f2(g):
        for k in range(KH):
            deps = list(wqs[k])
            if g == 0:
                deps.append(t2[NT - 1])
            if g == 0 and k < 4:
                deps += sum(wqs[NWQ - 4:], [])
            if k >= 4:
                deps.append(f2[(g, k - 4)])
            elif g >= 1:
                deps.append(f2[(g - 1, KH - 4 + k)])
            parts = []
            for qt in range(4):
                def f_wq2ld(gp, sem, count_only=False, g=g, k=k, qt=qt):
                    if count_only:
                        return 1
                    gp.dma_start(wq2r[32 * qt:32 * qt + 32, k % 4, :],
                                 wq2_scr.ap()[128 * k + 32 * qt:128 * k + 32 * qt + 32,
                                              :]).then_inc(sem, 16)
                parts.append(P.dma("gp" if qt % 2 == 0 else "sy", deps, f_wq2ld,
                                   f"wq2l{g}_{k}_{qt}"))
            wq2_ld[(g, k)] = parts

            def f_f2(te, g=g, k=k):
                r = None
                for m in range(8):
                    bank = (pA[:, 512 * m:512 * m + 512] if m < 4
                            else pB[:, 512 * (m - 4):512 * (m - 4) + 512])
                    r = te.matmul(bank, wq2r[:, k % 4, 128 * m:128 * m + 128],
                                  Ahfm[:, k, :], start=(k == 0), stop=(k == KH - 1))
                return r
            deps = list(wq2_ld[(g, k)])
            if k % 8 == 0:
                deps += [t5e[(4 * g + cc2, k // 8)] for cc2 in range(GC)]
            if k == 0:
                deps += [f1g[(4 * g + 3, 3)], t5e[(4 * g + 3, 2)],
                         t5e[(4 * g + 3, 3)]]
                if g >= 1:
                    deps.append(t6e[(g - 1, 3)])
            f2[(g, k)] = P.step("pe", deps, f_f2, f"f2_{g}_{k}")

        def f_f2ev(v, g=g):
            r = None
            for m in range(4, 8):
                r = v.tensor_copy(o2fm[:, m, :],
                                  pB[:, 512 * (m - 4):512 * (m - 4) + 512])
            return r
        fdeps = [f2[(g, KH - 1)]]
        if g == 0:
            fdeps.append(wqsub[NWQ - 1])
        f2ev[g] = P.step("dv", fdeps, f_f2ev, f"f2ev{g}")

        def f_f2evA(s, g=g):
            r = None
            for m in range(4):
                r = s.activation(o2fm[:, m, :], pA[:, 512 * m:512 * m + 512],
                                 AF.Copy, scale=1.0)
            return r
        f2evA[g] = P.step("se", list(fdeps), f_f2evA, f"f2evA{g}")

    def emit_tail(g, cc):
        c = 4 * g + cc

        def f_t6(te, g=g, cc=cc):
            bank = pB[:, 1536:2048].bitcast(BF16)
            r = None
            for m in range(8):
                r = te.transpose(bank[:, 128 * m:128 * m + 128],
                                 o2fm[:, m, 128 * cc:128 * cc + 128], idb[:])
            return r
        deps = [f2ev[g], f2evA[g]]
        if cc >= 1:
            deps.append(t6e[(g, cc - 1)])
        t6[(g, cc)] = P.step("pe", deps, f_t6, f"t6_{g}_{cc}")

        def f_t6e(v, g=g, cc=cc, c=c):
            bank = pB[:, 1536:2048].bitcast(BF16)
            return v.tensor_scalar(y2_sb[:, :], bank,
                                   fac2buf[:, c % 8:c % 8 + 1], None, ALU.mult)
        deps = [t6[(g, cc)], qh_sc[c]]
        if c == 0:
            deps.append(wqge[NWQ - 1])
        if c >= 1:
            deps += [ln4a[c - 1], ln4s[c - 1]]
        t6e[(g, cc)] = P.step("dv", deps, f_t6e, f"t6e{g}_{cc}")

        L = LNS["l4"][c % 2]

        def f_ln4s(v, c=c, L=L):
            v.bn_stats(L["bn"][:, 0:6], y2_sb[:, 0:512])
            v.bn_stats(L["bn"][:, 6:12], y2_sb[:, 512:1024])
            v.drain()
            v.bn_aggr(L["ag"][:], L["bn"][:])
            v.drain()
            return v.tensor_scalar(L["rs"][:], L["ag"][:, 1:2], EPS_LN, None,
                                   ALU.add)
        ln4s[c] = P.step("dv", [t6e[(g, cc)]], f_ln4s, f"ln4s{c}")

        def f_ln4q(s, L=L):
            return s.activation(L["rs"][:], L["rs"][:], AF.Sqrt, scale=1.0)
        ln4q[c] = P.step("se", [ln4s[c]], f_ln4q, f"ln4q{c}")

        def f_ln4f(v, L=L):
            v.reciprocal(L["rs"][:], L["rs"][:])
            v.drain()
            return v.scalar_tensor_tensor(L["nm"][:], L["ag"][:, 0:1], -1.0,
                                          L["rs"][:], ALU.mult, ALU.mult)
        ln4f[c] = P.step("dv", [ln4q[c]], f_ln4f, f"ln4f{c}")

        def f_ln4a(s, c=c, L=L):
            return s.activation(yn4_sb[:, :], y2_sb[:, :], AF.Identity,
                                bias=L["nm"][:], scale=L["rs"][:])
        adeps = [ln4f[c]]
        if c == 0:
            adeps.append(wqge[NWQ - 1])
        if c >= 1:
            adeps.append(outadd[c - 1])
        ln4a[c] = P.step("se", adeps, f_ln4a, f"ln4a{c}")

        def f_out(gp, c=c):
            return gp.tensor_tensor(yout_sb[:, :], x_tm[:, c, :],
                                    yn4_sb[:, :], ALU.add)
        odeps = [ln4a[c], ln3s[c]]
        if c == 0:
            odeps += [wqge[NWQ - 1]] + sum(wqs[NWQ - 4:], [])
        if c >= 1:
            odeps.append(yst[c - 1])
        outadd[c] = P.step("gp", odeps, f_out, f"out{c}")

        def f_yst(sy, sem, count_only=False, c=c):
            if count_only:
                return 1
            sy.dma_start(y_out.ap()[128 * c:128 * (c + 1), :],
                         yout_sb[:, :]).then_inc(sem, 16)
        yst[c] = P.dma("sy", [outadd[c]], f_yst, f"yst{c}")

    for g in range(NG):
        if g >= 1:
            emit_strips(4 * g - 1)
            emit_tfb(4 * g - 1)
            emit_f2(g - 1)
        emit_lnqx(4 * g)
        emit_lnqx(4 * g + 1)
        emit_t4(4 * g)
        for cc in range(GC):
            c = 4 * g + cc
            emit_f1(c)
            if cc < GC - 1:
                emit_t4(c + 1)
            if cc >= 1:
                emit_strips(c - 1)
                emit_tfb(c - 1)
            emit_qhx(c)
            if cc + 2 < GC:
                emit_lnqx(c + 2)
            if g >= 1:
                emit_tail(g - 1, cc)
    emit_strips(NT - 1)
    emit_tfb(NT - 1)
    emit_f2(NG - 1)
    for cc in range(GC):
        emit_tail(NG - 1, cc)

    def f_fin(sy):
        return sy.nop()
    P.step("sy", [yst[c] for c in range(NT)], f_fin, "fin")

    with nc.Block("mk") as blk:
        P.emit(blk, ctr_h, qctr_h)

    st.close()
    return nc



def kernel(L0_attn_x, L0_attn_w_qkv, L0_attn_proj, L0_attn_w_out, L0_attn_b_out,
           L0_ffn_w1, L0_ffn_w2):
    x = np.asarray(L0_attn_x, dtype=np.float32)
    w_qkv = np.asarray(L0_attn_w_qkv, dtype=np.float32)
    proj = np.ascontiguousarray(np.asarray(L0_attn_proj, dtype=np.float32))
    w_out = np.ascontiguousarray(np.asarray(L0_attn_w_out, dtype=np.float32))
    b_out = np.asarray(L0_attn_b_out, dtype=np.float32)
    w1 = np.asarray(L0_ffn_w1, dtype=np.float32)
    w2 = np.asarray(L0_ffn_w2, dtype=np.float32)

    wqkT = np.ascontiguousarray(w_qkv[:, :2048].T)
    w_v = np.ascontiguousarray(w_qkv[:, 2048:])
    b_out_r = np.ascontiguousarray(b_out.reshape(8, 128).T)
    w1T = np.ascontiguousarray(w1.T)
    w2T = np.ascontiguousarray(w2.T)

    jj, rr = np.meshgrid(np.arange(128), np.arange(128), indexing="ij")
    mask1 = ((jj - rr >= 0) & (jj - rr <= 6)).astype(np.float32)
    j2, r2 = np.meshgrid(np.arange(6), np.arange(128), indexing="ij")
    mask2 = (r2 - j2 >= 122).astype(np.float32)
    identf = np.eye(128, dtype=np.float32)
    identb = np.eye(128).astype(ml_dtypes.bfloat16)
    identh = np.eye(128, dtype=np.float16)

    in_maps = []
    for core in range(8):
        b = core // 4
        s0 = (core % 4) * TOK
        xh = np.zeros((HALO, D), dtype=np.float32)
        lo, hi = s0 - 3, s0 + TOK + 3
        src_lo, src_hi = max(lo, 0), min(hi, 8192)
        xh[src_lo - lo:src_hi - lo, :] = x[b, src_lo:src_hi, :]
        edge = np.ones((128, 6), dtype=np.float32)
        if s0 == 0:
            edge[:, 0:3] = 0.0
        if s0 + TOK == 8192:
            edge[:, 3:6] = 0.0
        in_maps.append({
            "x_halo": xh, "wqkT": wqkT, "w_v": w_v, "proj": proj,
            "w_out": w_out, "b_out_r": b_out_r, "w1T": w1T, "w2T": w2T,
            "mask1": mask1, "mask2": mask2, "identf": identf, "identb": identb,
            "identh": identh, "edgem": edge,
        })

    nc = build_nc()
    res = run_bass_kernel_spmd(nc, in_maps, core_ids=list(range(8)))
    out = np.concatenate([np.asarray(res.results[i]["y"]) for i in range(8)], axis=0)
    return out.reshape(2, 8192, D)


# version 30
# speedup vs baseline: 1.0582x; 1.0334x over previous
import numpy as np
import ml_dtypes

import concourse.bass as bass
import concourse.mybir as mybir
from concourse.bass_utils import run_bass_kernel_spmd

try:
    from concourse.compiler_utils import get_compiler_flags, set_compiler_flags
    set_compiler_flags([f.replace("--enable-ldw-opt=false", "--enable-ldw-opt=true")
                        for f in get_compiler_flags()])
except Exception:
    pass

F32 = mybir.dt.float32
BF16 = mybir.dt.bfloat16
FP16 = mybir.dt.float16
I16 = mybir.dt.int16
I32 = mybir.dt.int32
AF = mybir.ActivationFunctionType
ALU = mybir.AluOpType
AX = mybir.AxisListType

D = 1024
KD = 8
HEADS = 8
PAIRS = 4
TOK = 2048
NT = 16
HALO = 2054
NVT = 17
HID = 4096
KH = 32
NG = 4
GC = 4
EPS_LN = 1e-5
EPS_Z = 1e-8
EPS_Q = 1e-5
OFF = 1536.0
PHI_NT = [512, 512, 512, 512, 6]

ENGS = ("pe", "dv", "se", "gp", "sy")


NQS = 4


class Step:
    __slots__ = ("eng", "idx", "cord", "deps", "fn", "dma", "qord", "name")

    def __init__(self, eng, idx, deps, fn, dma, name):
        self.eng = eng
        self.idx = idx
        self.cord = None
        self.deps = deps
        self.fn = fn
        self.dma = dma
        self.qord = None
        self.name = name


class Prog:
    def __init__(self):
        self.steps = {e: [] for e in ENGS}

    def step(self, eng, deps, fn, name=""):
        s = Step(eng, len(self.steps[eng]), [d for d in deps if d is not None],
                 fn, False, name)
        s.cord = sum(1 for t in self.steps[eng] if not t.dma)
        self.steps[eng].append(s)
        return s

    def dma(self, eng, deps, fn, name=""):
        s = Step(eng, len(self.steps[eng]), [d for d in deps if d is not None],
                 fn, True, name)
        s.qord = sum(1 for t in self.steps[eng] if t.dma)
        self.steps[eng].append(s)
        return s

    def emit(self, blk, ctrs, qctrs):
        engobjs = {"pe": blk.tensor, "dv": blk.vector, "se": blk.scalar,
                   "gp": blk.gpsimd, "sy": blk.sync}
        dmalist = {e: [s for s in self.steps[e] if s.dma] for e in ENGS}

        def make_body(eng_name):
            def body(eng):
                hw = {}

                def wait_for(d):
                    if d.dma:
                        key = ("q", d.eng, d.qord % NQS)
                        val = 16 * (d.qord // NQS + 1)
                        sem = qctrs[d.eng][d.qord % NQS]
                    else:
                        key, val, sem = ("c", d.eng), d.cord + 1, ctrs[d.eng]
                    if hw.get(key, 0) < val:
                        hw[key] = val
                        eng.wait_ge(sem, val)

                for s in self.steps[eng_name]:
                    for d in s.deps:
                        if d.eng == eng_name and not d.dma and not s.dma:
                            continue
                        wait_for(d)
                    if s.dma:
                        if s.qord >= NQS:
                            wait_for(dmalist[eng_name][s.qord - NQS])
                        sem = qctrs[eng_name][s.qord % NQS]
                        s.fn(eng, sem, count_only=False)
                    else:
                        if eng_name in ("dv", "se", "gp"):
                            eng.drain()
                        last = s.fn(eng)
                        assert last is not None, f"step {s.name} returned None"
                        last.then_inc(ctrs[eng_name], 1)
            return body

        for e in ENGS:
            engobjs[e](make_body(e))


def build_nc():
    nc = bass.Bass("TRN2", target_bir_lowering=False)

    x_in = nc.dram_tensor("x_halo", [HALO, D], F32, kind="ExternalInput")
    wqkT_in = nc.dram_tensor("wqkT", [2 * D, D], F32, kind="ExternalInput")
    w_v_in = nc.dram_tensor("w_v", [D, D], F32, kind="ExternalInput")
    proj_in = nc.dram_tensor("proj", [128, 64], F32, kind="ExternalInput")
    w_out_in = nc.dram_tensor("w_out", [D, D], F32, kind="ExternalInput")
    b_out_in = nc.dram_tensor("b_out_r", [128, 8], F32, kind="ExternalInput")
    w1T_in = nc.dram_tensor("w1T", [D, HID], F32, kind="ExternalInput")
    w2T_in = nc.dram_tensor("w2T", [HID, D], F32, kind="ExternalInput")
    mask1_in = nc.dram_tensor("mask1", [128, 128], F32, kind="ExternalInput")
    mask2_in = nc.dram_tensor("mask2", [6, 128], F32, kind="ExternalInput")
    idf_in = nc.dram_tensor("identf", [128, 128], F32, kind="ExternalInput")
    idb_in = nc.dram_tensor("identb", [128, 128], BF16, kind="ExternalInput")
    idh_in = nc.dram_tensor("identh", [128, 128], FP16, kind="ExternalInput")
    edge_in = nc.dram_tensor("edgem", [128, 6], F32, kind="ExternalInput")
    y_out = nc.dram_tensor("y", [TOK, D], F32, kind="ExternalOutput")
    wq1_scr = nc.dram_tensor("wq1_scr", [D, HID], BF16, kind="Internal")
    wq2_scr = nc.dram_tensor("wq2_scr", [HID, D], BF16, kind="Internal")

    from contextlib import ExitStack
    st = ExitStack()
    SB = lambda n, s, d: st.enter_context(nc.sbuf_tensor(n, s, d))

    x_tm = SB("x_tm", [128, NT, D], BF16)
    mask1 = SB("m1", [128, 128], F32)
    mask2 = SB("m2", [6, 128], F32)
    idf = SB("idf", [128, 128], F32)
    idb = SB("idb", [128, 128], BF16)
    idh = SB("idh", [128, 128], FP16)
    edge = SB("edge", [128, 6], F32)
    b_out = SB("b_out", [128, 8], F32)
    proj_f = SB("proj_f", [128, 64], F32)
    proj_b = SB("proj_b", [128, 64], BF16)
    ones_col = SB("ones_col", [128, 1], F32)
    ones_row = SB("ones_row", [1, 128], F32)
    LNS = {}
    for lname in ("l1", "l2", "l3", "l4"):
        LNS[lname] = [dict(bn=SB(f"{lname}bn{i}", [128, 12], F32),
                           ag=SB(f"{lname}ag{i}", [128, 2], F32),
                           rs=SB(f"{lname}rs{i}", [128, 1], F32),
                           nm=SB(f"{lname}nm{i}", [128, 1], F32)) for i in range(2)]
    QXs = [dict(mx=SB(f"qxm{i}", [128, 1], F32), isc=SB(f"qxi{i}", [128, 1], F32),
                sc=SB(f"qxs{i}", [128, 1], F32), fac=SB(f"qxf{i}", [128, 1], F32))
           for i in range(2)]
    QHs = [dict(mx=SB(f"qhm{i}", [128, 2], F32), isc=SB(f"qhi{i}", [128, 1], F32),
                sc=SB(f"qhs{i}", [128, 1], F32)) for i in range(2)]
    fac2buf = SB("fac2buf", [128, 8], F32)
    za = SB("za", [128, 1], F32)
    zb = SB("zb", [128, 1], F32)
    w1acc = SB("w1acc", [128, 32], F32)
    w2acc = SB("w2acc", [128, 32], F32)
    wtot2 = SB("wtot2", [128, 1], F32)
    wtot1 = SB("wtot1", [128, 1], F32)
    sv1 = SB("sv1", [1, 1], F32)
    sv2 = SB("sv2", [1, 1], F32)
    s1_t = SB("s1_t", [128, 1], F32)
    s2_t = SB("s2_t", [128, 1], F32)
    rh1_t = SB("rh1_t", [128, 1], F32)
    rh2_t = SB("rh2_t", [128, 1], F32)

    pA = st.enter_context(nc.psum_tensor("pA", [128, 2048], F32))
    pB = st.enter_context(nc.psum_tensor("pB", [128, 2048], F32))

    class Ovl:
        def __init__(self, base_off):
            self.off = base_off

        def sbuf_tensor(self, name, shape, dtype):
            import functools, operator
            per = functools.reduce(operator.mul, shape[1:], 1) * mybir.dt.size(dtype)
            off = self.off
            self.off = (self.off + per + 31) & ~31
            hnd = nc.virtual_sbuf_tensor(name, shape, dtype)
            nc._tensor(hnd.name, hnd.shape, hnd.dtype, type="SB")
            mloc = nc.lookup_mloc(hnd)
            mloc.allocated = True
            pend.append((mloc, off))
            return hnd

    pend = []
    arAV = Ovl(0)
    A_tm = arAV.sbuf_tensor("A_tm", [128, NT, D], BF16)
    Vt = arAV.sbuf_tensor("Vt", [128, 3, HEADS, 129], BF16)
    S1m = arAV.sbuf_tensor("S1m", [128, 2, 256], BF16)
    S2m = arAV.sbuf_tensor("S2m", [6, 2, 256], BF16)
    AVSZ = (arAV.off + 31) & ~31

    arAT = Ovl(AVSZ)
    x_ln = arAT.sbuf_tensor("x_ln", [128, KD, HALO], BF16)
    qphi = arAT.sbuf_tensor("qphi", [128, PAIRS, HALO], BF16)
    kphi = arAT.sbuf_tensor("kphi", [128, PAIRS, HALO], BF16)
    wqp = arAT.sbuf_tensor("wqp", [128, KD, 512], BF16)
    wkp = arAT.sbuf_tensor("wkp", [128, KD, 512], BF16)
    wv_sb = arAT.sbuf_tensor("wv_sb", [128, KD, D], BF16)
    xin = arAT.sbuf_tensor("xin", [128, 2, D], F32)
    xhalo6 = arAT.sbuf_tensor("xh6", [6, D], F32)
    XLNTM_OFF = arAT.off
    xlntm = arAT.sbuf_tensor("xlntm", [128, D], F32)
    stage32 = arAT.sbuf_tensor("stage32", [128, 2, D], F32)
    wqk_b = arAT.sbuf_tensor("wqk_b", [128, 2, D], BF16)
    elu_t = arAT.sbuf_tensor("elu_t", [128, 2, 512], F32)

    arP23 = Ovl(AVSZ)
    A_fm = arP23.sbuf_tensor("A_fm", [128, KD, TOK], BF16)
    opfm = arP23.sbuf_tensor("opfm", [128, 8, TOK], BF16)
    wo_sb = arP23.sbuf_tensor("wo_sb", [128, KD, D], BF16)
    y_sb = arP23.sbuf_tensor("y_sb", [128, 2, D], F32)
    yn_sb = arP23.sbuf_tensor("yn_sb", [128, 2, D], F32)

    arP4 = Ovl(0)
    x3_sb = arP4.sbuf_tensor("x3_sb", [128, D], F32)
    Ax_tm = arP4.sbuf_tensor("Ax_tm", [128, 2, D], FP16)
    Axf = arP4.sbuf_tensor("Axf", [128, 2, KD, 128], BF16)
    u05h = arP4.sbuf_tensor("u05h", [128, 2, D], F32)
    wq2r = arP4.sbuf_tensor("wq2r", [128, 4, D], BF16)
    arP4.sbuf_tensor("pad4", [128, 2048], mybir.dt.uint8)
    h_sb = arP4.sbuf_tensor("h_sb", [128, 2, HID], BF16)
    Ah_tm = arP4.sbuf_tensor("Ah_tm", [128, HID], FP16)
    wq1g = arP4.sbuf_tensor("wq1g", [128, KD, HID], BF16)
    Ahfm = arP4.sbuf_tensor("Ahfm", [128, KH, 512], BF16)
    o2fm = arP4.sbuf_tensor("o2fm", [128, 8, 512], BF16)
    y2_sb = arP4.sbuf_tensor("y2_sb", [128, D], BF16)
    yn4_sb = arP4.sbuf_tensor("yn4_sb", [128, D], BF16)
    yout_sb = arP4.sbuf_tensor("yout", [128, D], F32)

    arWQ = Ovl(XLNTM_OFF)
    w23s = arWQ.sbuf_tensor("w23s", [128, 3, D], F32)
    wqt_o = arWQ.sbuf_tensor("wqt_o", [128, 3, D], BF16)
    wqtmp = arWQ.sbuf_tensor("wqtmp", [128, 2, D], BF16)

    OVLSZ = max(arAT.off, arP23.off, arP4.off, arWQ.off)
    ovl = st.enter_context(nc.sbuf_tensor("ovl", [128, OVLSZ], mybir.dt.uint8))
    ovl_base = nc.lookup_mloc(ovl).addr
    for mloc, off in pend:
        mloc.addr = ovl_base + off

    cbias = SB("cbias", [128, 4], F32)
    with nc.Block("initc") as iblk:
        @iblk.gpsimd
        def _(g):
            g.memset(cbias[:, 0:1], -OFF)
            g.memset(cbias[:, 1:2], OFF - 0.5)
            g.memset(cbias[:, 2:3], EPS_LN)
            g.memset(cbias[:, 3:4], OFF + 0.5)
    nc.const_aps.register(F32, -OFF, cbias[:, 0:1])
    nc.const_aps.register(F32, OFF - 0.5, cbias[:, 1:2])
    nc.const_aps.register(F32, EPS_LN, cbias[:, 2:3])
    nc.const_aps.register(F32, OFF + 0.5, cbias[:, 3:4])

    ctr_h, qctr_h = {}, {}
    for e in ENGS:
        ctr_h[e] = st.enter_context(nc.semaphore(f"c_{e}"))
        qctr_h[e] = [st.enter_context(nc.semaphore(f"q_{e}{i}")) for i in range(4)]

    P = Prog()

    _dc = {}
    for _nm, _dst, _src in (("m1", mask1, mask1_in), ("m2", mask2, mask2_in),
                            ("idf", idf, idf_in), ("idb", idb, idb_in),
                            ("idh", idh, idh_in), ("edge", edge, edge_in),
                            ("bout", b_out, b_out_in), ("proj", proj_f, proj_in)):
        def f_dc(sy, sem, count_only=False, _dst=_dst, _src=_src):
            if count_only:
                return 1
            sy.dma_start(_dst[:], _src.ap()).then_inc(sem, 16)
        _dc[_nm] = P.dma("sy", [], f_dc, f"dc_{_nm}")
    def f_dcx(sy, sem, count_only=False):
        if count_only:
            return 1
        sy.dma_start(xhalo6[0:3, :], x_in.ap()[0:3, :]).then_inc(sem, 16)
    _dc["xh6a"] = P.dma("sy", [], f_dcx, "dc_xh6a")
    dc_m1, dc_m2, dc_idf, dc_idb = _dc["m1"], _dc["m2"], _dc["idf"], _dc["idb"]
    dc_idh, dc_edge, dc_bout, dc_proj = _dc["idh"], _dc["edge"], _dc["bout"], _dc["proj"]
    dc_xh6a = _dc["xh6a"]

    def f_init0(v):
        v.memset(ones_col[:], 1.0)
        v.drain()
        v.memset(ones_row[:], 1.0)
        v.drain()
        v.memset(Vt[:, :, :, 128:129], 1.0)
        v.drain()
        return v.memset(S2m[:, :, :], 0.0)
    init0 = P.step("dv", [], f_init0, "init0")

    xinld = [None] * NT
    xst = [None] * NT; xsq = [None] * NT; xfin = [None] * NT
    xapply = [None] * NT; xcast = [None] * NT
    t1 = [None] * NT; t1e = [None] * NT

    for t in range(NT):
        def f_xin(sy, sem, count_only=False, t=t):
            if count_only:
                return 1
            sy.dma_start(xin[:, t % 2, :],
                         x_in.ap()[3 + 128 * t:3 + 128 * (t + 1), :]).then_inc(sem, 16)
        deps = [xst[t - 2], xapply[t - 2], xcast[t - 2]] if t >= 2 else []
        xinld[t] = P.dma("sy" if t % 2 == 0 else "se", deps, f_xin, f"xin{t}")

        L = LNS["l1"][t % 2]

        def f_xst(v, t=t, L=L):
            v.bn_stats(L["bn"][:, 0:6], xin[:, t % 2, 0:512])
            v.bn_stats(L["bn"][:, 6:12], xin[:, t % 2, 512:1024])
            v.drain()
            v.bn_aggr(L["ag"][:], L["bn"][:])
            v.drain()
            return v.tensor_scalar(L["rs"][:], L["ag"][:, 1:2], EPS_LN, None, ALU.add)
        xst[t] = P.step("dv", [xinld[t], xapply[t - 2] if t >= 2 else None],
                        f_xst, f"xst{t}")

        def f_xsq(s, L=L):
            return s.activation(L["rs"][:], L["rs"][:], AF.Sqrt, scale=1.0)
        xsq[t] = P.step("se", [xst[t]], f_xsq, f"xsq{t}")

        def f_xfin(v, L=L):
            v.reciprocal(L["rs"][:], L["rs"][:])
            v.drain()
            return v.scalar_tensor_tensor(L["nm"][:], L["ag"][:, 0:1], -1.0, L["rs"][:],
                                          ALU.mult, ALU.mult)
        xfin[t] = P.step("dv", [xsq[t]], f_xfin, f"xfin{t}")

        def f_xapply(s, t=t, L=L):
            return s.activation(xlntm[:, :], xin[:, t % 2, :], AF.Identity,
                                bias=L["nm"][:], scale=L["rs"][:])
        xapply[t] = P.step("se", [xfin[t], t1[t - 1] if t >= 1 else None],
                           f_xapply, f"xap{t}")

        def f_xcast(v, t=t):
            return v.tensor_copy(x_tm[:, t, :], xin[:, t % 2, :])
        xcast[t] = P.step("dv", [xinld[t]], f_xcast, f"xcast{t}")

        def f_t1(te, t=t):
            r = None
            for k in range(KD):
                half, q = k // 4, k % 4
                r = te.transpose(pB[:, 1024 * half + 128 * q:1024 * half + 128 * q + 128],
                                 xlntm[:, 128 * k:128 * k + 128], idf[:])
            return r
        t1[t] = P.step("pe", [xapply[t], dc_idf, t1e[t - 1] if t >= 1 else None],
                       f_t1, f"t1_{t}")

        def f_t1e(s, t=t):
            r = None
            for half in range(2):
                src = pB[:, 1024 * half:1024 * half + 512]
                dst = x_ln[:, 4 * half:4 * half + 4, 3 + 128 * t:3 + 128 * t + 128]
                r = s.activation(dst, src.rearrange("p (k n) -> p k n", k=4),
                                 AF.Copy, scale=1.0)
                if half == 0:
                    s.drain()
            return r
        t1e[t] = P.step("se", [t1[t]], f_t1e, f"t1e{t}")

    def f_xin6(sy, sem, count_only=False):
        if count_only:
            return 1
        sy.dma_start(xhalo6[3:6, :], x_in.ap()[2051:2054, :]).then_inc(sem, 16)
    xin6 = P.dma("sy", [], f_xin6, "xin6")

    L6 = dict(bn=SB("l6bn", [6, 12], F32), ag=SB("l6ag", [6, 2], F32),
              rs=SB("l6rs", [6, 1], F32), nm=SB("l6nm", [6, 1], F32))

    def f_xst6(v):
        v.bn_stats(L6["bn"][0:6, 0:6], xhalo6[:, 0:512])
        v.bn_stats(L6["bn"][0:6, 6:12], xhalo6[:, 512:1024])
        v.drain()
        v.bn_aggr(L6["ag"][0:6, :], L6["bn"][0:6, :])
        v.drain()
        return v.tensor_scalar(L6["rs"][0:6, :], L6["ag"][0:6, 1:2], EPS_LN, None,
                               ALU.add)
    xst6 = P.step("dv", [dc_xh6a, xin6], f_xst6, "xst6")

    def f_xsq6(s):
        return s.activation(L6["rs"][0:6, :], L6["rs"][0:6, :], AF.Sqrt, scale=1.0)
    xsq6 = P.step("se", [xst6], f_xsq6, "xsq6")

    def f_xfin6(v):
        v.reciprocal(L6["rs"][0:6, :], L6["rs"][0:6, :])
        v.drain()
        return v.scalar_tensor_tensor(L6["nm"][0:6, :], L6["ag"][0:6, 0:1], -1.0,
                                      L6["rs"][0:6, :], ALU.mult, ALU.mult)
    xfin6 = P.step("dv", [xsq6], f_xfin6, "xfin6")

    def f_xap6(s):
        return s.activation(xhalo6[:, :], xhalo6[:, :], AF.Identity,
                            bias=L6["nm"][0:6, :], scale=L6["rs"][0:6, :])
    xap6 = P.step("se", [xfin6], f_xap6, "xap6")

    def f_t16(te):
        r = None
        for k in range(KD):
            r = te.transpose(pA[:, 1024 + 12 * k:1024 + 12 * k + 6],
                             xhalo6[:, 128 * k:128 * k + 128], idf[0:6, 0:6])
        return r
    t16 = P.step("pe", [xap6, dc_idf], f_t16, "t16")

    def f_t16e(s):
        r = None
        for k in range(KD):
            s.activation(x_ln[:, k, 0:3], pA[:, 1024 + 12 * k:1024 + 12 * k + 3],
                         AF.Copy, scale=1.0)
            s.drain()
            r = s.activation(x_ln[:, k, 2051:2054], pA[:, 1024 + 12 * k + 3:1024 + 12 * k + 6],
                             AF.Copy, scale=1.0)
            if k < KD - 1:
                s.drain()
        return r
    t16e = P.step("se", [t16], f_t16e, "t16e")

    def f_pjc(v):
        return v.tensor_copy(proj_b[:], proj_f[:])
    pjc = P.step("dv", [dc_proj], f_pjc, "pjc")

    wq_ld = [None] * 16; wq_cast = [None] * 16
    wqmm = [None] * 16; wqev = [None] * 16
    for i in range(16):
        def f_wqld(gp, sem, count_only=False, i=i):
            if count_only:
                return 1
            gp.dma_start(stage32[:, i % 2, :],
                         wqkT_in.ap()[128 * i:128 * (i + 1), :]).then_inc(sem, 16)
        wq_ld[i] = P.dma("gp", [wq_cast[i - 2]] if i >= 2 else [], f_wqld, f"wqld{i}")

        def f_wqcast(v, i=i):
            return v.tensor_copy(wqk_b[:, i % 2, :], stage32[:, i % 2, :])
        wq_cast[i] = P.step("dv", [wq_ld[i], wqmm[i - 2] if i >= 2 else None],
                            f_wqcast, f"wqc{i}")

        def f_wqmm(te, i=i):
            bank = pA[:, 512 * (i % 2):512 * (i % 2) + 512]
            r = None
            for kd in range(KD):
                r = te.matmul(bank[:, 64 * kd:64 * kd + 64],
                              wqk_b[:, i % 2, 128 * kd:128 * kd + 128],
                              proj_b[:], start=True, stop=True)
            return r
        wqmm[i] = P.step("pe", [wq_cast[i], pjc, wqev[i - 2] if i >= 2 else None],
                         f_wqmm, f"wqmm{i}")

        def f_wqev(s, i=i):
            qk, h = divmod(i, 8)
            wt = wqp if qk == 0 else wkp
            bank = pA[:, 512 * (i % 2):512 * (i % 2) + 512]
            p, hh = divmod(h, 2)
            dst = wt[:, 0:KD, 128 * p + 64 * hh:128 * p + 64 * hh + 64]
            return s.activation(dst, bank.rearrange("p (k f) -> p k f", k=KD),
                                AF.Copy, scale=1.0)
        wqev[i] = P.step("se", [wqmm[i]], f_wqev, f"wqev{i}")

    wv_ld = [None] * 8; wv_cast = [None] * 8
    for j in range(8):
        def f_wvld(gp, sem, count_only=False, j=j):
            if count_only:
                return 1
            gp.dma_start(stage32[:, j % 2, :],
                         w_v_in.ap()[128 * j:128 * (j + 1), :]).then_inc(sem, 16)
        prev = wq_cast[14 + j] if 14 + j < 16 else wv_cast[j - 2]
        wv_ld[j] = P.dma("gp", [prev], f_wvld, f"wvld{j}")

        def f_wvcast(v, j=j):
            return v.tensor_copy(wv_sb[:, j, :], stage32[:, j % 2, :])
        wv_cast[j] = P.step("dv", [wv_ld[j]], f_wvcast, f"wvc{j}")

    wab_ld = [None] * 64
    wab_abs = [None] * 64

    def make_wab(k):
        which = "w2" if k < 32 else "w1"
        j = k if k < 32 else k - 32

        def f_ld(sy, sem, count_only=False, which=which, j=j, k=k):
            if count_only:
                return 1
            if which == "w2":
                s_ = w2T_in.ap()[128 * j:128 * (j + 1), :]
            else:
                dk, q = divmod(j, 4)
                s_ = w1T_in.ap()[128 * dk:128 * (dk + 1), 1024 * q:1024 * (q + 1)]
            sy.dma_start(stage32[:, k % 2, :], s_).then_inc(sem, 16)
        if k == 0:
            pdep = [wv_cast[6]]
        elif k == 1:
            pdep = [wv_cast[7]]
        else:
            pdep = [wab_abs[k - 2]]
        wab_ld[k] = P.dma("sy" if k % 2 == 0 else "se", pdep, f_ld, f"wabl{k}")

        def f_abs(s, which=which, j=j, k=k):
            acc = w2acc if which == "w2" else w1acc
            return s.activation(stage32[:, k % 2, :], stage32[:, k % 2, :], AF.Abs,
                                scale=1.0, accum_out=acc[:, j:j + 1])
        wab_abs[k] = P.step("se", [wab_ld[k]], f_abs, f"waba{k}")


    phimm = [None] * 40; phimin = [None] * 40
    phiexp = [None] * 40; phic = [None] * 40
    idx = 0
    for qk in range(2):
        for p in range(PAIRS):
            for ntile in range(5):
                n0 = 512 * ntile
                w = PHI_NT[ntile]

                def f_phimm(te, qk=qk, p=p, n0=n0, w=w, idx=idx):
                    wt = wqp if qk == 0 else wkp
                    bank = pA[:, 512 * (idx % 4):512 * (idx % 4) + w]
                    r = None
                    for kd in range(KD):
                        r = te.matmul(bank, wt[:, kd, 128 * p:128 * p + 128],
                                      x_ln[:, kd, n0:n0 + w],
                                      start=(kd == 0), stop=(kd == KD - 1))
                    return r
                deps = [t1e[NT - 1], t16e, wqev[7 if qk == 0 else 15]]
                if idx >= 4:
                    deps.append(phic[idx - 4])
                elif idx < 2:
                    deps.append(wqev[14 + idx])
                phimm[idx] = P.step("pe", deps, f_phimm, f"phimm{idx}")

                def f_phimin(v, w=w, idx=idx):
                    bank = pA[:, 512 * (idx % 4):512 * (idx % 4) + w]
                    return v.tensor_scalar(elu_t[:, 0, 0:w], bank, 0.0, None, ALU.min)
                phimin[idx] = P.step("dv", [phimm[idx],
                                            phiexp[idx - 1] if idx >= 1 else None],
                                     f_phimin, f"phimin{idx}")

                def f_phiexp(s, w=w):
                    return s.activation(elu_t[:, 1, 0:w], elu_t[:, 0, 0:w], AF.Exp,
                                       scale=1.0)
                phiexp[idx] = P.step("se", [phimin[idx],
                                            phic[idx - 1] if idx >= 1 else None],
                                     f_phiexp, f"phiexp{idx}")

                def f_phic(v, qk=qk, p=p, n0=n0, w=w, idx=idx):
                    dstT = qphi if qk == 0 else kphi
                    bank = pA[:, 512 * (idx % 4):512 * (idx % 4) + w]
                    return v.scalar_tensor_tensor(dstT[:, p, n0:n0 + w], bank, 1.0,
                                                  elu_t[:, 1, 0:w], ALU.add, ALU.max)
                phic[idx] = P.step("dv", [phiexp[idx]], f_phic, f"phic{idx}")
                for _k in range((idx * 64) // 40, ((idx + 1) * 64) // 40):
                    make_wab(_k)
                idx += 1

    def f_kedge(v):
        r = None
        for p in range(PAIRS):
            v.tensor_tensor(kphi[:, p, 0:3], kphi[:, p, 0:3], edge[:, 0:3], ALU.mult)
            v.drain()
            r = v.tensor_tensor(kphi[:, p, 2051:2054], kphi[:, p, 2051:2054],
                                edge[:, 3:6], ALU.mult)
            if p < PAIRS - 1:
                v.drain()
        return r
    kedge = P.step("dv", [phic[39], dc_edge], f_kedge, "kedge")
    def f_wcl(v):
        v.reduce_sum(wtot2[:], w2acc[:], axis=AX.X)
        v.drain()
        return v.reduce_sum(wtot1[:], w1acc[:], axis=AX.X)
    wcl = P.step("dv", [wab_abs[-1]], f_wcl, "wcl")

    def f_wsm(te):
        te.matmul(pA[0:1, 0:1], wtot2[:], ones_col[:], start=True, stop=True)
        return te.matmul(pA[0:1, 64:65], wtot1[:], ones_col[:], start=True, stop=True)
    wsm = P.step("pe", [wcl, init0, phic[36]], f_wsm, "wsm")

    def f_wsv(v):
        v.tensor_scalar(sv2[:], pA[0:1, 0:1], 1.0 / (HID * D), EPS_Q, ALU.mult, ALU.max)
        v.drain()
        return v.tensor_scalar(sv1[:], pA[0:1, 64:65], 1.0 / (HID * D), EPS_Q,
                               ALU.mult, ALU.max)
    wsv = P.step("dv", [wsm], f_wsv, "wsv")

    def f_wsb(te):
        te.matmul(pA[:, 32:33], ones_row[:], sv2[:], start=True, stop=True)
        return te.matmul(pA[:, 96:97], ones_row[:], sv1[:], start=True, stop=True)
    wsb = P.step("pe", [wsv], f_wsb, "wsb")

    def f_wsf(v):
        v.tensor_copy(s2_t[:], pA[:, 32:33])
        v.drain()
        v.tensor_copy(s1_t[:], pA[:, 96:97])
        v.drain()
        v.tensor_scalar(rh2_t[:], pA[:, 32:33], 0.5, None, ALU.mult)
        v.drain()
        v.reciprocal(rh2_t[:], rh2_t[:])
        v.drain()
        v.tensor_scalar(rh1_t[:], pA[:, 96:97], 0.5, None, ALU.mult)
        v.drain()
        return v.reciprocal(rh1_t[:], rh1_t[:])
    wsf = P.step("dv", [wsb], f_wsf, "wsf")

    NWQ = 64
    wql = [None] * NWQ; wqle = [None] * NWQ; wqge = [None] * NWQ
    wqsub = [None] * NWQ; wqs = [None] * NWQ

    def make_quant(k):
        which = "w2" if k < 32 else "w1"
        j = k if k < 32 else k - 32

        parts = []
        for qt in range(4):
            def f_ld(sy, sem, count_only=False, which=which, j=j, k=k, qt=qt):
                if count_only:
                    return 1
                r0, r1 = 32 * qt, 32 * qt + 32
                if which == "w2":
                    s_ = w2T_in.ap()[128 * j + r0:128 * j + r1, :]
                else:
                    dk, q = divmod(j, 4)
                    s_ = w1T_in.ap()[128 * dk + r0:128 * dk + r1,
                                     1024 * q:1024 * (q + 1)]
                sy.dma_start(w23s[r0:r1, k % 3, :], s_).then_inc(sem, 16)
            if k < 3:
                pdep = [wab_abs[-1], t1[NT - 1]]
            elif k in (32, 33, 34):
                pdep = [wo_cast[[5, 6, 7][k - 32]]]
            else:
                pdep = [wqge[k - 3]]
            parts.append(P.dma("sy", pdep, f_ld, f"wql{k}_{qt}"))
        wql[k] = parts

        def f_le(v, which=which, k=k):
            rh = rh1_t if which == "w1" else rh2_t
            return v.tensor_scalar(wqtmp[:, k % 2, :], w23s[:, k % 3, :], rh[:], -1.0,
                                   ALU.mult, ALU.is_le)
        ldeps = list(wql[k]) + [wsf]
        if k < 2:
            ldeps.append(phic[39])
        else:
            ldeps.append(wqsub[k - 2])
        wqle[k] = P.step("dv", ldeps, f_le, f"wqle{k}")

        def f_ge(v, which=which, k=k):
            rh = rh1_t if which == "w1" else rh2_t
            return v.tensor_scalar(wqt_o[:, k % 3, :], w23s[:, k % 3, :], rh[:], 1.0,
                                   ALU.mult, ALU.is_ge)
        gdeps = [wqle[k]]
        if k < 2:
            gdeps.append(wqmm[15])
        else:
            gdeps.append(wqsub[k - 2])
        if k >= 3:
            gdeps += wqs[k - 3]
        wqge[k] = P.step("dv", gdeps, f_ge, f"wqge{k}")

        def f_sub(gp, k=k):
            return gp.tensor_tensor(wqt_o[:, k % 3, :], wqt_o[:, k % 3, :],
                                    wqtmp[:, k % 2, :], ALU.subtract)
        wqsub[k] = P.step("gp", [wqge[k]], f_sub, f"wqsub{k}")

        sparts = []
        for sh in range(2):
            def f_st(s, sem, count_only=False, which=which, j=j, k=k, sh=sh):
                if count_only:
                    return 1
                r0, r1 = 64 * sh, 64 * sh + 64
                if which == "w2":
                    dst = wq2_scr.ap()[128 * j + r0:128 * j + r1, :]
                else:
                    dk, q = divmod(j, 4)
                    dst = wq1_scr.ap()[128 * dk + r0:128 * dk + r1,
                                       1024 * q:1024 * (q + 1)]
                s.dma_start(dst, wqt_o[r0:r1, k % 3, :]).then_inc(sem, 16)
            sparts.append(P.dma("se", [wqsub[k]], f_st, f"wqs{k}_{sh}"))
        wqs[k] = sparts

    vmm = [None] * NVT; vev = [None] * NVT
    NH = NT * PAIRS
    smm = [None] * NH; sev = [None] * NH
    amm = [None] * NH; aev = [None] * NH

    def make_v(i):
        rows = 128 if i < 16 else 6

        def f_vmm(te, i=i, rows=rows):
            half = pA[:, 0:1024]
            r = None
            for kd in range(KD):
                xs = x_ln[:, kd, 128 * i:128 * i + rows]
                r = te.matmul(half[0:rows, 0:512], xs, wv_sb[:, kd, 0:512],
                              start=(kd == 0), stop=(kd == KD - 1))
            for kd in range(KD):
                xs = x_ln[:, kd, 128 * i:128 * i + rows]
                r = te.matmul(half[0:rows, 512:1024], xs, wv_sb[:, kd, 512:1024],
                              start=(kd == 0), stop=(kd == KD - 1))
            return r
        deps = [t1e[NT - 1], t16e, wv_cast[7], wsf]
        if i == 0:
            deps += [phic[36], phic[37]]
        else:
            deps.append(vev[i - 1])
        vmm[i] = P.step("pe", deps, f_vmm, f"vmm{i}")

        def f_vev(s, i=i, rows=rows):
            half = pA[:, 0:1024]
            return s.activation(Vt[0:rows, i % 3, 0:8, 0:128],
                                half[0:rows, :].rearrange("p (h n) -> p h n", h=8),
                                AF.Copy, scale=1.0)
        vdeps = [vmm[i], init0]
        if i >= 3:
            vdeps.append(aev[4 * (i - 3) + 3])
        vev[i] = P.step("se", vdeps, f_vev, f"vev{i}")

    for i in range(3):
        make_v(i)

    def make_amm(j):
        c2, p2 = divmod(j, PAIRS)
        h0, h1 = 2 * p2, 2 * p2 + 1

        def f_amm(te, c=c2, i=j, h0=h0, h1=h1):
            sb = i % 2
            ab = i % 2
            bank = pB[:, 1024 + 512 * ab:1024 + 512 * ab + 512]
            te.matmul(bank[:, 0:129], S1m[:, sb, 0:128], Vt[:, c % 3, h0, :],
                      start=True, stop=False)
            te.matmul(bank[:, 0:129], S2m[0:6, sb, 0:128],
                      Vt[0:6, (c + 1) % 3, h0, :], start=False, stop=True)
            te.matmul(bank[:, 256:385], S1m[:, sb, 128:256], Vt[:, c % 3, h1, :],
                      start=True, stop=False)
            return te.matmul(bank[:, 256:385], S2m[0:6, sb, 128:256],
                             Vt[0:6, (c + 1) % 3, h1, :], start=False, stop=True)
        adeps = [sev[j], vev[c2], vev[c2 + 1]]
        if j >= 2:
            adeps.append(aev[j - 2])
        amm[j] = P.step("pe", adeps, f_amm, f"amm{j}")

        def f_aev(v, c=c2, i=j, h0=h0, h1=h1):
            ab = i % 2
            bank = pB[:, 1024 + 512 * ab:1024 + 512 * ab + 512]
            v.tensor_scalar(za[:], bank[:, 128:129], EPS_Z, None, ALU.add)
            v.tensor_scalar(zb[:], bank[:, 384:385], EPS_Z, None, ALU.add)
            v.drain()
            v.reciprocal(za[:], za[:])
            v.reciprocal(zb[:], zb[:])
            v.drain()
            v.tensor_scalar(A_tm[:, c, 128 * h0:128 * h0 + 128], bank[:, 0:128],
                            za[:], None, ALU.mult)
            return v.tensor_scalar(A_tm[:, c, 128 * h1:128 * h1 + 128],
                                   bank[:, 256:384], zb[:], None, ALU.mult)
        aev[j] = P.step("dv", [amm[j]], f_aev, f"aev{j}")

    for c in range(NT):
        for p in range(PAIRS):
            i = c * PAIRS + p

            def f_smm(te, c=c, p=p, i=i):
                base = pB[:, 0:1024] if (i % 2 == 0) else pA[:, 1024:2048]
                bankA, bankB = base[:, 0:512], base[:, 512:1024]
                qap0 = qphi[0:64, p, 128 * c + 3:128 * c + 131]
                qap1 = qphi[64:128, p, 128 * c + 3:128 * c + 131]
                te.matmul(bankA[:, 0:128], kphi[0:64, p, 128 * c:128 * c + 128],
                          qap0, start=True, stop=True)
                te.matmul(bankB[:, 0:128], kphi[64:128, p, 128 * c:128 * c + 128],
                          qap1, start=True, stop=True)
                te.matmul(bankA[0:6, 128:256],
                          kphi[0:64, p, 128 * c + 128:128 * c + 134], qap0,
                          start=True, stop=True)
                return te.matmul(bankB[0:6, 128:256],
                                 kphi[64:128, p, 128 * c + 128:128 * c + 134], qap1,
                                 start=True, stop=True)
            sdeps = [kedge]
            if i == 1:
                sdeps += [phic[38], phic[39], wsf]
            if i >= 2:
                sdeps.append(sev[i - 2])
            smm[i] = P.step("pe", sdeps, f_smm, f"smm{i}")

            def f_sev(v, i=i):
                sb = i % 2
                base = pB[:, 0:1024] if (i % 2 == 0) else pA[:, 1024:2048]
                bankA, bankB = base[:, 0:512], base[:, 512:1024]
                v.tensor_tensor(S1m[:, sb, 0:128], bankA[:, 0:128], mask1[:], ALU.mult)
                v.tensor_tensor(S1m[:, sb, 128:256], bankB[:, 0:128], mask1[:], ALU.mult)
                v.tensor_tensor(S2m[0:6, sb, 122:128], bankA[0:6, 250:256],
                                mask2[0:6, 122:128], ALU.mult)
                return v.tensor_tensor(S2m[0:6, sb, 250:256], bankB[0:6, 250:256],
                                       mask2[0:6, 122:128], ALU.mult)
            svdeps = [smm[i], init0, dc_m1, dc_m2]
            if i >= 2:
                svdeps.append(amm[i - 2])
            sev[i] = P.step("dv", svdeps, f_sev, f"sev{i}")

            if i >= 1:
                make_amm(i - 1)
            if p == 0 and c >= 1 and c + 2 <= NVT - 1:
                make_v(c + 2)

        if c >= 2:
            for _k in range(((c - 2) * 32) // 14, ((c - 1) * 32) // 14):
                make_quant(_k)

    make_amm(NT * PAIRS - 1)


    wo_ld = [None] * KD; wo_cast = [None] * KD
    for j in range(KD):
        def f_wold(gp, sem, count_only=False, j=j):
            if count_only:
                return 1
            gp.dma_start(w23s[:, j % 3, :],
                         w_out_in.ap()[128 * j:128 * (j + 1), :]).then_inc(sem, 16)
        deps = [wqge[[30, 31, 29][j]]] if j < 3 else [wo_cast[j - 3]]
        wo_ld[j] = P.dma("gp", deps, f_wold, f"wold{j}")

        def f_wocast(v, j=j):
            return v.tensor_copy(wo_sb[:, j, :], w23s[:, j % 3, :])
        wo_cast[j] = P.step("dv", [wo_ld[j]] + ([smm[NH - 1]] if j < 2 else []),
                            f_wocast, f"woc{j}")

    t2 = [None] * NT; t2e = [None] * NT
    for c in range(NT):
        def f_t2(te, c=c):
            bank = pA[:, 512 * (c % 2):512 * (c % 2) + 512].bitcast(BF16)
            r = None
            for k in range(KD):
                r = te.transpose(bank[:, 128 * k:128 * k + 128],
                                 A_tm[:, c, 128 * k:128 * k + 128], idb[:])
            return r
        deps = [aev[4 * c + 3], vev[NVT - 1], dc_idb]
        if c >= 2:
            deps.append(t2e[c - 2])
        t2[c] = P.step("pe", deps, f_t2, f"t2_{c}")

        def f_t2e(s, c=c):
            bank = pA[:, 512 * (c % 2):512 * (c % 2) + 512].bitcast(BF16)
            dst = A_fm[:, :, 128 * c:128 * c + 128]
            return s.activation(dst, bank.rearrange("p (k n) -> p k n", k=KD),
                                AF.Copy, scale=1.0)
        t2e[c] = P.step("se", [t2[c]], f_t2e, f"t2e{c}")

    womm = {}; woev = {}
    for m in range(8):
        for n in range(4):
            def f_womm(te, m=m, n=n):
                bank = pB[:, 512 * n:512 * n + 512]
                r = None
                for kd in range(KD):
                    r = te.matmul(bank, wo_sb[:, kd, 128 * m:128 * m + 128],
                                  A_fm[:, kd, 512 * n:512 * n + 512],
                                  start=(kd == 0), stop=(kd == KD - 1))
                return r
            deps = [wo_cast[KD - 1], t2e[4 * n + 3], aev[NH - 1]]
            if m >= 1:
                deps.append(woev[(m - 1, n)])
            womm[(m, n)] = P.step("pe", deps, f_womm, f"womm{m}_{n}")

            def f_woev(s, m=m, n=n):
                bank = pB[:, 512 * n:512 * n + 512]
                return s.activation(opfm[:, m, 512 * n:512 * n + 512], bank, AF.Identity,
                                    bias=b_out[:, m:m + 1], scale=1.0)
            woev[(m, n)] = P.step("se", [womm[(m, n)], dc_bout], f_woev, f"woev{m}_{n}")
            make_quant(32 + 4 * m + n)

    t3 = [None] * NT; y_ev = [None] * NT
    ln2s = [None] * NT; ln2q = [None] * NT; ln2f = [None] * NT
    ln2a = [None] * NT; res2 = [None] * NT
    ln3s = [None] * NT; ln3q = [None] * NT; ln3f = [None] * NT; ln3a = [None] * NT
    qx_mx = [None] * NT; qx_sc = [None] * NT; qx_u = [None] * NT; qx_c = [None] * NT
    qx_b = [None] * NT; qx_r = [None] * NT
    for c in range(NT):
        def f_t3(te, c=c):
            bank = pA[:, 512 * (c % 2):512 * (c % 2) + 512].bitcast(BF16)
            r = None
            for m in range(8):
                r = te.transpose(bank[:, 128 * m:128 * m + 128],
                                 opfm[:, m, 128 * c:128 * c + 128], idb[:])
            return r
        deps = [woev[(7, c // 4)], t2e[NT - 1]]
        if c >= 2:
            deps.append(y_ev[c - 2])
        t3[c] = P.step("pe", deps, f_t3, f"t3_{c}")

        def f_yev(s, c=c):
            bank = pA[:, 512 * (c % 2):512 * (c % 2) + 512].bitcast(BF16)
            return s.activation(y_sb[:, c % 2, :], bank, AF.Copy, scale=1.0)
        ydeps = [t3[c]]
        if c >= 2:
            ydeps += [ln2a[c - 2], ln2s[c - 2]]
        y_ev[c] = P.step("se", ydeps, f_yev, f"yev{c}")

        L = LNS["l2"][c % 2]

        def f_ln2s(v, c=c, L=L):
            v.bn_stats(L["bn"][:, 0:6], y_sb[:, c % 2, 0:512])
            v.bn_stats(L["bn"][:, 6:12], y_sb[:, c % 2, 512:1024])
            v.drain()
            v.bn_aggr(L["ag"][:], L["bn"][:])
            v.drain()
            return v.tensor_scalar(L["rs"][:], L["ag"][:, 1:2], EPS_LN, None, ALU.add)
        ln2s[c] = P.step("dv", [y_ev[c]], f_ln2s, f"ln2s{c}")

        def f_ln2q(s, L=L):
            return s.activation(L["rs"][:], L["rs"][:], AF.Sqrt, scale=1.0)
        ln2q[c] = P.step("se", [ln2s[c]], f_ln2q, f"ln2q{c}")

        def f_ln2f(v, L=L):
            v.reciprocal(L["rs"][:], L["rs"][:])
            v.drain()
            return v.scalar_tensor_tensor(L["nm"][:], L["ag"][:, 0:1], -1.0, L["rs"][:],
                                          ALU.mult, ALU.mult)
        ln2f[c] = P.step("dv", [ln2q[c]], f_ln2f, f"ln2f{c}")

        def f_ln2a(s, c=c, L=L):
            return s.activation(yn_sb[:, c % 2, :], y_sb[:, c % 2, :], AF.Identity,
                                bias=L["nm"][:], scale=L["rs"][:])
        adeps = [ln2f[c]]
        if c >= 2:
            adeps.append(res2[c - 2])
        ln2a[c] = P.step("se", adeps, f_ln2a, f"ln2a{c}")

        def f_res2(gp, c=c):
            return gp.tensor_tensor(x_tm[:, c, :], x_tm[:, c, :], yn_sb[:, c % 2, :],
                                    ALU.add)
        res2[c] = P.step("gp", [ln2a[c], xcast[c]], f_res2, f"res2_{c}")


    wq1_ld = []
    for k in range(KD):
        ldeps = sum([wqs[32 + 4 * k + q] for q in range(4)], [])
        ldeps += [womm[(7, 3)], t3[NT - 1]]
        for qt in range(4):
            def f_wq1ld(gp, sem, count_only=False, k=k, qt=qt):
                if count_only:
                    return 1
                r0, r1 = 32 * qt, 32 * qt + 32
                gp.dma_start(wq1g[r0:r1, k, :],
                             wq1_scr.ap()[128 * k + r0:128 * k + r1,
                                          :]).then_inc(sem, 16)
            wq1_ld.append(P.dma("gp" if qt % 2 == 0 else "sy", ldeps, f_wq1ld,
                                f"wq1l{k}_{qt}"))

    t4 = [None] * NT; t4e = [None] * NT
    f1 = {}; f1g = {}
    qh_mx2 = {}; qh_sc = [None] * NT
    qh_u = {}; qh_c = {}; qh_b = {}; qh_r = {}
    t5 = {}; t5e = {}
    f2 = {}; f2ev = {}; f2evA = {}
    wq2_ld = {}
    t6 = {}; t6e = {}
    ln4s = [None] * NT; ln4q = [None] * NT; ln4f = [None] * NT; ln4a = [None] * NT
    outadd = [None] * NT; yst = [None] * NT

    def emit_lnqx(c):
        L3 = LNS["l3"][c % 2]
        Q = QXs[c % 2]

        def f_ln3s(v, c=c, L=L3):
            v.bn_stats(L["bn"][:, 0:6], x_tm[:, c, 0:512])
            v.bn_stats(L["bn"][:, 6:12], x_tm[:, c, 512:1024])
            v.drain()
            v.bn_aggr(L["ag"][:], L["bn"][:])
            v.drain()
            return v.tensor_scalar(L["rs"][:], L["ag"][:, 1:2], EPS_LN, None,
                                   ALU.add)
        deps = [res2[c]]
        if c >= 2:
            deps.append(ln3a[c - 2])
        ln3s[c] = P.step("dv", deps, f_ln3s, f"ln3s{c}")

        def f_ln3q(s, L=L3):
            return s.activation(L["rs"][:], L["rs"][:], AF.Sqrt, scale=1.0)
        ln3q[c] = P.step("se", [ln3s[c]], f_ln3q, f"ln3q{c}")

        def f_ln3f(v, L=L3):
            v.reciprocal(L["rs"][:], L["rs"][:])
            v.drain()
            return v.scalar_tensor_tensor(L["nm"][:], L["ag"][:, 0:1], -1.0,
                                          L["rs"][:], ALU.mult, ALU.mult)
        ln3f[c] = P.step("dv", [ln3q[c]], f_ln3f, f"ln3f{c}")

        def f_ln3a(s, c=c, L=L3):
            return s.activation(x3_sb[:, :], x_tm[:, c, :], AF.Identity,
                                bias=L["nm"][:], scale=L["rs"][:])
        adeps = [ln3f[c]]
        if c == 0:
            adeps.append(t2[NT - 1])
        if c >= 1:
            adeps.append(qx_b[c - 1])
        ln3a[c] = P.step("se", adeps, f_ln3a, f"ln3a{c}")

        def f_qxmx(v, c=c, Q=Q):
            return v.reduce_max(Q["mx"][:], x3_sb[:, :], axis=AX.X,
                                apply_absolute_value=True)
        qx_mx[c] = P.step("dv", [ln3a[c]], f_qxmx, f"qxmx{c}")

        def f_qxsc(v, Q=Q):
            v.tensor_scalar(Q["isc"][:], Q["mx"][:], EPS_Q, 1.0 / 127.0,
                            ALU.max, ALU.mult)
            v.drain()
            v.reciprocal(Q["sc"][:], Q["isc"][:])
            return v.tensor_tensor(Q["fac"][:], s1_t[:], Q["isc"][:], ALU.mult)
        qx_sc[c] = P.step("dv", [qx_mx[c], wsf], f_qxsc, f"qxsc{c}")

        def f_qxu(v, c=c, Q=Q):
            return v.tensor_scalar(x3_sb[:, :], x3_sb[:, :],
                                   Q["sc"][:], OFF + 0.5, ALU.mult, ALU.add)
        cdeps = [qx_sc[c]]
        if c == 0:
            cdeps.append(t2[NT - 1])
        if c >= 1:
            cdeps.append(qx_r[c - 1])
        qx_u[c] = qx_c[c] = P.step("dv", cdeps, f_qxu, f"qxu{c}")

        def f_qxr(v, c=c):
            v.tensor_scalar(x3_sb[:, :].bitcast(I32), x3_sb[:, :].bitcast(I32),
                            -8192, None, ALU.bitwise_and)
            v.drain()
            return v.tensor_copy(Ax_tm[:, c % 2, :], x3_sb[:, :])
        rdeps = [qx_u[c]]
        if c >= 2:
            rdeps.append(t4[c - 2])
        qx_r[c] = qx_b[c] = P.step("dv", rdeps, f_qxr, f"qxr{c}")

    def emit_t4(c):
        g, cc = divmod(c, GC)

        def f_t4(te, c=c):
            bank = pB[:, 0:512].bitcast(FP16)
            r = None
            for k in range(KD):
                r = te.transpose(bank[:, 128 * k:128 * k + 128],
                                 Ax_tm[:, c % 2, 128 * k:128 * k + 128], idh[:])
            return r
        deps = [qx_r[c], dc_idh]
        if c == 0:
            deps.append(woev[(7, 3)])
        else:
            deps.append(t4e[c - 1])
        if c % GC == 0 and g >= 1:
            deps.append(f2ev[g - 1])
        t4[c] = P.step("pe", deps, f_t4, f"t4_{c}")

        def f_t4e(s, c=c):
            src = pB[:, 0:512].bitcast(FP16).rearrange("p (k n) -> p k n", k=KD)
            return s.activation(Axf[:, c % 2, :, :], src, AF.Identity, bias=-OFF,
                                scale=1.0)
        edeps = [t4[c]]
        if c == 0:
            edeps.append(t2[NT - 1])
        elif c >= 2:
            edeps.append(f1[(c - 2, 3)])
        t4e[c] = P.step("se", edeps, f_t4e, f"t4e{c}")

    def emit_f1(c):
        g, cc = divmod(c, GC)
        for quarter in range(4):
            def f_f1(te, c=c, quarter=quarter):
                base = 1024 * (quarter % 2)
                r = None
                for kd in range(KD):
                    for n in range(2):
                        off = 1024 * quarter + 512 * n
                        r = te.matmul(pA[:, base + 512 * n:base + 512 * n + 512],
                                      Axf[:, c % 2, kd, :],
                                      wq1g[:, kd, off:off + 512],
                                      start=(kd == 0), stop=(kd == KD - 1))
                return r
            deps = [t4e[c]] + list(wq1_ld)
            if quarter >= 2:
                deps.append(f1g[(c, quarter - 2)])
            elif c >= 1:
                deps.append(f1g[(c - 1, quarter + 2)])
            else:
                deps.append(y_ev[NT - 1])
            if quarter < 2 and c % GC == 0 and c >= 4:
                deps.append(f2evA[c // GC - 1])
            f1[(c, quarter)] = P.step("pe", deps, f_f1, f"f1_{c}_{quarter}")

            Q = QXs[c % 2]

            def f_f1g(s, c=c, quarter=quarter, Q=Q):
                base = 1024 * (quarter % 2)
                return s.activation(h_sb[:, c % 2, 1024 * quarter:1024 * quarter + 1024],
                                    pA[:, base:base + 1024], AF.Gelu, bias=0.0,
                                    scale=Q["fac"][:])
            gdeps = [f1[(c, quarter)], qx_sc[c]]
            if c < 2:
                gdeps.append(womm[(7, 3)])
            if c >= 2:
                gdeps += [qh_u[(c - 2, quarter)], qh_mx2[(c - 2, quarter // 2)]]
            f1g[(c, quarter)] = P.step("se", gdeps, f_f1g, f"f1g{c}_{quarter}")



    def emit_qhx(c):
        QH = QHs[c % 2]
        for half in range(2):
            def f_qhmx(v, c=c, half=half, QH=QH):
                return v.reduce_max(QH["mx"][:, half:half + 1],
                                    h_sb[:, c % 2, 2048 * half:2048 * half + 2048],
                                    axis=AX.X, apply_absolute_value=True)
            qh_mx2[(c, half)] = P.step("dv", [f1g[(c, 2 * half)],
                                              f1g[(c, 2 * half + 1)]], f_qhmx,
                                       f"qhmx{c}_{half}")

        def f_qhsc(v, c=c, QH=QH):
            v.tensor_tensor(QH["mx"][:, 0:1], QH["mx"][:, 0:1], QH["mx"][:, 1:2],
                            ALU.max)
            v.drain()
            v.tensor_scalar(QH["isc"][:], QH["mx"][:, 0:1], EPS_Q, 1.0 / 127.0,
                            ALU.max, ALU.mult)
            v.drain()
            v.reciprocal(QH["sc"][:], QH["isc"][:])
            return v.tensor_tensor(fac2buf[:, c % 8:c % 8 + 1], s2_t[:],
                                   QH["isc"][:], ALU.mult)
        qh_sc[c] = P.step("dv", [qh_mx2[(c, 0)], qh_mx2[(c, 1)], wsf],
                          f_qhsc, f"qhsc{c}")

    def emit_strips(c):
        QH = QHs[c % 2]
        for q in range(4):
            def f_qhu(v, c=c, q=q, QH=QH):
                return v.tensor_scalar(u05h[:, q % 2, :],
                                       h_sb[:, c % 2, 1024 * q:1024 * q + 1024],
                                       QH["sc"][:], OFF + 0.5, ALU.mult, ALU.add)
            ueng = "dv"
            deps = [qh_sc[c]]
            if q >= 2:
                deps.append(qh_r[(c, q - 2)])
            elif c >= 1:
                deps.append(qh_r[(c - 1, q + 2)])
            elif q < 2:
                deps.append(t2[NT - 1])
            qh_u[(c, q)] = qh_c[(c, q)] = P.step(ueng, deps, f_qhu, f"qhu{c}_{q}")

            def f_qhr(v, c=c, q=q):
                v.tensor_scalar(u05h[:, q % 2, :].bitcast(I32),
                                u05h[:, q % 2, :].bitcast(I32),
                                -8192, None, ALU.bitwise_and)
                v.drain()
                return v.tensor_copy(Ah_tm[:, 1024 * q:1024 * q + 1024],
                                     u05h[:, q % 2, :])
            deps = [qh_u[(c, q)]]
            if c >= 1:
                deps.append(t5[(c - 1, 3)])
            elif q == 0:
                deps.append(womm[(7, 3)])
            qh_r[(c, q)] = qh_b[(c, q)] = P.step("dv", deps, f_qhr, f"qhr{c}_{q}")

    def emit_tfb(c):
        g, cc = divmod(c, GC)
        for j in range(4):
            def f_t5(te, c=c, j=j):
                bank = pB[:, 512 + 512 * (j % 2):1024 + 512 * (j % 2)].bitcast(FP16)
                r = None
                for b in range(8):
                    k = 8 * j + b
                    r = te.transpose(bank[:, 128 * b:128 * b + 128],
                                     Ah_tm[:, 128 * k:128 * k + 128], idh[:])
                return r
            deps = [qh_r[(c, j)]]
            if j == 0:
                if c >= 1:
                    deps.append(t5e[(c - 1, 2)])
            elif j == 1:
                if c >= 1:
                    deps.append(t5e[(c - 1, 3)])
            else:
                deps.append(t5e[(c, j - 2)])
            t5[(c, j)] = P.step("pe", deps, f_t5, f"t5_{c}_{j}")

            def f_t5e(v, c=c, cc=cc, j=j):
                src = pB[:, 512 + 512 * (j % 2):1024 + 512 * (j % 2)].bitcast(FP16)
                dst = Ahfm[:, 8 * j:8 * j + 8, 128 * cc:128 * cc + 128]
                return v.tensor_scalar(dst, src.rearrange("p (k n) -> p k n", k=8),
                                       -OFF, None, ALU.add)
            deps = [t5[(c, j)]]
            if c == 0:
                deps += [womm[(7, 3)], ln2a[NT - 1], ln2s[NT - 1], res2[NT - 1]]
            if cc == 0 and g >= 1:
                deps.append(f2[(g - 1, KH - 1)])
            t5e[(c, j)] = P.step("dv", deps, f_t5e, f"t5e{c}_{j}")

    def emit_f2(g):
        for k in range(KH):
            deps = list(wqs[k])
            if g == 0:
                deps.append(t2[NT - 1])
            if g == 0 and k < 4:
                deps += sum(wqs[NWQ - 4:], [])
            if k >= 4:
                deps.append(f2[(g, k - 4)])
            elif g >= 1:
                deps.append(f2[(g - 1, KH - 4 + k)])
            parts = []
            for qt in range(4):
                def f_wq2ld(gp, sem, count_only=False, g=g, k=k, qt=qt):
                    if count_only:
                        return 1
                    gp.dma_start(wq2r[32 * qt:32 * qt + 32, k % 4, :],
                                 wq2_scr.ap()[128 * k + 32 * qt:128 * k + 32 * qt + 32,
                                              :]).then_inc(sem, 16)
                parts.append(P.dma("gp" if qt % 2 == 0 else "sy", deps, f_wq2ld,
                                   f"wq2l{g}_{k}_{qt}"))
            wq2_ld[(g, k)] = parts

            def f_f2(te, g=g, k=k):
                r = None
                for m in range(8):
                    bank = (pA[:, 512 * m:512 * m + 512] if m < 4
                            else pB[:, 512 * (m - 4):512 * (m - 4) + 512])
                    r = te.matmul(bank, wq2r[:, k % 4, 128 * m:128 * m + 128],
                                  Ahfm[:, k, :], start=(k == 0), stop=(k == KH - 1))
                return r
            deps = list(wq2_ld[(g, k)])
            if k % 8 == 0:
                deps += [t5e[(4 * g + cc2, k // 8)] for cc2 in range(GC)]
            if k == 0:
                deps += [f1g[(4 * g + 3, 3)], t5e[(4 * g + 3, 2)],
                         t5e[(4 * g + 3, 3)]]
                if g >= 1:
                    deps.append(t6e[(g - 1, 3)])
            f2[(g, k)] = P.step("pe", deps, f_f2, f"f2_{g}_{k}")

        def f_f2ev(v, g=g):
            r = None
            for m in range(4, 8):
                r = v.tensor_copy(o2fm[:, m, :],
                                  pB[:, 512 * (m - 4):512 * (m - 4) + 512])
            return r
        fdeps = [f2[(g, KH - 1)]]
        if g == 0:
            fdeps.append(wqsub[NWQ - 1])
        f2ev[g] = P.step("dv", fdeps, f_f2ev, f"f2ev{g}")

        def f_f2evA(s, g=g):
            r = None
            for m in range(4):
                r = s.activation(o2fm[:, m, :], pA[:, 512 * m:512 * m + 512],
                                 AF.Copy, scale=1.0)
            return r
        f2evA[g] = P.step("se", list(fdeps), f_f2evA, f"f2evA{g}")

    def emit_tail(g, cc):
        c = 4 * g + cc

        def f_t6(te, g=g, cc=cc):
            bank = pB[:, 1536:2048].bitcast(BF16)
            r = None
            for m in range(8):
                r = te.transpose(bank[:, 128 * m:128 * m + 128],
                                 o2fm[:, m, 128 * cc:128 * cc + 128], idb[:])
            return r
        deps = [f2ev[g], f2evA[g]]
        if cc >= 1:
            deps.append(t6e[(g, cc - 1)])
        t6[(g, cc)] = P.step("pe", deps, f_t6, f"t6_{g}_{cc}")

        def f_t6e(v, g=g, cc=cc, c=c):
            bank = pB[:, 1536:2048].bitcast(BF16)
            return v.tensor_scalar(y2_sb[:, :], bank,
                                   fac2buf[:, c % 8:c % 8 + 1], None, ALU.mult)
        deps = [t6[(g, cc)], qh_sc[c]]
        if c == 0:
            deps.append(wqge[NWQ - 1])
        if c >= 1:
            deps += [ln4a[c - 1], ln4s[c - 1]]
        t6e[(g, cc)] = P.step("dv", deps, f_t6e, f"t6e{g}_{cc}")

        L = LNS["l4"][c % 2]

        def f_ln4s(v, c=c, L=L):
            v.bn_stats(L["bn"][:, 0:6], y2_sb[:, 0:512])
            v.bn_stats(L["bn"][:, 6:12], y2_sb[:, 512:1024])
            v.drain()
            v.bn_aggr(L["ag"][:], L["bn"][:])
            v.drain()
            return v.tensor_scalar(L["rs"][:], L["ag"][:, 1:2], EPS_LN, None,
                                   ALU.add)
        ln4s[c] = P.step("dv", [t6e[(g, cc)]], f_ln4s, f"ln4s{c}")

        def f_ln4q(s, L=L):
            return s.activation(L["rs"][:], L["rs"][:], AF.Sqrt, scale=1.0)
        ln4q[c] = P.step("se", [ln4s[c]], f_ln4q, f"ln4q{c}")

        def f_ln4f(v, L=L):
            v.reciprocal(L["rs"][:], L["rs"][:])
            v.drain()
            return v.scalar_tensor_tensor(L["nm"][:], L["ag"][:, 0:1], -1.0,
                                          L["rs"][:], ALU.mult, ALU.mult)
        ln4f[c] = P.step("dv", [ln4q[c]], f_ln4f, f"ln4f{c}")

        def f_ln4a(s, c=c, L=L):
            return s.activation(yn4_sb[:, :], y2_sb[:, :], AF.Identity,
                                bias=L["nm"][:], scale=L["rs"][:])
        adeps = [ln4f[c]]
        if c == 0:
            adeps.append(wqge[NWQ - 1])
        if c >= 1:
            adeps.append(outadd[c - 1])
        ln4a[c] = P.step("se", adeps, f_ln4a, f"ln4a{c}")

        def f_out(gp, c=c):
            return gp.tensor_tensor(yout_sb[:, :], x_tm[:, c, :],
                                    yn4_sb[:, :], ALU.add)
        odeps = [ln4a[c], ln3s[c]]
        if c == 0:
            odeps += [wqge[NWQ - 1]] + sum(wqs[NWQ - 4:], [])
        if c >= 1:
            odeps.append(yst[c - 1])
        outadd[c] = P.step("gp", odeps, f_out, f"out{c}")

        def f_yst(sy, sem, count_only=False, c=c):
            if count_only:
                return 1
            sy.dma_start(y_out.ap()[128 * c:128 * (c + 1), :],
                         yout_sb[:, :]).then_inc(sem, 16)
        yst[c] = P.dma("sy", [outadd[c]], f_yst, f"yst{c}")

    for g in range(NG):
        if g >= 1:
            emit_strips(4 * g - 1)
            emit_tfb(4 * g - 1)
            emit_f2(g - 1)
        emit_lnqx(4 * g)
        emit_lnqx(4 * g + 1)
        emit_t4(4 * g)
        for cc in range(GC):
            c = 4 * g + cc
            emit_f1(c)
            if cc < GC - 1:
                emit_t4(c + 1)
            if cc >= 1:
                emit_strips(c - 1)
                emit_tfb(c - 1)
            emit_qhx(c)
            if cc + 2 < GC:
                emit_lnqx(c + 2)
            if g >= 1:
                emit_tail(g - 1, cc)
    emit_strips(NT - 1)
    emit_tfb(NT - 1)
    emit_f2(NG - 1)
    for cc in range(GC):
        emit_tail(NG - 1, cc)

    def f_fin(sy):
        return sy.nop()
    P.step("sy", [yst[c] for c in range(NT)], f_fin, "fin")

    with nc.Block("mk") as blk:
        P.emit(blk, ctr_h, qctr_h)

    st.close()
    return nc



def kernel(L0_attn_x, L0_attn_w_qkv, L0_attn_proj, L0_attn_w_out, L0_attn_b_out,
           L0_ffn_w1, L0_ffn_w2):
    x = np.asarray(L0_attn_x, dtype=np.float32)
    w_qkv = np.asarray(L0_attn_w_qkv, dtype=np.float32)
    proj = np.ascontiguousarray(np.asarray(L0_attn_proj, dtype=np.float32))
    w_out = np.ascontiguousarray(np.asarray(L0_attn_w_out, dtype=np.float32))
    b_out = np.asarray(L0_attn_b_out, dtype=np.float32)
    w1 = np.asarray(L0_ffn_w1, dtype=np.float32)
    w2 = np.asarray(L0_ffn_w2, dtype=np.float32)

    wqkT = np.ascontiguousarray(w_qkv[:, :2048].T)
    w_v = np.ascontiguousarray(w_qkv[:, 2048:])
    b_out_r = np.ascontiguousarray(b_out.reshape(8, 128).T)
    w1T = np.ascontiguousarray(w1.T)
    w2T = np.ascontiguousarray(w2.T)

    jj, rr = np.meshgrid(np.arange(128), np.arange(128), indexing="ij")
    mask1 = ((jj - rr >= 0) & (jj - rr <= 6)).astype(np.float32)
    j2, r2 = np.meshgrid(np.arange(6), np.arange(128), indexing="ij")
    mask2 = (r2 - j2 >= 122).astype(np.float32)
    identf = np.eye(128, dtype=np.float32)
    identb = np.eye(128).astype(ml_dtypes.bfloat16)
    identh = np.eye(128, dtype=np.float16)

    in_maps = []
    for core in range(8):
        b = core // 4
        s0 = (core % 4) * TOK
        xh = np.zeros((HALO, D), dtype=np.float32)
        lo, hi = s0 - 3, s0 + TOK + 3
        src_lo, src_hi = max(lo, 0), min(hi, 8192)
        xh[src_lo - lo:src_hi - lo, :] = x[b, src_lo:src_hi, :]
        edge = np.ones((128, 6), dtype=np.float32)
        if s0 == 0:
            edge[:, 0:3] = 0.0
        if s0 + TOK == 8192:
            edge[:, 3:6] = 0.0
        in_maps.append({
            "x_halo": xh, "wqkT": wqkT, "w_v": w_v, "proj": proj,
            "w_out": w_out, "b_out_r": b_out_r, "w1T": w1T, "w2T": w2T,
            "mask1": mask1, "mask2": mask2, "identf": identf, "identb": identb,
            "identh": identh, "edgem": edge,
        })

    nc = build_nc()
    res = run_bass_kernel_spmd(nc, in_maps, core_ids=list(range(8)))
    out = np.concatenate([np.asarray(res.results[i]["y"]) for i in range(8)], axis=0)
    return out.reshape(2, 8192, D)
